# version 06
# baseline (speedup 1.0000x reference)
import math
import numpy as np
import ml_dtypes
import concourse.bass as bass
import concourse.mybir as mybir
from concourse.bass_utils import run_bass_kernel_spmd

F32 = mybir.dt.float32
BF16 = mybir.dt.bfloat16
AF = mybir.ActivationFunctionType
ALU = mybir.AluOpType
AX = mybir.AxisListType

N_CORES = 8
D = 1024
T = 4096
DCH = D // 128
NT = T // 512


ENGINES = ("sync", "scalar", "vector", "tensor", "gpsimd")


class Op:
    __slots__ = ("engine", "idx", "fn", "deps", "is_dma", "signals", "tag",
                 "sem_key")

    def __init__(self, engine, idx, fn, deps, is_dma, tag, sem_key):
        self.engine = engine
        self.idx = idx
        self.fn = fn
        self.deps = deps
        self.is_dma = is_dma
        self.signals = False
        self.tag = tag
        self.sem_key = sem_key


class Buf:
    __slots__ = ("name", "writer", "readers")

    def __init__(self, name=""):
        self.name = name
        self.writer = None
        self.readers = []


class Prog:
    def __init__(self, nc):
        self.nc = nc
        self.streams = {e: [] for e in ENGINES}
        self.order = []

    def op(self, engine, fn, reads=(), writes=(), deps=(), dma=False, tag=""):
        dl = []
        for b in reads:
            if b.writer is not None:
                dl.append(b.writer)
        for b in writes:
            if b.writer is not None:
                dl.append(b.writer)
            dl.extend(b.readers)
        dl.extend(d for d in deps if d is not None)
        if dma:
            bs = list(writes) or list(reads)
            assert bs, f"DMA op {tag} needs a read or write Buf"
            sem_key = "dma_" + bs[0].name
        else:
            sem_key = engine
        o = Op(engine, len(self.streams[engine]), fn, dl, dma, tag, sem_key)
        self.streams[engine].append(o)
        self.order.append(o)
        for b in reads:
            b.readers.append(o)
        for b in writes:
            b.writer = o
            b.readers = []
        return o

    def finalize(self, block, es, nc):
        for e in ENGINES:
            for o in self.streams[e]:
                if o.is_dma:
                    o.signals = True
                for d in o.deps:
                    if d.engine == o.engine and d.engine == "tensor" and not d.is_dma:
                        continue
                    d.signals = True
        cum = {}
        tot = {}
        for o in self.order:
            if o.signals:
                tot[o.sem_key] = tot.get(o.sem_key, 0) + (16 if o.is_dma else 1)
            cum[(o.engine, o.idx)] = tot.get(o.sem_key, 0)
        sems = {}
        for ki, key in enumerate(list(tot) + [e for e in ENGINES]):
            if key not in sems:
                sems[key] = es.enter_context(nc.semaphore(f"s{ki}_{key}"[:48]))

        def emit(eng_obj, engine_name):
            waited = {}
            for o in self.streams[engine_name]:
                need = {}
                for d in o.deps:
                    if d.engine == engine_name and not d.is_dma \
                            and engine_name == "tensor":
                        continue
                    v = cum[(d.engine, d.idx)]
                    assert v > 0, f"dep not marked: {d.engine}:{d.idx} ({d.tag})"
                    need[d.sem_key] = max(need.get(d.sem_key, 0), v)
                for sk, v in need.items():
                    if waited.get(sk, 0) < v:
                        eng_obj.wait_ge(sems[sk], v)
                        waited[sk] = v
                inst = o.fn(eng_obj)
                if o.signals:
                    inst.then_inc(sems[o.sem_key], 16 if o.is_dma else 1)
            return waited

        @block.sync
        def _(sync):
            w = emit(sync, "sync")
            for key, v in tot.items():
                if v > 0 and w.get(key, 0) < v:
                    sync.wait_ge(sems[key], v)

        @block.scalar
        def _(scalar):
            emit(scalar, "scalar")

        @block.vector
        def _(vector):
            emit(vector, "vector")

        @block.tensor
        def _(tensor):
            emit(tensor, "tensor")

        @block.gpsimd
        def _(gpsimd):
            emit(gpsimd, "gpsimd")



def _tile_bias(b):
    d = b.shape[0]
    return np.ascontiguousarray(b.reshape(d // 128, 128).T)


def _t(w):
    return np.ascontiguousarray(w.T)


def host_constants():
    c = {}
    c["c_ident_bf"] = np.eye(128, dtype=ml_dtypes.bfloat16)
    c["c_ones_row"] = np.ones((1, 128), np.float32)
    c["c_ones_all"] = np.ones((128, 128), np.float32)
    c["c_ones_col"] = np.ones((128, 1), np.float32)
    c["c_striu"] = np.triu(np.ones((128, 128), np.float32), 1)
    c["c_triu_mask"] = np.triu(np.ones((64, 64), np.float32), 0)
    s16 = np.zeros((16, 16 * 128), np.float32)
    for e in range(16):
        s16[e, e * 128:(e + 1) * 128] = 1.0
    c["c_sel16"] = s16
    s8 = np.zeros((8, 8 * 128), np.float32)
    for e in range(8):
        s8[e, e * 128:(e + 1) * 128] = 1.0
    c["c_sel8"] = s8
    return c



class Alloc:
    def __init__(self, nc, ctxstack):
        self.nc = nc
        self.es = ctxstack
        self.handles = {}

    def sb(self, name, shape, dtype):
        if name in self.handles:
            return self.handles[name]
        h = self.es.enter_context(self.nc.sbuf_tensor(name, shape, dtype))
        self.handles[name] = h
        return h

    def ps(self, name, shape, dtype=F32):
        if name in self.handles:
            return self.handles[name]
        h = self.es.enter_context(self.nc.psum_tensor(name, shape, dtype))
        self.handles[name] = h
        return h


class Ctx:

    def __init__(self, nc, prog, alloc):
        self.nc = nc
        self.P = prog
        self.alloc = alloc
        self.bufs = {}
        self.cb16 = {}
        self.cf32 = {}
        self.cbuf = Buf("consts")

    def buf(self, name):
        if name not in self.bufs:
            self.bufs[name] = Buf(name)
        return self.bufs[name]


CONST_SHAPES = {
    "ident": (128, 128),
    "ones_all": (128, 128),
    "ones_row": (1, 128),
    "ones_col": (128, 1),
    "striu": (128, 128),
    "triu_mask": (64, 64),
    "sel16": (16, 16 * 128),
    "sel8": (8, 8 * 128),
}


def emit_load_constants(ctx, const_drams, names=None):
    P, a = ctx.P, ctx.alloc
    names = names or list(CONST_SHAPES)
    for nm in names:
        shp = CONST_SHAPES[nm]
        sf = a.sb("sf_" + nm, list(shp), F32)
        sb = a.sb("sc_" + nm, list(shp), BF16)
        ctx.cf32[nm] = sf
        ctx.cb16[nm] = sb
        P.op("sync", lambda e, sf=sf, nm=nm: e.dma_start(
            out=sf[:], in_=const_drams[nm]),
            writes=[ctx.cbuf], dma=True, tag=f"c_{nm}_ld")
        P.op("vector", lambda e, sb=sb, sf=sf: e.tensor_copy(sb[:], sf[:]),
            reads=[ctx.cbuf], writes=[ctx.cbuf], tag=f"c_{nm}_cast")
    ctx.cb16 = {k: v.ap() for k, v in ctx.cb16.items()}
    ctx.cf32 = {k: v.ap() for k, v in ctx.cf32.items()}


def load_weight_bf16(ctx, w_dram, w_sb, stage_sb, stage_bufs, rows, cols,
                     tag, cast_engine="vector", dma_q="sync"):
    P = ctx.P
    wb = ctx.buf(tag)
    nk = rows // 128
    nst = len(stage_sb)
    for k in range(nk):
        st = stage_sb[k % nst]
        stb = stage_bufs[k % nst]
        P.op(dma_q, lambda e, k=k, st=st: e.dma_start(
            out=st[:, 0:cols], in_=w_dram[k * 128:(k + 1) * 128, :]),
            writes=[stb], dma=True, tag=f"{tag}_ld{k}")
        P.op(cast_engine, lambda e, k=k, st=st: e.tensor_copy(
            w_sb[:, k * cols:(k + 1) * cols], st[:, 0:cols]),
            reads=[stb], writes=[wb], tag=f"{tag}_cast{k}")
    return wb



def emit_mlp(ctx, xn_dram, w1_dram, b1_dram, w2_dram, b2_dram, out_mode,
             out_dram, b2row_dram=None, tag="mlp"):
    nc, P, a = ctx.nc, ctx.P, ctx.alloc
    HID = 4096
    HCH = HID // 128
    w1_sb = a.sb(f"{tag}_w1", [128, DCH * HID], BF16)
    w2_sb = a.sb(f"{tag}_w2", [128, HCH * D], BF16)
    stage = [a.sb(f"{tag}_st{i}", [128, 2048], F32) for i in range(2)]
    stageb = [ctx.buf(f"{tag}_st{i}") for i in range(2)]
    b1_sb = a.sb(f"{tag}_b1", [128, HCH], F32)
    xn_t = [a.sb(f"{tag}_xn{i}", [128, DCH * 512], BF16) for i in range(2)]
    h_t = a.sb(f"{tag}_h", [128, HCH * 512], BF16)
    p1 = [a.ps(f"{tag}_p1_{i}", [128, 512], F32) for i in range(4)]
    p2 = [a.ps(f"{tag}_p2_{i}", [128, 512], F32) for i in range(4)]
    b_w1 = ctx.buf(f"{tag}_w1")
    b_w2 = ctx.buf(f"{tag}_w2")
    b_b1 = ctx.buf(f"{tag}_b1")
    b_h = ctx.buf(f"{tag}_h")
    bp1 = [ctx.buf(f"{tag}_p1_{i}") for i in range(4)]
    bp2 = [ctx.buf(f"{tag}_p2_{i}") for i in range(4)]

    def load_w(w_dram, w_sb, wb, rows, cols, t2):
        nk = rows // 128
        cw = min(2048, cols)
        j = 0
        for k in range(nk):
            for half in range(cols // cw):
                st, stb = stage[j % 2], stageb[j % 2]
                j += 1
                P.op("sync", lambda e, k=k, h=half, st=st: e.dma_start(
                    out=st[:, 0:cw], in_=w_dram[k * 128:(k + 1) * 128,
                                                h * cw:(h + 1) * cw]),
                    writes=[stb], dma=True, tag=f"{t2}_ld{k}_{half}")
                P.op("vector", lambda e, k=k, h=half, st=st, cols=cols: e.tensor_copy(
                    w_sb[:, k * cols + h * cw: k * cols + (h + 1) * cw], st[:, 0:cw]),
                    reads=[stb], writes=[wb], tag=f"{t2}_cast{k}_{half}")

    load_w(w1_dram, w1_sb, b_w1, D, HID, tag + "w1")
    load_w(w2_dram, w2_sb, b_w2, HID, D, tag + "w2")
    P.op("sync", lambda e: e.dma_start(out=b1_sb[:], in_=b1_dram),
         writes=[b_b1], dma=True, tag=f"{tag}_b1ld")

    if out_mode == "fm":
        b2_sb = a.sb(f"{tag}_b2", [128, DCH], F32)
        b_b2 = ctx.buf(f"{tag}_b2")
        P.op("sync", lambda e: e.dma_start(out=b2_sb[:], in_=b2_dram),
             writes=[b_b2], dma=True, tag=f"{tag}_b2ld")
        out_t = a.sb(f"{tag}_out", [128, DCH * 512], BF16)
        b_out = ctx.buf(f"{tag}_out")
    else:
        b2r_sb = a.sb(f"{tag}_b2r", [1, D], F32)
        b2bc_sb = a.sb(f"{tag}_b2bc", [128, D], F32)
        b_b2r = ctx.buf(f"{tag}_b2r")
        b_b2bc = ctx.buf(f"{tag}_b2bc")
        P.op("sync", lambda e: e.dma_start(out=b2r_sb[:], in_=b2row_dram),
             writes=[b_b2r], dma=True, tag=f"{tag}_b2rld")
        b2rb_sb = a.sb(f"{tag}_b2rb", [1, D], BF16)
        P.op("vector", lambda e: e.tensor_copy(b2rb_sb[:], b2r_sb[:]),
             reads=[b_b2r], writes=[b_b2r], tag=f"{tag}_b2rcast")
        for seg in range(2):
            P.op("tensor", lambda e, s=seg: e.matmul(
                p2[3][:, 0:512], ctx.cb16["ones_row"],
                b2rb_sb[0:1, s * 512:(s + 1) * 512], start=True, stop=True),
                reads=[b_b2r, ctx.cbuf], writes=[bp2[3]], tag=f"{tag}_b2bc{seg}")
            P.op("scalar", lambda e, s=seg: e.copy(
                b2bc_sb[:, s * 512:(s + 1) * 512], p2[3][:, 0:512]),
                reads=[bp2[3]], writes=[b_b2bc], tag=f"{tag}_b2bcv{seg}")
        vaug = a.sb(f"{tag}_vaug", [128, 1040], BF16)
        b_vaug = ctx.buf(f"{tag}_vaug")
        P.op("vector", lambda e: e.memset(vaug[:], 1.0),
             writes=[b_vaug], tag=f"{tag}_vmemset")

    for n in range(NT):
        i = n % 2
        b_xn = ctx.buf(f"{tag}_xn{i}")
        P.op("sync", lambda e, n=n, i=i: e.dma_start(
            out=xn_t[i].ap().rearrange("p (c t) -> p c t", c=DCH),
            in_=xn_dram.rearrange("(c p) t -> p c t", p=128)[:, :, n * 512:(n + 1) * 512]),
            writes=[b_xn], dma=True, tag=f"{tag}_xnld{n}")
        for m in range(HCH):
            bk = m % 4
            for k in range(DCH):
                P.op("tensor", lambda e, m=m, k=k, i=i, bk=bk: e.matmul(
                    p1[bk][:], w1_sb[:, k * HID + m * 128: k * HID + (m + 1) * 128],
                    xn_t[i][:, k * 512:(k + 1) * 512],
                    start=(k == 0), stop=(k == DCH - 1)),
                    reads=[b_w1, b_xn], writes=[bp1[bk]],
                    tag=f"{tag}_mm1_{n}_{m}_{k}")
            P.op("scalar", lambda e, m=m, bk=bk: e.activation(
                out=h_t[:, m * 512:(m + 1) * 512], in_=p1[bk][:], func=AF.Gelu,
                bias=b1_sb[:, m:m + 1]),
                reads=[bp1[bk], b_b1], writes=[b_h], tag=f"{tag}_gelu{n}_{m}")
        if out_mode == "fm":
            for m2 in range(DCH):
                bk = m2 % 4
                for k2 in range(HCH):
                    P.op("tensor", lambda e, m2=m2, k2=k2, bk=bk: e.matmul(
                        p2[bk][:], w2_sb[:, k2 * D + m2 * 128: k2 * D + (m2 + 1) * 128],
                        h_t[:, k2 * 512:(k2 + 1) * 512],
                        start=(k2 == 0), stop=(k2 == HCH - 1)),
                        reads=[b_w2, b_h], writes=[bp2[bk]],
                        tag=f"{tag}_mm2_{n}_{m2}_{k2}")
                P.op("scalar", lambda e, m2=m2, bk=bk: e.activation(
                    out=out_t[:, m2 * 512:(m2 + 1) * 512], in_=p2[bk][:],
                    func=AF.Identity, bias=b2_sb[:, m2:m2 + 1]),
                    reads=[bp2[bk], b_b2], writes=[b_out], tag=f"{tag}_ev2_{n}_{m2}")
            P.op("sync", lambda e, n=n: e.dma_start(
                out=out_dram.rearrange("(c p) t -> p c t", p=128)[:, :, n * 512:(n + 1) * 512],
                in_=out_t.ap().rearrange("p (c t) -> p c t", c=DCH)),
                reads=[b_out], dma=True, tag=f"{tag}_ost{n}")
        else:
            for msub in range(4):
                for n2 in range(2):
                    bk = n2 % 2
                    for k2 in range(HCH):
                        P.op("tensor", lambda e, msub=msub, n2=n2, k2=k2, bk=bk: e.matmul(
                            p2[bk][:], h_t[:, k2 * 512 + msub * 128: k2 * 512 + (msub + 1) * 128],
                            w2_sb[:, k2 * D + n2 * 512: k2 * D + (n2 + 1) * 512],
                            start=(k2 == 0), stop=(k2 == HCH - 1)),
                            reads=[b_w2, b_h], writes=[bp2[bk]],
                            tag=f"{tag}_vmm2_{n}_{msub}_{n2}_{k2}")
                    P.op("vector", lambda e, n2=n2, bk=bk: e.tensor_add(
                        vaug.ap().rearrange("p (h s) -> p h s", s=65)[:, n2 * 8:(n2 + 1) * 8, 0:64],
                        p2[bk].ap().rearrange("p (h s) -> p h s", s=64),
                        b2bc_sb.ap().rearrange("p (h s) -> p h s", s=64)[:, n2 * 8:(n2 + 1) * 8, :]),
                        reads=[bp2[bk], b_b2bc], writes=[b_vaug],
                        tag=f"{tag}_vev_{n}_{msub}_{n2}")
                P.op("sync", lambda e, n=n, msub=msub: e.dma_start(
                    out=out_dram[n * 512 + msub * 128: n * 512 + (msub + 1) * 128, :],
                    in_=vaug[:]),
                    reads=[b_vaug], dma=True, tag=f"{tag}_vst{n}_{msub}")

def emit_ln(ctx, x_dram, outs, kind="ln", w_tiled_dram=None, eps=1e-5,
            tag="ln"):
    nc, P = ctx.nc, ctx.P
    a = ctx.alloc
    x_t = [a.sb(f"{tag}_x{i}", [128, DCH * 512], F32) for i in range(2)]
    xb_t = [a.sb(f"{tag}_xb{i}", [128, DCH * 512], BF16) for i in range(2)]
    sq_t = a.sb(f"{tag}_sq", [128, DCH * 512], BF16)
    orow = a.sb(f"{tag}_orow", [1, 512 * 4], F32)
    out_t = [a.sb(f"{tag}_o{i}", [128, DCH * 512], BF16) for i in range(2)]
    out32_t = [a.sb(f"{tag}_o32{i}", [128, DCH * 512], F32) for i in range(2)] \
        if any(dt == F32 for _, dt in outs) else None
    wln = a.sb(f"{tag}_w", [128, DCH], F32) if w_tiled_dram is not None else None
    ones_col = ctx.cb16["ones_col"]
    ones_row32 = ctx.cf32["ones_row"]

    p_stat = a.ps(f"{tag}_pstat", [1, 1024], F32)
    p_bc = a.ps(f"{tag}_pbc", [128, 1024], F32)

    if w_tiled_dram is not None:
        bw = ctx.buf(f"{tag}_w")
        P.op("sync", lambda e: e.dma_start(out=wln[:], in_=w_tiled_dram),
             writes=[bw], dma=True, tag=f"{tag}_wld")
    else:
        bw = None

    for n in range(NT):
        i = n % 2
        bx = ctx.buf(f"{tag}_x{i}")
        bxb = ctx.buf(f"{tag}_xb{i}")
        bsq = ctx.buf(f"{tag}_sq")
        bst = ctx.buf(f"{tag}_pstat")
        bbc = ctx.buf(f"{tag}_pbc")
        bor = ctx.buf(f"{tag}_orow")
        bo = ctx.buf(f"{tag}_o{i}")
        bo32 = ctx.buf(f"{tag}_o32{i}")
        P.op("sync", lambda e, n=n, i=i: e.dma_start(
            out=x_t[i].ap().rearrange("p (c t) -> p c t", c=DCH),
            in_=x_dram.rearrange("(c p) t -> p c t", p=128)[:, :, n * 512:(n + 1) * 512]),
            writes=[bx], dma=True, tag=f"{tag}_xld{n}")
        P.op("vector", lambda e, i=i: e.tensor_copy(xb_t[i][:], x_t[i][:]),
             reads=[bx], writes=[bxb], tag=f"{tag}_cast{n}")
        if kind == "ln":
            P.op("vector", lambda e, i=i: e.tensor_mul(sq_t[:], xb_t[i][:], xb_t[i][:]),
                 reads=[bxb], writes=[bsq], tag=f"{tag}_sq{n}")
        else:
            P.op("vector", lambda e, i=i: e.tensor_mul(sq_t[:], xb_t[i][:], xb_t[i][:]),
                 reads=[bxb], writes=[bsq], tag=f"{tag}_sq{n}")
        for c in range(DCH):
            if kind == "ln":
                P.op("tensor", lambda e, c=c, i=i: e.matmul(
                    p_stat[0:1, 0:512], ones_col, xb_t[i][:, c * 512:(c + 1) * 512],
                    start=(c == 0), stop=(c == DCH - 1)),
                    reads=[bxb, ctx.cbuf], writes=[bst], tag=f"{tag}_smm{n}_{c}")
            P.op("tensor", lambda e, c=c: e.matmul(
                p_stat[0:1, 512:1024], ones_col, sq_t[:, c * 512:(c + 1) * 512],
                start=(c == 0), stop=(c == DCH - 1)),
                reads=[bsq, ctx.cbuf], writes=[bst], tag=f"{tag}_qmm{n}_{c}")
        P.op("scalar", lambda e: e.copy(orow[0:1, 0:1024], p_stat[0:1, 0:1024]),
             reads=[bst], writes=[bor], tag=f"{tag}_evst{n}")
        if kind == "ln":
            P.op("vector", lambda e: e.tensor_scalar_mul(
                orow[0:1, 1024:1536], orow[0:1, 0:512], 1.0 / D),
                reads=[bor], writes=[bor], tag=f"{tag}_mean{n}")
            P.op("vector", lambda e: e.tensor_mul(
                orow[0:1, 0:512], orow[0:1, 1024:1536], orow[0:1, 1024:1536]),
                reads=[bor], writes=[bor], tag=f"{tag}_m2{n}")
            P.op("vector", lambda e: e.scalar_tensor_tensor(
                orow[0:1, 1536:2048], orow[0:1, 512:1024], 1.0 / D,
                orow[0:1, 0:512], ALU.mult, ALU.subtract),
                reads=[bor], writes=[bor], tag=f"{tag}_var{n}")
        else:
            P.op("vector", lambda e: e.tensor_scalar_mul(
                orow[0:1, 1536:2048], orow[0:1, 512:1024], 1.0 / D),
                reads=[bor], writes=[bor], tag=f"{tag}_var{n}")
        P.op("vector", lambda e: e.tensor_scalar_add(
            orow[0:1, 1536:2048], orow[0:1, 1536:2048], eps),
            reads=[bor], writes=[bor], tag=f"{tag}_eps{n}")
        P.op("scalar", lambda e: e.activation(
            out=orow[0:1, 512:1024], in_=orow[0:1, 1536:2048], func=AF.Sqrt),
            reads=[bor], writes=[bor], tag=f"{tag}_sqrt{n}")
        P.op("vector", lambda e: e.reciprocal(
            orow[0:1, 1536:2048], orow[0:1, 512:1024]),
            reads=[bor], writes=[bor], tag=f"{tag}_rcp{n}")
        if kind == "ln":
            P.op("tensor", lambda e: e.matmul(
                p_bc[:, 0:512], ones_row32, orow[0:1, 1024:1536],
                start=True, stop=True),
                reads=[bor, ctx.cbuf], writes=[bbc], tag=f"{tag}_bcm{n}")
        P.op("tensor", lambda e: e.matmul(
            p_bc[:, 512:1024], ones_row32, orow[0:1, 1536:2048],
            start=True, stop=True),
            reads=[bor, ctx.cbuf], writes=[bbc], tag=f"{tag}_bcr{n}")
        for c in range(DCH):
            if kind == "ln":
                P.op("vector", lambda e, c=c, i=i: e.tensor_sub(
                    x_t[i][:, c * 512:(c + 1) * 512],
                    x_t[i][:, c * 512:(c + 1) * 512], p_bc[:, 0:512]),
                    reads=[bx, bbc], writes=[bx], tag=f"{tag}_sub{n}_{c}")
            tgt32 = out32_t[i] if out32_t is not None else None
            if tgt32 is not None:
                P.op("vector", lambda e, c=c, i=i: e.tensor_mul(
                    out32_t[i][:, c * 512:(c + 1) * 512],
                    x_t[i][:, c * 512:(c + 1) * 512], p_bc[:, 512:1024]),
                    reads=[bx, bbc], writes=[bo32], tag=f"{tag}_mul{n}_{c}")
                src, bsrc = out32_t[i], bo32
            else:
                src, bsrc = x_t[i], bx
                P.op("vector", lambda e, c=c, i=i: e.tensor_mul(
                    x_t[i][:, c * 512:(c + 1) * 512],
                    x_t[i][:, c * 512:(c + 1) * 512], p_bc[:, 512:1024]),
                    reads=[bx, bbc], writes=[bx], tag=f"{tag}_mul{n}_{c}")
            if w_tiled_dram is not None:
                P.op("vector", lambda e, c=c, s=src: e.tensor_scalar_mul(
                    s[:, c * 512:(c + 1) * 512], s[:, c * 512:(c + 1) * 512],
                    wln[:, c:c + 1]),
                    reads=[bsrc, bw], writes=[bsrc], tag=f"{tag}_w{n}_{c}")
            P.op("vector", lambda e, c=c, i=i, s=src: e.tensor_copy(
                out_t[i][:, c * 512:(c + 1) * 512], s[:, c * 512:(c + 1) * 512]),
                reads=[bsrc], writes=[bo], tag=f"{tag}_obf{n}_{c}")
        for oi, (od, odt) in enumerate(outs):
            srcsb = out_t[i] if odt == BF16 else out32_t[i]
            srcb = bo if odt == BF16 else bo32
            P.op("sync", lambda e, n=n, s=srcsb, od=od: e.dma_start(
                out=od.rearrange("(c p) t -> p c t", p=128)[:, :, n * 512:(n + 1) * 512],
                in_=s.ap().rearrange("p (c t) -> p c t", c=DCH)),
                reads=[srcb], dma=True, tag=f"{tag}_st{n}_{oi}")



from contextlib import ExitStack


def build_launch(dram_specs, const_names, phase_fns):
    nc = bass.Bass("TRN2", target_bir_lowering=False, debug=False)
    draps = {}
    for nm, (shape, dtype, kind) in dram_specs.items():
        draps[nm] = nc.dram_tensor(nm, list(shape), dtype, kind=kind).ap()
    with ExitStack() as es:
        alloc = Alloc(nc, es)
        P = Prog(nc)
        ctx = Ctx(nc, P, alloc)
        if const_names:
            cd = {nm: draps["c_" + nm] for nm in const_names}
            emit_load_constants(ctx, cd, const_names)
        for f in phase_fns:
            f(ctx, draps)
        block = es.enter_context(nc.Block())
        P.finalize(block, es, nc)
    return nc


_NC_CACHE = {}


def _run(cache_key, dram_specs, const_names, phase_fns, in_maps):
    if cache_key not in _NC_CACHE:
        _NC_CACHE[cache_key] = build_launch(dram_specs, const_names, phase_fns)
    nc = _NC_CACHE[cache_key]
    res = run_bass_kernel_spmd(nc, in_maps, core_ids=list(range(N_CORES)))
    return res.results


CONST_HOST = None


def get_const_host():
    global CONST_HOST
    if CONST_HOST is None:
        c = host_constants()
        CONST_HOST = {k[2:]: v.astype(np.float32) for k, v in c.items()}
        CONST_HOST["ident"] = np.eye(128, dtype=np.float32)
    return CONST_HOST


def const_specs(names):
    ch = get_const_host()
    return {("c_" + n): (ch[n].shape, F32, "ExternalInput") for n in names}


def const_inmaps(names):
    ch = get_const_host()
    return {("c_" + n): ch[n] for n in names}



def run_ln0(x_fm_cores):
    names = ["ones_row", "ones_col"]
    specs = {
        "x_fm": ((D, T), F32, "ExternalInput"),
        "xn0": ((D, T), BF16, "ExternalOutput"),
        **const_specs(names),
    }

    def ph(ctx, dr):
        emit_ln(ctx, dr["x_fm"], [(dr["xn0"], BF16)], kind="ln", tag="ln0")

    im = [dict(x_fm=x, **const_inmaps(names)) for x in x_fm_cores]
    res = _run("ln0", specs, names, [ph], im)
    return [r["xn0"] for r in res]


def run_mlp(xn_cores, w1, b1, w2, b2, mode, tag):
    names = ["ones_row", "ones_col"]
    out_spec = ((D, T), BF16) if mode == "fm" else ((T, 16 * 65), BF16)
    specs = {
        "xn": ((D, T), BF16, "ExternalInput"),
        "w1": ((D, 4096), F32, "ExternalInput"),
        "b1": ((128, 32), F32, "ExternalInput"),
        "w2": ((4096, D), F32, "ExternalInput"),
        "b2": ((128, 8), F32, "ExternalInput"),
        "out": (out_spec[0], out_spec[1], "ExternalOutput"),
        **const_specs(names),
    }
    if mode != "fm":
        specs["b2row"] = ((1, D), F32, "ExternalInput")

    def ph(ctx, dr):
        emit_mlp(ctx, dr["xn"], dr["w1"], dr["b1"], dr["w2"], dr["b2"],
                 mode, dr["out"], b2row_dram=dr.get("b2row"), tag="mlp")

    base = dict(w1=np.ascontiguousarray(w1), b1=_tile_bias(b1),
                w2=np.ascontiguousarray(w2), b2=_tile_bias(b2),
                **const_inmaps(names))
    if mode != "fm":
        base["b2row"] = np.ascontiguousarray(b2.reshape(1, D))
    im = [dict(xn=xn, **base) for xn in xn_cores]
    res = _run(("mlp", mode), specs, names, [ph], im)
    return [r["out"] for r in res]


def emit_linear(ctx, x_dram, w_dram, b_dram, out_mode, out_dram, din, dout,
                act=None, out_dtype=BF16, b_row_dram=None, x2_dram=None,
                res_drams=(), aug=False, tag="lin"):
    nc, P, a = ctx.nc, ctx.P, ctx.alloc
    KCH = din // 128
    MCH = dout // 128
    w_sb = a.sb(f"{tag}_w", [128, KCH * dout], BF16)
    stage = [a.sb(f"{tag}_st{i}", [128, 2048], F32) for i in range(2)]
    stageb = [ctx.buf(f"{tag}_st{i}") for i in range(2)]
    b_w = ctx.buf(f"{tag}_w")

    cw = min(2048, dout)
    j = 0
    for k in range(KCH):
        for half in range(dout // cw):
            st, stb = stage[j % 2], stageb[j % 2]
            j += 1
            P.op("sync", lambda e, k=k, h=half, st=st: e.dma_start(
                out=st[:, 0:cw], in_=w_dram[k * 128:(k + 1) * 128,
                                            h * cw:(h + 1) * cw]),
                writes=[stb], dma=True, tag=f"{tag}_wld{k}_{half}")
            P.op("vector", lambda e, k=k, h=half, st=st: e.tensor_copy(
                w_sb[:, k * dout + h * cw: k * dout + (h + 1) * cw], st[:, 0:cw]),
                reads=[stb], writes=[b_w], tag=f"{tag}_wc{k}_{half}")

    if b_dram is not None and out_mode == "fm":
        b_sb = a.sb(f"{tag}_b", [128, MCH], F32)
        b_b = ctx.buf(f"{tag}_b")
        P.op("sync", lambda e: e.dma_start(out=b_sb[:], in_=b_dram),
             writes=[b_b], dma=True, tag=f"{tag}_bld")
    else:
        b_sb, b_b = None, None

    x_t = [a.sb(f"{tag}_x{i}", [128, KCH * 512], BF16) for i in range(2)]
    if x2_dram is not None:
        x2_t = [a.sb(f"{tag}_x2{i}", [128, KCH * 512], BF16) for i in range(2)]

    nres = len(res_drams)
    if nres:
        r_t = [a.sb(f"{tag}_r{i}", [128, DCH * 512], F32) for i in range(nres)]

    tmp = a.sb(f"{tag}_tmp", [128, 512], F32)
    b_tmp = ctx.buf(f"{tag}_tmp")
    if out_mode == "fm":
        ot = a.sb(f"{tag}_o", [128, MCH * 512],
                  F32 if (out_dtype == F32 or nres) else BF16)
        b_ot = ctx.buf(f"{tag}_o")
    else:
        nh = dout // 64
        width = nh * 65 if aug else dout
        vaug = a.sb(f"{tag}_vaug", [128, width], BF16)
        b_vaug = ctx.buf(f"{tag}_vaug")
        P.op("vector", lambda e: e.memset(vaug[:], 1.0),
             writes=[b_vaug], tag=f"{tag}_vmemset")
        if b_row_dram is not None:
            b2r_sb = a.sb(f"{tag}_b2r", [1, dout], F32)
            b2rb_sb = a.sb(f"{tag}_b2rb", [1, dout], BF16)
            b2bc_sb = a.sb(f"{tag}_b2bc", [128, dout], F32)
            b_b2r = ctx.buf(f"{tag}_b2r")
            b_b2bc = ctx.buf(f"{tag}_b2bc")
            P.op("sync", lambda e: e.dma_start(out=b2r_sb[:], in_=b_row_dram),
                 writes=[b_b2r], dma=True, tag=f"{tag}_b2rld")
            P.op("vector", lambda e: e.tensor_copy(b2rb_sb[:], b2r_sb[:]),
                 reads=[b_b2r], writes=[b_b2r], tag=f"{tag}_b2rc")
    p_mm = [a.ps(f"{tag}_p{i}", [128, 512], F32) for i in range(4)]
    bp = [ctx.buf(f"{tag}_p{i}") for i in range(4)]
    if out_mode == "tm" and b_row_dram is not None:
        segw = min(512, dout)
        for seg in range(dout // segw):
            P.op("tensor", lambda e, s=seg: e.matmul(
                p_mm[3][:, 0:segw], ctx.cb16["ones_row"],
                b2rb_sb[0:1, s * segw:(s + 1) * segw], start=True, stop=True),
                reads=[b_b2r, ctx.cbuf], writes=[bp[3]], tag=f"{tag}_bbc{seg}")
            P.op("scalar", lambda e, s=seg: e.copy(
                b2bc_sb[:, s * segw:(s + 1) * segw], p_mm[3][:, 0:segw]),
                reads=[bp[3]], writes=[b_b2bc], tag=f"{tag}_bbcv{seg}")

    def evac_act(dst_ap, p_ap, bias_ap, bsrcs, bdst, t2):
        if act is None or act == "gelu" or act == "silu":
            func = {None: AF.Identity, "gelu": AF.Gelu, "silu": AF.Silu}[act]
            if bias_ap is not None:
                P.op("scalar", lambda e: e.activation(
                    out=dst_ap, in_=p_ap, func=func, bias=bias_ap),
                    reads=bsrcs, writes=[bdst], tag=t2)
            else:
                P.op("scalar", lambda e: e.activation(
                    out=dst_ap, in_=p_ap, func=func),
                    reads=bsrcs, writes=[bdst], tag=t2)
        elif act == "elu1":
            P.op("vector", lambda e: e.tensor_scalar_min(tmp[:], p_ap, 0.0),
                 reads=bsrcs, writes=[b_tmp], tag=t2 + "_min")
            P.op("scalar", lambda e: e.activation(
                out=tmp[:], in_=tmp[:], func=AF.Exp),
                reads=[b_tmp], writes=[b_tmp], tag=t2 + "_exp")
            P.op("vector", lambda e: e.scalar_tensor_tensor(
                dst_ap, p_ap, 0.0, tmp[:], ALU.max, ALU.add),
                reads=bsrcs + [b_tmp], writes=[bdst], tag=t2 + "_add")
        else:
            raise ValueError(act)

    for n in range(NT):
        i = n % 2
        b_x = ctx.buf(f"{tag}_x{i}")
        P.op("sync", lambda e, n=n, i=i: e.dma_start(
            out=x_t[i].ap().rearrange("p (c t) -> p c t", c=KCH),
            in_=x_dram.rearrange("(c p) t -> p c t", p=128)[:, :, n * 512:(n + 1) * 512]),
            writes=[b_x], dma=True, tag=f"{tag}_xld{n}")
        if x2_dram is not None:
            b_x2 = ctx.buf(f"{tag}_x2{i}")
            P.op("sync", lambda e, n=n, i=i: e.dma_start(
                out=x2_t[i].ap().rearrange("p (c t) -> p c t", c=KCH),
                in_=x2_dram.rearrange("(c p) t -> p c t", p=128)[:, :, n * 512:(n + 1) * 512]),
                writes=[b_x2], dma=True, tag=f"{tag}_x2ld{n}")
            P.op("vector", lambda e, i=i: e.tensor_mul(
                x_t[i][:], x_t[i][:], x2_t[i][:]),
                reads=[b_x, b_x2], writes=[b_x], tag=f"{tag}_xmul{n}")
        if nres:
            for ri, rd in enumerate(res_drams):
                b_r = ctx.buf(f"{tag}_r{ri}")
                P.op("sync", lambda e, n=n, rd=rd, ri=ri: e.dma_start(
                    out=r_t[ri].ap().rearrange("p (c t) -> p c t", c=DCH),
                    in_=rd.rearrange("(c p) t -> p c t", p=128)[:, :, n * 512:(n + 1) * 512]),
                    writes=[b_r], dma=True, tag=f"{tag}_rld{n}_{ri}")

        if out_mode == "fm":
            for m in range(MCH):
                bk = m % 3
                for k in range(KCH):
                    P.op("tensor", lambda e, m=m, k=k, i=i, bk=bk: e.matmul(
                        p_mm[bk][:], w_sb[:, k * dout + m * 128: k * dout + (m + 1) * 128],
                        x_t[i][:, k * 512:(k + 1) * 512],
                        start=(k == 0), stop=(k == KCH - 1)),
                        reads=[b_w, b_x], writes=[bp[bk]],
                        tag=f"{tag}_mm{n}_{m}_{k}")
                bias_ap = b_sb[:, m:m + 1] if b_sb is not None else None
                rsrc = [bp[bk]] + ([b_b] if b_sb is not None else [])
                evac_act(ot[:, m * 512:(m + 1) * 512], p_mm[bk][:], bias_ap,
                         rsrc, b_ot, f"{tag}_ev{n}_{m}")
            if nres:
                for ri in range(nres):
                    b_r = ctx.buf(f"{tag}_r{ri}")
                    P.op("vector", lambda e, ri=ri: e.tensor_add(
                        ot[:], ot[:], r_t[ri][:]),
                        reads=[ctx.buf(f"{tag}_r{ri}"), b_ot], writes=[b_ot],
                        tag=f"{tag}_res{n}_{ri}")
            ot_is_f32 = (out_dtype == F32 or nres)
            store_q = "gpsimd" if (ot_is_f32 and out_dtype == BF16) else "sync"
            P.op(store_q, lambda e, n=n: e.dma_start(
                out=out_dram.rearrange("(c p) t -> p c t", p=128)[:, :, n * 512:(n + 1) * 512],
                in_=ot.ap().rearrange("p (c t) -> p c t", c=MCH)),
                reads=[b_ot], dma=True, tag=f"{tag}_ost{n}")
        else:
            segw = min(512, dout)
            for msub in range(4):
                for n2 in range(dout // segw):
                    bk = n2 % 2
                    for k in range(KCH):
                        P.op("tensor", lambda e, msub=msub, n2=n2, k=k, bk=bk, i=i: e.matmul(
                            p_mm[bk][:, 0:segw],
                            x_t[i][:, k * 512 + msub * 128: k * 512 + (msub + 1) * 128],
                            w_sb[:, k * dout + n2 * segw: k * dout + (n2 + 1) * segw],
                            start=(k == 0), stop=(k == KCH - 1)),
                            reads=[b_w, b_x], writes=[bp[bk]],
                            tag=f"{tag}_tmm{n}_{msub}_{n2}_{k}")
                    nh_seg = segw // 64
                    if aug:
                        dst = vaug.ap().rearrange("p (h s) -> p h s", s=65)[
                            :, n2 * nh_seg:(n2 + 1) * nh_seg, 0:64]
                    else:
                        dst = vaug[:, n2 * segw:(n2 + 1) * segw]
                    if act == "elu1":
                        evac_act(dst, p_mm[bk][:, 0:segw], None, [bp[bk]],
                                 b_vaug, f"{tag}_tev{n}_{msub}_{n2}")
                    elif b_row_dram is not None:
                        P.op("vector", lambda e, n2=n2, bk=bk, dst=dst: e.tensor_add(
                            dst,
                            p_mm[bk].ap()[:, 0:segw].rearrange("p (h s) -> p h s", s=64)
                            if aug else p_mm[bk][:, 0:segw],
                            b2bc_sb.ap()[:, n2 * segw:(n2 + 1) * segw].rearrange(
                                "p (h s) -> p h s", s=64)
                            if aug else b2bc_sb[:, n2 * segw:(n2 + 1) * segw]),
                            reads=[bp[bk], b_b2bc], writes=[b_vaug],
                            tag=f"{tag}_tev{n}_{msub}_{n2}")
                    else:
                        if aug:
                            src = p_mm[bk].ap()[:, 0:segw].rearrange(
                                "p (h s) -> p h s", s=64)
                        else:
                            src = p_mm[bk][:, 0:segw]
                        P.op("vector", lambda e, dst=dst, src=src: e.tensor_copy(
                            dst, src),
                            reads=[bp[bk]], writes=[b_vaug],
                            tag=f"{tag}_tev{n}_{msub}_{n2}")
                P.op("sync", lambda e, n=n, msub=msub: e.dma_start(
                    out=out_dram[n * 512 + msub * 128: n * 512 + (msub + 1) * 128, :],
                    in_=vaug[:]),
                    reads=[b_vaug], dma=True, tag=f"{tag}_tst{n}_{msub}")


def emit_attn(ctx, q_dram, k_dram, k_rep, vaug_dram, nheads, kvheads,
              proj_w_dram, proj_b_dram, out_dram, res_drams=(), tag="attn"):
    nc, P, a = ctx.nc, ctx.P, ctx.alloc
    W = kvheads * 65
    k_sb = a.sb(f"{tag}_k", [128, DCH * T], BF16)
    v_sb = a.sb(f"{tag}_v", [128, 32 * W], BF16)
    q_t = [a.sb(f"{tag}_q0", [128, DCH * 512], BF16)]
    pt_sb = a.sb(f"{tag}_pt", [128, 4 * 512], BF16)
    cneg_sb = a.sb(f"{tag}_cneg", [128, nheads], F32)
    mx_sb = a.sb(f"{tag}_mx", [128, 2], F32)
    csc_sb = a.sb(f"{tag}_csc", [2, 1], F32)
    trow_sb = a.sb(f"{tag}_trow", [2, 128], F32)
    diag2_sb = a.sb(f"{tag}_diag", [2, 2], F32)
    srow_sb = a.sb(f"{tag}_srow", [1, 1024], F32)
    rbc_sb = a.sb(f"{tag}_rbc", [64, 1024], F32)
    ao_sb = a.sb(f"{tag}_ao", [128, DCH * 512], BF16)
    pw_sb = a.sb(f"{tag}_pw", [128, DCH * D], BF16)
    pb_sb = a.sb(f"{tag}_pb", [128, DCH], F32)
    stage = [a.sb(f"{tag}_st{i}", [128, 1024], F32) for i in range(1)]
    stageb = [ctx.buf(f"{tag}_st{i}") for i in range(1)]
    ot = a.sb(f"{tag}_ot", [128, 2 * 512], F32)
    r_t = [a.sb(f"{tag}_r{i}", [128, 2 * 512], F32) for i in range(len(res_drams))]

    p_sc = [a.ps(f"{tag}_psc{i}", [128, 512], F32) for i in range(4)]
    p_av = [a.ps(f"{tag}_pav{i}", [65, 512], F32) for i in range(2)]
    p_bc = a.ps(f"{tag}_pbc", [128, 512], F32)
    bsc = [ctx.buf(f"{tag}_psc{i}") for i in range(4)]
    bav = [ctx.buf(f"{tag}_pav{i}") for i in range(2)]
    bbc = ctx.buf(f"{tag}_pbc")
    b_k = ctx.buf(f"{tag}_k")
    b_v = ctx.buf(f"{tag}_v")
    b_pt = ctx.buf(f"{tag}_pt")
    b_cn = ctx.buf(f"{tag}_cneg")
    b_mx = ctx.buf(f"{tag}_mx")
    b_csc = ctx.buf(f"{tag}_csc")
    b_trow = ctx.buf(f"{tag}_trow")
    b_diag = ctx.buf(f"{tag}_diag")
    b_srow = ctx.buf(f"{tag}_srow")
    b_rbc = ctx.buf(f"{tag}_rbc")
    b_ao = ctx.buf(f"{tag}_ao")
    b_pw = ctx.buf(f"{tag}_pw")
    b_pb = ctx.buf(f"{tag}_pb")
    b_ot = ctx.buf(f"{tag}_ot")

    if not k_rep:
        for c in range(DCH):
            P.op("sync", lambda e, c=c: e.dma_start(
                out=k_sb[:, c * T:(c + 1) * T],
                in_=k_dram[c * 128:(c + 1) * 128, :]),
                writes=[b_k], dma=True, tag=f"{tag}_kld{c}")
    else:
        for c in range(DCH):
            j = c // 2
            for half in range(2):
                P.op("sync", lambda e, c=c, j=j, half=half: e.dma_start(
                    out=k_sb[half * 64:(half + 1) * 64, c * T:(c + 1) * T],
                    in_=k_dram[j * 64:(j + 1) * 64, :]),
                    writes=[b_k], dma=True, tag=f"{tag}_kld{c}_{half}")
    P.op("sync", lambda e: e.dma_start(
        out=v_sb.ap().rearrange("p (t w) -> p t w", w=W),
        in_=vaug_dram.rearrange("(t p) w -> p t w", p=128)),
        writes=[b_v], dma=True, tag=f"{tag}_vld")
    for k in range(DCH):
        st, stb = stage[0], stageb[0]
        P.op("sync", lambda e, k=k, st=st: e.dma_start(
            out=st[:, 0:1024], in_=proj_w_dram[k * 128:(k + 1) * 128, :]),
            writes=[stb], dma=True, tag=f"{tag}_pwld{k}")
        P.op("vector", lambda e, k=k, st=st: e.tensor_copy(
            pw_sb[:, k * D:(k + 1) * D], st[:, 0:1024]),
            reads=[stb], writes=[b_pw], tag=f"{tag}_pwc{k}")
    P.op("sync", lambda e: e.dma_start(out=pb_sb[:], in_=proj_b_dram),
         writes=[b_pb], dma=True, tag=f"{tag}_pbld")

    ones_row32 = ctx.cf32["ones_row"]
    ones_all32 = ctx.cf32["ones_all"]
    ident32 = ctx.cf32["ident"]
    first_done = [False] * nheads

    for qt in range(NT):
        i = 0
        b_q = ctx.buf(f"{tag}_q0")
        P.op("sync", lambda e, qt=qt, i=i: e.dma_start(
            out=q_t[i].ap().rearrange("p (c t) -> p c t", c=DCH),
            in_=q_dram.rearrange("(c p) t -> p c t", p=128)[:, :, qt * 512:(qt + 1) * 512]),
            writes=[b_q], dma=True, tag=f"{tag}_qld{qt}")

        for c in range(nheads // 2):
            for hh in range(2):
                h = 2 * c + hh
                pass
            for t in range(32):
                for hh in range(2):
                    sbk = 2 * (t % 2) + hh
                    P.op("tensor", lambda e, c=c, hh=hh, t=t, i=i, sbk=sbk: e.matmul(
                        p_sc[sbk][:],
                        k_sb[hh * 64:(hh + 1) * 64, c * T + t * 128: c * T + (t + 1) * 128],
                        q_t[i][hh * 64:(hh + 1) * 64, c * 512:(c + 1) * 512],
                        start=True, stop=True),
                        reads=[b_k, b_q], writes=[bsc[sbk]],
                        tag=f"{tag}_smm{qt}_{c}_{t}_{hh}")
                if not first_done[c]:
                    first_done[c] = True
                    for hh in range(2):
                        P.op("vector", lambda e, hh=hh: e.reduce_max(
                            mx_sb[:, hh:hh + 1], p_sc[hh][:], axis=AX.X),
                            reads=[bsc[hh]], writes=[b_mx], tag=f"{tag}_cm1_{c}_{hh}")
                    P.op("tensor", lambda e: e.transpose(
                        p_bc[0:2, 0:128], mx_sb[:, 0:2], ident32),
                        reads=[b_mx, ctx.cbuf], writes=[bbc], tag=f"{tag}_ctr_{c}")
                    P.op("scalar", lambda e: e.copy(
                        srow_sb[0:2, 0:128] if False else trow_sb[:],
                        p_bc[0:2, 0:128]),
                        reads=[bbc], writes=[b_trow], tag=f"{tag}_cte_{c}")
                    P.op("vector", lambda e: e.reduce_max(
                        csc_sb[0:2, 0:1], trow_sb[:], axis=AX.X),
                        reads=[b_trow], writes=[b_csc], tag=f"{tag}_cm2_{c}")
                    P.op("vector", lambda e: e.tensor_scalar_mul(
                        csc_sb[0:2, 0:1], csc_sb[0:2, 0:1], -0.125),
                        reads=[b_csc], writes=[b_csc], tag=f"{tag}_cns_{c}")
                    P.op("vector", lambda e: e.tensor_mul(
                        diag2_sb[:], csc_sb[0:2, 0:1].to_broadcast([2, 2]),
                        ident32[0:2, 0:2]),
                        reads=[b_csc, ctx.cbuf], writes=[b_diag],
                        tag=f"{tag}_cdg_{c}")
                    P.op("tensor", lambda e: e.matmul(
                        p_bc[:, 0:2], ones_all32[0:2, 0:128], diag2_sb[:],
                        start=True, stop=True),
                        reads=[b_diag, ctx.cbuf], writes=[bbc], tag=f"{tag}_cbc_{c}")
                    P.op("scalar", lambda e, c=c: e.copy(
                        cneg_sb[:, 2 * c:2 * c + 2], p_bc[:, 0:2]),
                        reads=[bbc], writes=[b_cn], tag=f"{tag}_cev_{c}")
                for hh in range(2):
                    h = 2 * c + hh
                    sbk = 2 * (t % 2) + hh
                    P.op("scalar", lambda e, h=h, hh=hh, sbk=sbk: e.activation(
                        out=pt_sb[:, sbk * 512:(sbk + 1) * 512], in_=p_sc[sbk][:],
                        func=AF.Exp, bias=cneg_sb[:, h:h + 1], scale=0.125),
                        reads=[bsc[sbk], b_cn], writes=[b_pt],
                        tag=f"{tag}_exp{qt}_{c}_{t}_{hh}")
                    kv = h * kvheads // nheads
                    P.op("tensor", lambda e, t=t, kv=kv, hh=hh, sbk=sbk: e.matmul(
                        p_av[hh][:],
                        v_sb[:, t * W + kv * 65: t * W + kv * 65 + 65],
                        pt_sb[:, sbk * 512:(sbk + 1) * 512],
                        start=(t == 0), stop=(t == 31)),
                        reads=[b_v, b_pt], writes=[bav[hh]],
                        tag=f"{tag}_av{qt}_{c}_{t}_{hh}")
            for hh in range(2):
                h = 2 * c + hh
                P.op("scalar", lambda e, hh=hh: e.copy(
                    srow_sb[0:1, hh * 512:(hh + 1) * 512], p_av[hh][64:65, :]),
                    reads=[bav[hh]], writes=[b_srow], tag=f"{tag}_sev{qt}_{h}")
                P.op("vector", lambda e, hh=hh: e.reciprocal(
                    srow_sb[0:1, hh * 512:(hh + 1) * 512],
                    srow_sb[0:1, hh * 512:(hh + 1) * 512]),
                    reads=[b_srow], writes=[b_srow], tag=f"{tag}_rcp{qt}_{h}")
                P.op("tensor", lambda e, hh=hh: e.matmul(
                    p_bc[0:64, :], ones_row32[0:1, 0:64],
                    srow_sb[0:1, hh * 512:(hh + 1) * 512],
                    start=True, stop=True),
                    reads=[b_srow, ctx.cbuf], writes=[bbc], tag=f"{tag}_rbc{qt}_{h}")
                P.op("scalar", lambda e, hh=hh: e.copy(
                    rbc_sb[:, hh * 512:(hh + 1) * 512], p_bc[0:64, :]),
                    reads=[bbc], writes=[b_rbc], tag=f"{tag}_rbe{qt}_{h}")
                P.op("vector", lambda e, c=c, hh=hh: e.tensor_mul(
                    ao_sb[hh * 64:(hh + 1) * 64, c * 512:(c + 1) * 512],
                    p_av[hh].ap()[0:64, :],
                    rbc_sb[:, hh * 512:(hh + 1) * 512]),
                    reads=[bav[hh], b_rbc], writes=[b_ao],
                    tag=f"{tag}_nrm{qt}_{h}")
        for mg in range(DCH // 2):
            for ri, rd in enumerate(res_drams):
                b_r = ctx.buf(f"{tag}_r{ri}")
                P.op("sync", lambda e, qt=qt, rd=rd, ri=ri, mg=mg: e.dma_start(
                    out=r_t[ri].ap().rearrange("p (c t) -> p c t", c=2),
                    in_=rd.rearrange("(c p) t -> p c t", p=128)[
                        :, mg * 2:(mg + 1) * 2, qt * 512:(qt + 1) * 512]),
                    writes=[b_r], dma=True, tag=f"{tag}_rld{qt}_{mg}_{ri}")
            for mm in range(2):
                m = mg * 2 + mm
                pbk = m % 2
                for k in range(DCH):
                    P.op("tensor", lambda e, m=m, k=k, pbk=pbk: e.matmul(
                        p_sc[pbk][:], pw_sb[:, k * D + m * 128: k * D + (m + 1) * 128],
                        ao_sb[:, k * 512:(k + 1) * 512],
                        start=(k == 0), stop=(k == DCH - 1)),
                        reads=[b_pw, b_ao], writes=[bsc[pbk]],
                        tag=f"{tag}_pmm{qt}_{m}_{k}")
                P.op("scalar", lambda e, mm=mm, pbk=pbk, m=m: e.activation(
                    out=ot[:, mm * 512:(mm + 1) * 512], in_=p_sc[pbk][:],
                    func=AF.Identity, bias=pb_sb[:, m:m + 1]),
                    reads=[bsc[pbk], b_pb], writes=[b_ot], tag=f"{tag}_pev{qt}_{m}")
            for ri in range(len(res_drams)):
                P.op("vector", lambda e, ri=ri: e.tensor_add(
                    ot[:], ot[:], r_t[ri][:]),
                    reads=[ctx.buf(f"{tag}_r{ri}"), b_ot], writes=[b_ot],
                    tag=f"{tag}_pres{qt}_{mg}_{ri}")
            P.op("sync", lambda e, qt=qt, mg=mg: e.dma_start(
                out=out_dram.rearrange("(c p) t -> p c t", p=128)[
                    :, mg * 2:(mg + 1) * 2, qt * 512:(qt + 1) * 512],
                in_=ot.ap().rearrange("p (c t) -> p c t", c=2)),
                reads=[b_ot], dma=True, tag=f"{tag}_ost{qt}_{mg}")


def run_attn(q_cores, k_cores, vaug_cores, proj_w, proj_b, nheads, kvheads,
             k_rep, res_cores=None, cache_key="attn0"):
    names = ["ones_row", "ones_col", "ident", "ones_all"]
    kshape = (256, T) if k_rep else (D, T)
    specs = {
        "q": ((D, T), BF16, "ExternalInput"),
        "k": (kshape, BF16, "ExternalInput"),
        "vaug": ((T, kvheads * 65), BF16, "ExternalInput"),
        "pw": ((D, D), F32, "ExternalInput"),
        "pb": ((128, 8), F32, "ExternalInput"),
        "out": ((D, T), F32, "ExternalOutput"),
        **const_specs(names),
    }
    if res_cores is not None:
        specs["res0"] = ((D, T), F32, "ExternalInput")

    def ph(ctx, dr):
        emit_attn(ctx, dr["q"], dr["k"], k_rep, dr["vaug"], nheads, kvheads,
                  dr["pw"], dr["pb"], dr["out"],
                  res_drams=[dr["res0"]] if res_cores is not None else (),
                  tag="at")

    base = dict(pw=np.ascontiguousarray(proj_w), pb=_tile_bias(proj_b),
                **const_inmaps(names))
    im = []
    for ci in range(N_CORES):
        m = dict(q=q_cores[ci], k=k_cores[ci], vaug=vaug_cores[ci], **base)
        if res_cores is not None:
            m["res0"] = res_cores[ci]
        im.append(m)
    res = _run(cache_key, specs, names, [ph], im)
    return [r["out"] for r in res]


def emit_add(ctx, in_drams, out_dram, out_dtype=F32, tag="add"):
    nc, P, a = ctx.nc, ctx.P, ctx.alloc
    ts = [a.sb(f"{tag}_t{i}", [128, DCH * 512], F32) for i in range(2)]
    acc = a.sb(f"{tag}_acc", [128, DCH * 512], out_dtype)
    b_acc = ctx.buf(f"{tag}_acc")
    for n in range(NT):
        for ii, ind in enumerate(in_drams):
            i = ii % 2
            b_t = ctx.buf(f"{tag}_t{i}")
            P.op("sync", lambda e, n=n, ind=ind, i=i: e.dma_start(
                out=ts[i].ap().rearrange("p (c t) -> p c t", c=DCH),
                in_=ind.rearrange("(c p) t -> p c t", p=128)[:, :, n * 512:(n + 1) * 512]),
                writes=[b_t], dma=True, tag=f"{tag}_ld{n}_{ii}")
            if ii == 0:
                continue
            if ii == 1:
                P.op("vector", lambda e: e.tensor_add(acc[:], ts[0][:], ts[1][:]),
                     reads=[ctx.buf(f"{tag}_t0"), ctx.buf(f"{tag}_t1")],
                     writes=[b_acc], tag=f"{tag}_a{n}_{ii}")
            else:
                P.op("vector", lambda e, i=i: e.tensor_add(acc[:], acc[:], ts[i][:]),
                     reads=[b_acc, b_t], writes=[b_acc], tag=f"{tag}_a{n}_{ii}")
        P.op("sync", lambda e, n=n: e.dma_start(
            out=out_dram.rearrange("(c p) t -> p c t", p=128)[:, :, n * 512:(n + 1) * 512],
            in_=acc.ap().rearrange("p (c t) -> p c t", c=DCH)),
            reads=[b_acc], dma=True, tag=f"{tag}_st{n}")


def emit_ffn0_route(ctx, xn_dram, rwT_dram, cb_dram, cmb_dram, tag="rt"):
    nc, P, a = ctx.nc, ctx.P, ctx.alloc
    rw_sb = a.sb(f"{tag}_rw", [128, DCH * 16], BF16)
    rw32 = a.sb(f"{tag}_rw32", [128, DCH * 16], F32)
    cbr = a.sb(f"{tag}_cbr", [1, 16], F32)
    cbb = a.sb(f"{tag}_cbb", [128, 16], F32)
    xn_t = a.sb(f"{tag}_xn", [128, DCH * 512], BF16)
    pr_sb = a.sb(f"{tag}_pr", [128, 16], F32)
    sbv = a.sb(f"{tag}_sbv", [128, 16], F32)
    scratch = a.sb(f"{tag}_scr", [128, 8], F32)
    scratch2 = a.sb(f"{tag}_sc2", [128, 8], F32)
    pairbuf = a.sb(f"{tag}_pb", [128, 24], F32)
    gsc = a.sb(f"{tag}_gsc", [128, 4], F32)
    gmask = a.sb(f"{tag}_gm", [128, 4], F32)
    sel = a.sb(f"{tag}_sel", [128, 16], F32)
    den = a.sb(f"{tag}_den", [128, 1], F32)
    cmb = a.sb(f"{tag}_cmb", [128, 16], BF16)
    cemT = a.sb(f"{tag}_cemT", [16, 512], BF16)
    p_lg = a.ps(f"{tag}_plg", [128, 16], F32)
    p_tr = a.ps(f"{tag}_ptr", [16, 128], BF16)
    bufs = {n: ctx.buf(f"{tag}_{n}") for n in
            ("rw", "rw32", "cbr", "cbb", "xn", "pr", "sbv", "scr", "sc2",
             "pb", "gsc", "gm", "sel", "den", "cmb", "cemT", "plg", "ptr")}
    identb = ctx.cb16["ident"]

    P.op("sync", lambda e: e.dma_start(
        out=rw32.ap().rearrange("p (c o) -> p c o", c=DCH),
        in_=rwT_dram.rearrange("(c p) o -> p c o", p=128)),
        writes=[bufs["rw32"]], dma=True, tag=f"{tag}_rwld")
    P.op("vector", lambda e: e.tensor_copy(rw_sb[:], rw32[:]),
         reads=[bufs["rw32"]], writes=[bufs["rw"]], tag=f"{tag}_rwc")
    P.op("sync", lambda e: e.dma_start(out=cbr[:], in_=cb_dram),
         writes=[bufs["cbr"]], dma=True, tag=f"{tag}_cbld")
    P.op("tensor", lambda e: e.matmul(
        p_lg[:], ctx.cf32["ones_row"], cbr[:], start=True, stop=True),
        reads=[bufs["cbr"], ctx.cbuf], writes=[bufs["plg"]], tag=f"{tag}_cbbm")
    P.op("scalar", lambda e: e.copy(cbb[:], p_lg[:]),
         reads=[bufs["plg"]], writes=[bufs["cbb"]], tag=f"{tag}_cbbe")

    for n in range(NT):
        P.op("sync", lambda e, n=n: e.dma_start(
            out=xn_t.ap().rearrange("p (c t) -> p c t", c=DCH),
            in_=xn_dram.rearrange("(c p) t -> p c t", p=128)[:, :, n * 512:(n + 1) * 512]),
            writes=[bufs["xn"]], dma=True, tag=f"{tag}_xnld{n}")
        for ms in range(4):
            for k in range(DCH):
                P.op("tensor", lambda e, ms=ms, k=k: e.matmul(
                    p_lg[:], xn_t[:, k * 512 + ms * 128: k * 512 + (ms + 1) * 128],
                    rw_sb[:, k * 16:(k + 1) * 16],
                    start=(k == 0), stop=(k == DCH - 1)),
                    reads=[bufs["xn"], bufs["rw"]], writes=[bufs["plg"]],
                    tag=f"{tag}_lmm{n}_{ms}_{k}")
            P.op("scalar", lambda e: e.activation(
                out=pr_sb[:], in_=p_lg[:], func=AF.Sigmoid),
                reads=[bufs["plg"]], writes=[bufs["pr"]], tag=f"{tag}_sig{n}_{ms}")
            P.op("vector", lambda e: e.tensor_add(sbv[:], pr_sb[:], cbb[:]),
                 reads=[bufs["pr"], bufs["cbb"]], writes=[bufs["sbv"]],
                 tag=f"{tag}_sb{n}_{ms}")
            def lane(ii):
                return sbv.ap().rearrange("p (g i) -> p i g", i=4)[
                    :, ii:ii + 1, :].rearrange("p i g -> p (i g)")
            pairs = [(0, 1), (0, 2), (0, 3), (1, 2), (1, 3), (2, 3)]
            for pi, (ii, jj) in enumerate(pairs):
                P.op("vector", lambda e, ii=ii, jj=jj, pi=pi: e.tensor_add(
                    pairbuf[:, pi * 4:(pi + 1) * 4], lane(ii), lane(jj)),
                    reads=[bufs["sbv"]], writes=[bufs["pb"]],
                    tag=f"{tag}_gp{n}_{ms}_{pi}")
            for pi in range(1, 6):
                dst = gsc[:] if pi == 5 else pairbuf[:, 0:4]
                bw = bufs["gsc"] if pi == 5 else bufs["pb"]
                P.op("vector", lambda e, pi=pi, dst=dst: e.tensor_max(
                    dst, pairbuf[:, 0:4], pairbuf[:, pi * 4:(pi + 1) * 4]),
                    reads=[bufs["pb"]], writes=[bw],
                    tag=f"{tag}_gmx{n}_{ms}_{pi}")
            P.op("vector", lambda e: e.memset(scratch[:], -1e30),
                 writes=[bufs["scr"]], tag=f"{tag}_gpad{n}_{ms}")
            P.op("vector", lambda e: e.tensor_copy(scratch[:, 0:4], gsc[:]),
                 reads=[bufs["gsc"]], writes=[bufs["scr"]],
                 tag=f"{tag}_gcp{n}_{ms}")
            P.op("vector", lambda e: e.max(out=scratch2[:], in_=scratch[:]),
                 reads=[bufs["scr"]], writes=[bufs["sc2"]],
                 tag=f"{tag}_g2{n}_{ms}")
            P.op("vector", lambda e: e.tensor_single_scalar(
                gmask[:], gsc[:], scratch2[:, 1:2], ALU.is_ge),
                reads=[bufs["gsc"], bufs["sc2"]], writes=[bufs["gm"]],
                tag=f"{tag}_gmk{n}_{ms}")
            for g in range(4):
                P.op("vector", lambda e, g=g: e.tensor_copy(
                    sel[:, g * 4:(g + 1) * 4],
                    gmask[:, g:g + 1].to_broadcast([128, 4])),
                    reads=[bufs["gm"]], writes=[bufs["sel"]],
                    tag=f"{tag}_gex{n}_{ms}_{g}")
            P.op("vector", lambda e: e.scalar_tensor_tensor(
                sbv[:], sbv[:], 100.0, sel[:], ALU.add, ALU.mult),
                reads=[bufs["sbv"], bufs["sel"]], writes=[bufs["sbv"]],
                tag=f"{tag}_msk{n}_{ms}")
            P.op("vector", lambda e: e.tensor_scalar_add(sbv[:], sbv[:], -100.0),
                 reads=[bufs["sbv"]], writes=[bufs["sbv"]],
                 tag=f"{tag}_msk2{n}_{ms}")
            P.op("vector", lambda e: e.max(out=scratch2[:], in_=sbv[:]),
                 reads=[bufs["sbv"]], writes=[bufs["sc2"]],
                 tag=f"{tag}_t8{n}_{ms}")
            P.op("vector", lambda e: e.tensor_single_scalar(
                sel[:], sbv[:], scratch2[:, 3:4], ALU.is_ge),
                reads=[bufs["sbv"], bufs["sc2"]], writes=[bufs["sel"]],
                tag=f"{tag}_sel{n}_{ms}")
            P.op("vector", lambda e: e.tensor_mul(sel[:], sel[:], pr_sb[:]),
                 reads=[bufs["sel"], bufs["pr"]], writes=[bufs["sel"]],
                 tag=f"{tag}_w{n}_{ms}")
            P.op("vector", lambda e: e.reduce_sum(den[:], sel[:], axis=AX.X),
                 reads=[bufs["sel"]], writes=[bufs["den"]],
                 tag=f"{tag}_den{n}_{ms}")
            P.op("vector", lambda e: e.tensor_scalar_add(den[:], den[:], 1e-20),
                 reads=[bufs["den"]], writes=[bufs["den"]],
                 tag=f"{tag}_dene{n}_{ms}")
            P.op("vector", lambda e: e.reciprocal(den[:], den[:]),
                 reads=[bufs["den"]], writes=[bufs["den"]],
                 tag=f"{tag}_denr{n}_{ms}")
            P.op("vector", lambda e: e.tensor_scalar_mul(cmb[:], sel[:], den[:]),
                 reads=[bufs["sel"], bufs["den"]], writes=[bufs["cmb"]],
                 tag=f"{tag}_cmb{n}_{ms}")
            P.op("tensor", lambda e: e.transpose(p_tr[:], cmb[:], identb),
                 reads=[bufs["cmb"], ctx.cbuf], writes=[bufs["ptr"]],
                 tag=f"{tag}_tr{n}_{ms}")
            P.op("scalar", lambda e, ms=ms: e.copy(
                cemT[:, ms * 128:(ms + 1) * 128], p_tr[:]),
                reads=[bufs["ptr"]], writes=[bufs["cemT"]],
                tag=f"{tag}_tre{n}_{ms}")
        P.op("sync", lambda e, n=n: e.dma_start(
            out=cmb_dram[:, n * 512:(n + 1) * 512], in_=cemT[:]),
            reads=[bufs["cemT"]], dma=True, tag=f"{tag}_st{n}")


def emit_ffn0_experts(ctx, xn_dram, cmb_dram, gate_dram, up_dram, down_dram,
                      experts, out_dram, accumulate, tag="ex"):
    nc, P, a = ctx.nc, ctx.P, ctx.alloc
    ne = len(experts)
    gw = a.sb(f"{tag}_gw", [128, ne * DCH * 256], BF16)
    uw = a.sb(f"{tag}_uw", [128, ne * DCH * 256], BF16)
    dw = a.sb(f"{tag}_dw", [128, ne * 2 * 1024], BF16)
    stage = a.sb(f"{tag}_st0", [128, 2048], F32)
    stb = ctx.buf(f"{tag}_st0")
    xn_t = a.sb(f"{tag}_xn", [128, DCH * 512], BF16)
    cem_t = a.sb(f"{tag}_cem", [16, 512], BF16)
    cbc = a.sb(f"{tag}_cbc", [128, 512], BF16)
    sg = a.sb(f"{tag}_sg", [128, 512], F32)
    h_all = a.sb(f"{tag}_h", [128, ne * 2 * 512], BF16)
    rout = a.sb(f"{tag}_ro", [128, 2 * 512], F32)
    p_gu = [a.ps(f"{tag}_pgu{i}", [128, 512], F32) for i in range(4)]
    p_d = [a.ps(f"{tag}_pd{i}", [128, 512], F32) for i in range(3)]
    p_cb = a.ps(f"{tag}_pcb", [128, 512], F32)
    b = {n: ctx.buf(f"{tag}_{n}") for n in
         ("gw", "uw", "dw", "xn", "cem", "cbc", "sg", "h", "ro", "pcb")}
    bgu = [ctx.buf(f"{tag}_pgu{i}") for i in range(4)]
    bd = [ctx.buf(f"{tag}_pd{i}") for i in range(3)]
    sel16 = ctx.cb16["sel16"]

    for ei, ex in enumerate(experts):
        for k in range(DCH):
            P.op("sync", lambda e, ex=ex, k=k: e.dma_start(
                out=stage[:, 0:256],
                in_=gate_dram[ex * 1024 + k * 128: ex * 1024 + (k + 1) * 128, :]),
                writes=[stb], dma=True, tag=f"{tag}_gld{ei}_{k}")
            P.op("vector", lambda e, ei=ei, k=k: e.tensor_copy(
                gw[:, (ei * DCH + k) * 256:(ei * DCH + k + 1) * 256],
                stage[:, 0:256]),
                reads=[stb], writes=[b["gw"]], tag=f"{tag}_gc{ei}_{k}")
            P.op("sync", lambda e, ex=ex, k=k: e.dma_start(
                out=stage[:, 256:512],
                in_=up_dram[ex * 1024 + k * 128: ex * 1024 + (k + 1) * 128, :]),
                writes=[stb], dma=True, tag=f"{tag}_uld{ei}_{k}")
            P.op("vector", lambda e, ei=ei, k=k: e.tensor_copy(
                uw[:, (ei * DCH + k) * 256:(ei * DCH + k + 1) * 256],
                stage[:, 256:512]),
                reads=[stb], writes=[b["uw"]], tag=f"{tag}_uc{ei}_{k}")
        for k2 in range(2):
            P.op("sync", lambda e, ex=ex, k2=k2: e.dma_start(
                out=stage[:, 1024:2048],
                in_=down_dram[ex * 256 + k2 * 128: ex * 256 + (k2 + 1) * 128, :]),
                writes=[stb], dma=True, tag=f"{tag}_dld{ei}_{k2}")
            P.op("vector", lambda e, ei=ei, k2=k2: e.tensor_copy(
                dw[:, (ei * 2 + k2) * 1024:(ei * 2 + k2 + 1) * 1024],
                stage[:, 1024:2048]),
                reads=[stb], writes=[b["dw"]], tag=f"{tag}_dc{ei}_{k2}")

    for n in range(NT):
        P.op("sync", lambda e, n=n: e.dma_start(
            out=xn_t.ap().rearrange("p (c t) -> p c t", c=DCH),
            in_=xn_dram.rearrange("(c p) t -> p c t", p=128)[:, :, n * 512:(n + 1) * 512]),
            writes=[b["xn"]], dma=True, tag=f"{tag}_xnld{n}")
        P.op("sync", lambda e, n=n: e.dma_start(
            out=cem_t[:], in_=cmb_dram[:, n * 512:(n + 1) * 512]),
            writes=[b["cem"]], dma=True, tag=f"{tag}_cmld{n}")
        for ei, ex in enumerate(experts):
            P.op("tensor", lambda e, ex=ex: e.matmul(
                p_cb[:], sel16[:, ex * 128:(ex + 1) * 128], cem_t[:],
                start=True, stop=True),
                reads=[b["cem"], ctx.cbuf], writes=[b["pcb"]],
                tag=f"{tag}_cbm{n}_{ei}")
            P.op("scalar", lambda e: e.copy(cbc[:], p_cb[:]),
                 reads=[b["pcb"]], writes=[b["cbc"]], tag=f"{tag}_cbe{n}_{ei}")
            for m in range(2):
                bkg, bku = m, 2 + m
                for k in range(DCH):
                    P.op("tensor", lambda e, ei=ei, m=m, k=k, bkg=bkg: e.matmul(
                        p_gu[bkg][:],
                        gw[:, (ei * DCH + k) * 256 + m * 128: (ei * DCH + k) * 256 + (m + 1) * 128],
                        xn_t[:, k * 512:(k + 1) * 512],
                        start=(k == 0), stop=(k == DCH - 1)),
                        reads=[b["gw"], b["xn"]], writes=[bgu[bkg]],
                        tag=f"{tag}_gmm{n}_{ei}_{m}_{k}")
                    P.op("tensor", lambda e, ei=ei, m=m, k=k, bku=bku: e.matmul(
                        p_gu[bku][:],
                        uw[:, (ei * DCH + k) * 256 + m * 128: (ei * DCH + k) * 256 + (m + 1) * 128],
                        xn_t[:, k * 512:(k + 1) * 512],
                        start=(k == 0), stop=(k == DCH - 1)),
                        reads=[b["uw"], b["xn"]], writes=[bgu[bku]],
                        tag=f"{tag}_umm{n}_{ei}_{m}_{k}")
                P.op("scalar", lambda e, bkg=bkg: e.activation(
                    out=sg[:], in_=p_gu[bkg][:], func=AF.Silu),
                    reads=[bgu[bkg]], writes=[b["sg"]],
                    tag=f"{tag}_silu{n}_{ei}_{m}")
                P.op("vector", lambda e, ei=ei, m=m, bku=bku: e.scalar_tensor_tensor(
                    h_all[:, (ei * 2 + m) * 512:(ei * 2 + m + 1) * 512],
                    p_gu[bku][:], 1.0, sg[:], ALU.mult, ALU.mult),
                    reads=[bgu[bku], b["sg"]], writes=[b["h"]],
                    tag=f"{tag}_hmul{n}_{ei}_{m}")
                P.op("vector", lambda e, ei=ei, m=m: e.tensor_mul(
                    h_all[:, (ei * 2 + m) * 512:(ei * 2 + m + 1) * 512],
                    h_all[:, (ei * 2 + m) * 512:(ei * 2 + m + 1) * 512], cbc[:]),
                    reads=[b["h"], b["cbc"]], writes=[b["h"]],
                    tag=f"{tag}_hcb{n}_{ei}_{m}")
        for mo in range(DCH):
            bk = mo % 3
            nacc = ne * 2
            jj = 0
            for ei in range(ne):
                for k2 in range(2):
                    P.op("tensor", lambda e, ei=ei, k2=k2, mo=mo, bk=bk, jj=jj: e.matmul(
                        p_d[bk][:],
                        dw[:, (ei * 2 + k2) * 1024 + mo * 128: (ei * 2 + k2) * 1024 + (mo + 1) * 128],
                        h_all[:, (ei * 2 + k2) * 512:(ei * 2 + k2 + 1) * 512],
                        start=(jj == 0), stop=(jj == nacc - 1)),
                        reads=[b["dw"], b["h"]], writes=[bd[bk]],
                        tag=f"{tag}_dmm{n}_{mo}_{ei}_{k2}")
                    jj += 1
            P.op("scalar", lambda e, mo=mo, bk=bk: e.copy(
                rout[:, (mo % 2) * 512:(mo % 2 + 1) * 512], p_d[bk][:]),
                reads=[bd[bk]], writes=[b["ro"]], tag=f"{tag}_dev{n}_{mo}")
            if mo % 2 == 1:
                q_ = "gpsimd" if accumulate else "sync"
                P.op(q_, lambda e, n=n, mo=mo: e.dma_start(
                    out=out_dram.rearrange("(c p) t -> p c t", p=128)[
                        :, mo - 1:mo + 1, n * 512:(n + 1) * 512],
                    in_=rout.ap().rearrange("p (c t) -> p c t", c=2),
                    **(dict(accum_op=ALU.add) if accumulate else {})),
                    reads=[b["ro"]], dma=True, tag=f"{tag}_ost{n}_{mo}")



def run_linear(x_cores, w, b, mode, din, dout, act=None, out_dtype=BF16,
               b_row=None, x2_cores=None, res_cores=None, aug=False,
               cache_key="lin"):
    names = ["ones_row", "ones_col"]
    if mode == "fm":
        oshape = (dout, T)
    else:
        nh = dout // 64
        oshape = (T, nh * 65) if aug else (T, dout)
    specs = {
        "x": ((din, T), BF16, "ExternalInput"),
        "w": ((din, dout), F32, "ExternalInput"),
        "out": (oshape, out_dtype, "ExternalOutput"),
        **const_specs(names),
    }
    if b is not None:
        specs["b"] = ((128, dout // 128), F32, "ExternalInput")
    if b_row is not None:
        specs["brow"] = ((1, dout), F32, "ExternalInput")
    if x2_cores is not None:
        specs["x2"] = ((din, T), BF16, "ExternalInput")
    if res_cores:
        for ri in range(len(res_cores)):
            specs[f"res{ri}"] = ((D, T), F32, "ExternalInput")

    def ph(ctx, dr):
        emit_linear(ctx, dr["x"], dr["w"], dr.get("b"), mode, dr["out"],
                    din, dout, act=act, out_dtype=out_dtype,
                    b_row_dram=dr.get("brow"), x2_dram=dr.get("x2"),
                    res_drams=[dr[f"res{ri}"] for ri in range(len(res_cores))]
                    if res_cores else (), aug=aug, tag="li")

    base = dict(w=np.ascontiguousarray(w), **const_inmaps(names))
    if b is not None:
        base["b"] = _tile_bias(b)
    if b_row is not None:
        base["brow"] = np.ascontiguousarray(b_row.reshape(1, dout))
    im = []
    for ci in range(N_CORES):
        m = dict(x=x_cores[ci], **base)
        if x2_cores is not None:
            m["x2"] = x2_cores[ci]
        if res_cores:
            for ri in range(len(res_cores)):
                m[f"res{ri}"] = res_cores[ri][ci]
        im.append(m)
    res = _run(cache_key, specs, names, [ph], im)
    return [r["out"] for r in res]


def run_ffn0_route(xn_cores, router_w, correction_bias):
    names = ["ones_row", "ident"]
    specs = {
        "xn": ((D, T), BF16, "ExternalInput"),
        "rwT": ((D, 16), F32, "ExternalInput"),
        "cb": ((1, 16), F32, "ExternalInput"),
        "cmb": ((16, T), BF16, "ExternalOutput"),
        **const_specs(names),
    }

    def ph(ctx, dr):
        emit_ffn0_route(ctx, dr["xn"], dr["rwT"], dr["cb"], dr["cmb"], tag="rt")

    base = dict(rwT=_t(router_w), cb=correction_bias.reshape(1, 16),
                **const_inmaps(names))
    im = [dict(xn=xn, **base) for xn in xn_cores]
    res = _run("ffn0route", specs, names, [ph], im)
    return [r["cmb"] for r in res]


def run_ffn0_experts(xn_cores, cmb_cores, gate_w, up_w, down_w, experts,
                     cache_key):
    names = ["sel16"]
    specs = {
        "xn": ((D, T), BF16, "ExternalInput"),
        "cmb": ((16, T), BF16, "ExternalInput"),
        "gate": ((16 * 1024, 256), F32, "ExternalInput"),
        "up": ((16 * 1024, 256), F32, "ExternalInput"),
        "down": ((16 * 256, 1024), F32, "ExternalInput"),
        "out": ((D, T), F32, "ExternalOutput"),
        **const_specs(names),
    }

    def ph(ctx, dr):
        emit_ffn0_experts(ctx, dr["xn"], dr["cmb"], dr["gate"], dr["up"],
                          dr["down"], experts, dr["out"], False, tag="ex")

    base = dict(gate=gate_w.reshape(16 * 1024, 256),
                up=up_w.reshape(16 * 1024, 256),
                down=down_w.reshape(16 * 256, 1024), **const_inmaps(names))
    im = [dict(xn=xn, cmb=cm, **base) for xn, cm in zip(xn_cores, cmb_cores)]
    res = _run(cache_key, specs, names, [ph], im)
    return [r["out"] for r in res]


def run_add(in_cores_list, cache_key, out_dtype=F32):
    names = []
    specs = {"out": ((D, T), out_dtype, "ExternalOutput")}
    ni = len(in_cores_list)
    for ii in range(ni):
        specs[f"in{ii}"] = ((D, T), F32, "ExternalInput")

    def ph(ctx, dr):
        emit_add(ctx, [dr[f"in{ii}"] for ii in range(ni)], dr["out"],
                 out_dtype=out_dtype, tag="add")

    im = []
    for ci in range(N_CORES):
        im.append({f"in{ii}": in_cores_list[ii][ci] for ii in range(ni)})
    res = _run(cache_key, specs, names, [ph], im)
    return [r["out"] for r in res]


def emit_linscan(ctx, pq_dram, pk_dram, pkt_dram, vaug_dram, y_dram,
                 tag="ls"):
    nc, P, a = ctx.nc, ctx.P, ctx.alloc
    pq_sb = [a.sb(f"{tag}_pq{j}", [64, T], BF16) for j in range(2)]
    pk_sb = [a.sb(f"{tag}_pk{j}", [64, T], BF16) for j in range(2)]
    pkt_sb = [a.sb(f"{tag}_pkt{j}", [64, 64 * 64], BF16) for j in range(2)]
    va_sb = [a.sb(f"{tag}_va{j}", [64, 64 * 65], BF16) for j in range(2)]
    at_sb = [a.sb(f"{tag}_at{j}", [64, 64], BF16) for j in range(2)]
    s32 = [a.sb(f"{tag}_s32{j}", [64, 65], F32) for j in range(2)]
    s_bf = [a.sb(f"{tag}_sbf{j}", [64, 65], BF16) for j in range(2)]
    yt_sb = [a.sb(f"{tag}_yt{j}", [64, T], F32) for j in range(2)]
    yo_sb = a.sb(f"{tag}_yo", [64, T], BF16)
    den_sb = a.sb(f"{tag}_den", [1, 2 * T], F32)
    rbc_sb = a.sb(f"{tag}_rbc", [64, 512], F32)
    p_a = [a.ps(f"{tag}_pa{j}", [64, 64], F32) for j in range(2)]
    p_k = [a.ps(f"{tag}_pkk{j}", [64, 65], F32) for j in range(2)]
    p_y = [a.ps(f"{tag}_py{j}", [65, 64], F32) for j in range(2)]
    p_bc = a.ps(f"{tag}_pbc", [64, 512], F32)
    bq = [ctx.buf(f"{tag}_pq{j}") for j in range(2)]
    bkf = [ctx.buf(f"{tag}_pk{j}") for j in range(2)]
    bkt = [ctx.buf(f"{tag}_pkt{j}") for j in range(2)]
    bva = [ctx.buf(f"{tag}_va{j}") for j in range(2)]
    bat = [ctx.buf(f"{tag}_at{j}") for j in range(2)]
    bs32 = [ctx.buf(f"{tag}_s32{j}") for j in range(2)]
    bsbf = [ctx.buf(f"{tag}_sbf{j}") for j in range(2)]
    byt = [ctx.buf(f"{tag}_yt{j}") for j in range(2)]
    byo = ctx.buf(f"{tag}_yo")
    bden = ctx.buf(f"{tag}_den")
    brbc = ctx.buf(f"{tag}_rbc")
    ba = [ctx.buf(f"{tag}_pa{j}") for j in range(2)]
    bk = [ctx.buf(f"{tag}_pkk{j}") for j in range(2)]
    by = [ctx.buf(f"{tag}_py{j}") for j in range(2)]
    bbc = ctx.buf(f"{tag}_pbc")
    mask = ctx.cb16["triu_mask"]

    for hp in range(8):
        for hh in range(2):
            h = 2 * hp + hh
            P.op("sync", lambda e, h=h, hh=hh: e.dma_start(
                out=pq_sb[hh][:], in_=pq_dram[h * 64:(h + 1) * 64, :]),
                writes=[bq[hh]], dma=True, tag=f"{tag}_pqld{h}")
            P.op("sync", lambda e, h=h, hh=hh: e.dma_start(
                out=pk_sb[hh][:], in_=pk_dram[h * 64:(h + 1) * 64, :]),
                writes=[bkf[hh]], dma=True, tag=f"{tag}_pkld{h}")
            P.op("sync", lambda e, h=h, hh=hh: e.dma_start(
                out=pkt_sb[hh].ap().rearrange("p (n d) -> p n d", d=64),
                in_=pkt_dram[:, h * 64:(h + 1) * 64].rearrange(
                    "(n p) d -> p n d", p=64)),
                writes=[bkt[hh]], dma=True, tag=f"{tag}_pktld{h}")
            P.op("sync", lambda e, h=h, hh=hh: e.dma_start(
                out=va_sb[hh].ap().rearrange("p (n d) -> p n d", d=65),
                in_=vaug_dram[:, h * 65:(h + 1) * 65].rearrange(
                    "(n p) d -> p n d", p=64)),
                writes=[bva[hh]], dma=True, tag=f"{tag}_vald{h}")
            P.op("vector", lambda e, hh=hh: e.memset(s32[hh][:], 0.0),
                 writes=[bs32[hh]], tag=f"{tag}_s0_{hp}_{hh}")
            P.op("vector", lambda e, hh=hh: e.memset(s_bf[hh][:], 0.0),
                 writes=[bsbf[hh]], tag=f"{tag}_sb0_{hp}_{hh}")
        for n in range(64):
            for hh in range(2):
                P.op("tensor", lambda e, hh=hh, n=n: e.matmul(
                    p_a[hh][:], pk_sb[hh][:, n * 64:(n + 1) * 64],
                    pq_sb[hh][:, n * 64:(n + 1) * 64],
                    start=True, stop=True),
                    reads=[bkf[hh], bq[hh]], writes=[ba[hh]],
                    tag=f"{tag}_amm{hp}_{n}_{hh}")
                P.op("vector", lambda e, hh=hh: e.scalar_tensor_tensor(
                    at_sb[hh][:], p_a[hh][:], 1.0, mask, ALU.mult, ALU.mult),
                    reads=[ba[hh], ctx.cbuf], writes=[bat[hh]],
                    tag=f"{tag}_amsk{hp}_{n}_{hh}")
                P.op("tensor", lambda e, hh=hh, n=n: e.matmul(
                    p_y[hh][:], s_bf[hh][:], pq_sb[hh][:, n * 64:(n + 1) * 64],
                    start=True, stop=False),
                    reads=[bsbf[hh], bq[hh]], writes=[by[hh]],
                    tag=f"{tag}_imm{hp}_{n}_{hh}")
                P.op("tensor", lambda e, hh=hh, n=n: e.matmul(
                    p_y[hh][:], va_sb[hh][:, n * 65:(n + 1) * 65],
                    at_sb[hh][:], start=False, stop=True),
                    reads=[bva[hh], bat[hh]], writes=[by[hh]],
                    tag=f"{tag}_tmm{hp}_{n}_{hh}")
                P.op("tensor", lambda e, hh=hh, n=n: e.matmul(
                    p_k[hh][:], pkt_sb[hh][:, n * 64:(n + 1) * 64],
                    va_sb[hh][:, n * 65:(n + 1) * 65],
                    start=True, stop=True),
                    reads=[bkt[hh], bva[hh]], writes=[bk[hh]],
                    tag=f"{tag}_kmm{hp}_{n}_{hh}")
                P.op("vector", lambda e, hh=hh: e.tensor_add(
                    s32[hh][:], s32[hh][:], p_k[hh][:]),
                    reads=[bs32[hh], bk[hh]], writes=[bs32[hh]],
                    tag=f"{tag}_sup{hp}_{n}_{hh}")
                P.op("vector", lambda e, hh=hh: e.tensor_copy(
                    s_bf[hh][:], s32[hh][:]),
                    reads=[bs32[hh]], writes=[bsbf[hh]],
                    tag=f"{tag}_scp{hp}_{n}_{hh}")
                P.op("scalar", lambda e, hh=hh, n=n: e.copy(
                    yt_sb[hh][:, n * 64:(n + 1) * 64], p_y[hh].ap()[0:64, :]),
                    reads=[by[hh]], writes=[byt[hh]],
                    tag=f"{tag}_yev{hp}_{n}_{hh}")
                P.op("scalar", lambda e, hh=hh, n=n: e.copy(
                    den_sb[0:1, hh * T + n * 64: hh * T + (n + 1) * 64],
                    p_y[hh].ap()[64:65, :]),
                    reads=[by[hh]], writes=[bden],
                    tag=f"{tag}_dev{hp}_{n}_{hh}")
        P.op("vector", lambda e: e.tensor_scalar_max(den_sb[:], den_sb[:], 1e-6),
             reads=[bden], writes=[bden], tag=f"{tag}_dcl{hp}")
        P.op("vector", lambda e: e.reciprocal(den_sb[:], den_sb[:]),
             reads=[bden], writes=[bden], tag=f"{tag}_drc{hp}")
        for hh in range(2):
            h = 2 * hp + hh
            for tt in range(8):
                P.op("tensor", lambda e, hh=hh, tt=tt: e.matmul(
                    p_bc[:], ctx.cf32["ones_row"][0:1, 0:64],
                    den_sb[0:1, hh * T + tt * 512: hh * T + (tt + 1) * 512],
                    start=True, stop=True),
                    reads=[bden, ctx.cbuf], writes=[bbc],
                    tag=f"{tag}_nbc{hp}_{hh}_{tt}")
                P.op("scalar", lambda e: e.copy(rbc_sb[:], p_bc[:]),
                     reads=[bbc], writes=[brbc], tag=f"{tag}_nbe{hp}_{hh}_{tt}")
                P.op("vector", lambda e, hh=hh, tt=tt: e.tensor_mul(
                    yo_sb[:, tt * 512:(tt + 1) * 512],
                    yt_sb[hh][:, tt * 512:(tt + 1) * 512], rbc_sb[:]),
                    reads=[byt[hh], brbc], writes=[byo],
                    tag=f"{tag}_nml{hp}_{hh}_{tt}")
            P.op("sync", lambda e, h=h: e.dma_start(
                out=y_dram[h * 64:(h + 1) * 64, :], in_=yo_sb[:]),
                reads=[byo], dma=True, tag=f"{tag}_yst{h}")


CAPACITY1 = 10240.0


def emit_moe1_route(ctx, s2_dram, gwT_dram, cmb_dram, rank_dram, counts_dram,
                    tag="m1r"):
    nc, P, a = ctx.nc, ctx.P, ctx.alloc
    gw_sb = a.sb(f"{tag}_gw", [128, DCH * 8], BF16)
    gw32 = a.sb(f"{tag}_gw32", [128, DCH * 8], F32)
    xn_t = a.sb(f"{tag}_xn", [128, DCH * 512], BF16)
    pr = a.sb(f"{tag}_pr", [128, 8], F32)
    top = a.sb(f"{tag}_top", [128, 8], F32)
    sel = a.sb(f"{tag}_sel", [128, 8], F32)
    cnt = a.sb(f"{tag}_cnt", [128, 8], BF16)
    den = a.sb(f"{tag}_den", [128, 1], F32)
    cmb_t = a.sb(f"{tag}_cmb", [128, 4 * 8], F32)
    rank_t = a.sb(f"{tag}_rank", [128, 4 * 8], F32)
    offs = a.sb(f"{tag}_offs", [1, 8], F32)
    offb = a.sb(f"{tag}_offb", [128, 8], F32)
    tot = a.sb(f"{tag}_tot", [1, 8], F32)
    p_lg = a.ps(f"{tag}_plg", [128, 8], F32)
    p_pf = a.ps(f"{tag}_ppf", [128, 8], F32)
    p_t = a.ps(f"{tag}_pt", [1, 8], F32)
    p_ob = a.ps(f"{tag}_pob", [128, 8], F32)
    b = {n: ctx.buf(f"{tag}_{n}") for n in
         ("gw", "gw32", "xn", "pr", "top", "sel", "cnt", "den", "cmb",
          "rank", "offs", "offb", "tot", "plg", "ppf", "pt", "pob")}
    striu = ctx.cb16["striu"]
    ones_col = ctx.cb16["ones_col"]

    P.op("sync", lambda e: e.dma_start(
        out=gw32.ap().rearrange("p (c o) -> p c o", c=DCH),
        in_=gwT_dram.rearrange("(c p) o -> p c o", p=128)),
        writes=[b["gw32"]], dma=True, tag=f"{tag}_gld")
    P.op("vector", lambda e: e.tensor_copy(gw_sb[:], gw32[:]),
         reads=[b["gw32"]], writes=[b["gw"]], tag=f"{tag}_gc")
    P.op("vector", lambda e: e.memset(offs[:], 0.0),
         writes=[b["offs"]], tag=f"{tag}_o0")

    for n in range(NT):
        P.op("sync", lambda e, n=n: e.dma_start(
            out=xn_t.ap().rearrange("p (c t) -> p c t", c=DCH),
            in_=s2_dram.rearrange("(c p) t -> p c t", p=128)[:, :, n * 512:(n + 1) * 512]),
            writes=[b["xn"]], dma=True, tag=f"{tag}_xld{n}")
        for ms in range(4):
            for k in range(DCH):
                P.op("tensor", lambda e, ms=ms, k=k: e.matmul(
                    p_lg[:], xn_t[:, k * 512 + ms * 128: k * 512 + (ms + 1) * 128],
                    gw_sb[:, k * 8:(k + 1) * 8],
                    start=(k == 0), stop=(k == DCH - 1)),
                    reads=[b["xn"], b["gw"]], writes=[b["plg"]],
                    tag=f"{tag}_lmm{n}_{ms}_{k}")
            P.op("scalar", lambda e: e.activation(
                out=pr[:], in_=p_lg[:], func=AF.Sigmoid),
                reads=[b["plg"]], writes=[b["pr"]], tag=f"{tag}_sig{n}_{ms}")
            P.op("vector", lambda e: e.max(out=top[:], in_=pr[:]),
                 reads=[b["pr"]], writes=[b["top"]], tag=f"{tag}_top{n}_{ms}")
            P.op("vector", lambda e: e.tensor_single_scalar(
                sel[:], pr[:], top[:, 1:2], ALU.is_ge),
                reads=[b["pr"], b["top"]], writes=[b["sel"]],
                tag=f"{tag}_sel{n}_{ms}")
            P.op("vector", lambda e: e.tensor_add(den[:], top[:, 0:1], top[:, 1:2]),
                 reads=[b["top"]], writes=[b["den"]], tag=f"{tag}_dn{n}_{ms}")
            P.op("vector", lambda e: e.tensor_scalar_add(den[:], den[:], 1e-6),
                 reads=[b["den"]], writes=[b["den"]], tag=f"{tag}_dne{n}_{ms}")
            P.op("vector", lambda e: e.reciprocal(den[:], den[:]),
                 reads=[b["den"]], writes=[b["den"]], tag=f"{tag}_dnr{n}_{ms}")
            P.op("vector", lambda e, ms=ms: e.tensor_mul(
                cmb_t[:, ms * 8:(ms + 1) * 8], sel[:], pr[:]),
                reads=[b["sel"], b["pr"]], writes=[b["cmb"]],
                tag=f"{tag}_cw{n}_{ms}")
            P.op("vector", lambda e, ms=ms: e.tensor_scalar_mul(
                cmb_t[:, ms * 8:(ms + 1) * 8], cmb_t[:, ms * 8:(ms + 1) * 8],
                den[:]),
                reads=[b["cmb"], b["den"]], writes=[b["cmb"]],
                tag=f"{tag}_cs{n}_{ms}")
            P.op("vector", lambda e: e.tensor_copy(cnt[:], sel[:]),
                 reads=[b["sel"]], writes=[b["cnt"]], tag=f"{tag}_cb{n}_{ms}")
            P.op("tensor", lambda e: e.matmul(
                p_pf[:], striu, cnt[:], start=True, stop=True),
                reads=[b["cnt"], ctx.cbuf], writes=[b["ppf"]],
                tag=f"{tag}_pf{n}_{ms}")
            P.op("tensor", lambda e: e.matmul(
                p_t[:], ones_col, cnt[:], start=True, stop=True),
                reads=[b["cnt"], ctx.cbuf], writes=[b["pt"]],
                tag=f"{tag}_tt{n}_{ms}")
            P.op("tensor", lambda e: e.matmul(
                p_ob[:], ctx.cf32["ones_row"], offs[:], start=True, stop=True),
                reads=[b["offs"], ctx.cbuf], writes=[b["pob"]],
                tag=f"{tag}_ob{n}_{ms}")
            P.op("vector", lambda e, ms=ms: e.tensor_add(
                rank_t[:, ms * 8:(ms + 1) * 8], p_pf[:], offb[:]) if False
                else e.tensor_copy(offb[:], p_ob[:]),
                reads=[b["pob"]], writes=[b["offb"]], tag=f"{tag}_obe{n}_{ms}")
            P.op("vector", lambda e, ms=ms: e.tensor_add(
                rank_t[:, ms * 8:(ms + 1) * 8], p_pf[:], offb[:]),
                reads=[b["ppf"], b["offb"]], writes=[b["rank"]],
                tag=f"{tag}_rk{n}_{ms}")
            P.op("vector", lambda e: e.tensor_copy(tot[:], p_t[:]),
                 reads=[b["pt"]], writes=[b["tot"]], tag=f"{tag}_te{n}_{ms}")
            P.op("vector", lambda e: e.tensor_add(offs[:], offs[:], tot[:]),
                 reads=[b["offs"], b["tot"]], writes=[b["offs"]],
                 tag=f"{tag}_oa{n}_{ms}")
        P.op("sync", lambda e, n=n: e.dma_start(
            out=cmb_dram[n * 512:(n + 1) * 512, :].rearrange(
                "(ms p) e -> p ms e", p=128),
            in_=cmb_t.ap().rearrange("p (ms e) -> p ms e", e=8)),
            reads=[b["cmb"]], dma=True, tag=f"{tag}_cst{n}")
        P.op("sync", lambda e, n=n: e.dma_start(
            out=rank_dram[n * 512:(n + 1) * 512, :].rearrange(
                "(ms p) e -> p ms e", p=128),
            in_=rank_t.ap().rearrange("p (ms e) -> p ms e", e=8)),
            reads=[b["rank"]], dma=True, tag=f"{tag}_rst{n}")
    P.op("sync", lambda e: e.dma_start(out=counts_dram, in_=offs[:]),
         reads=[b["offs"]], dma=True, tag=f"{tag}_tst")


def emit_moe1_gate(ctx, cmb_dram, rank_dram, counts_all_dram, coremask_dram,
                   gct_dram, tag="m1g"):
    nc, P, a = ctx.nc, ctx.P, ctx.alloc
    ca = a.sb(f"{tag}_ca", [8, 8], F32)
    cm = a.sb(f"{tag}_cm", [8, 1], F32)
    offb = a.sb(f"{tag}_offb", [128, 8], F32)
    cmb_t = a.sb(f"{tag}_cmb", [128, 4 * 8], F32)
    rank_t = a.sb(f"{tag}_rank", [128, 4 * 8], F32)
    g_t = a.sb(f"{tag}_g", [128, 8], BF16)
    gct = a.sb(f"{tag}_gct", [8, 512], BF16)
    p_o = a.ps(f"{tag}_po", [128, 8], F32)
    p_tr = a.ps(f"{tag}_ptr", [8, 128], BF16)
    b = {n: ctx.buf(f"{tag}_{n}") for n in
         ("ca", "cm", "offb", "cmb", "rank", "g", "gct", "po", "ptr")}
    identb = ctx.cb16["ident"]

    P.op("sync", lambda e: e.dma_start(out=ca[:], in_=counts_all_dram),
         writes=[b["ca"]], dma=True, tag=f"{tag}_cald")
    P.op("sync", lambda e: e.dma_start(out=cm[:], in_=coremask_dram),
         writes=[b["cm"]], dma=True, tag=f"{tag}_cmld")
    P.op("tensor", lambda e: e.matmul(
        p_o[0:1, :], cm[:], ca[:], start=True, stop=True),
        reads=[b["ca"], b["cm"]], writes=[b["po"]], tag=f"{tag}_offmm")
    P.op("scalar", lambda e: e.copy(offb[0:1, :], p_o[0:1, :]),
         reads=[b["po"]], writes=[b["offb"]], tag=f"{tag}_offe")
    P.op("tensor", lambda e: e.matmul(
        p_o[:], ctx.cf32["ones_row"], offb[0:1, :], start=True, stop=True),
        reads=[b["offb"], ctx.cbuf], writes=[b["po"]], tag=f"{tag}_obc")
    P.op("scalar", lambda e: e.copy(offb[:], p_o[:]),
         reads=[b["po"]], writes=[b["offb"]], tag=f"{tag}_obce")

    for n in range(NT):
        P.op("sync", lambda e, n=n: e.dma_start(
            out=cmb_t.ap().rearrange("p (ms e) -> p ms e", e=8),
            in_=cmb_dram[n * 512:(n + 1) * 512, :].rearrange(
                "(ms p) e -> p ms e", p=128)),
            writes=[b["cmb"]], dma=True, tag=f"{tag}_cld{n}")
        P.op("sync", lambda e, n=n: e.dma_start(
            out=rank_t.ap().rearrange("p (ms e) -> p ms e", e=8),
            in_=rank_dram[n * 512:(n + 1) * 512, :].rearrange(
                "(ms p) e -> p ms e", p=128)),
            writes=[b["rank"]], dma=True, tag=f"{tag}_rld{n}")
        for ms in range(4):
            P.op("vector", lambda e, ms=ms: e.tensor_add(
                rank_t[:, ms * 8:(ms + 1) * 8], rank_t[:, ms * 8:(ms + 1) * 8],
                offb[:]),
                reads=[b["rank"], b["offb"]], writes=[b["rank"]],
                tag=f"{tag}_ga{n}_{ms}")
            P.op("vector", lambda e, ms=ms: e.tensor_single_scalar(
                rank_t[:, ms * 8:(ms + 1) * 8], rank_t[:, ms * 8:(ms + 1) * 8],
                CAPACITY1, ALU.is_lt),
                reads=[b["rank"]], writes=[b["rank"]], tag=f"{tag}_gv{n}_{ms}")
            P.op("vector", lambda e, ms=ms: e.tensor_mul(
                g_t[:], rank_t[:, ms * 8:(ms + 1) * 8],
                cmb_t[:, ms * 8:(ms + 1) * 8]),
                reads=[b["rank"], b["cmb"]], writes=[b["g"]],
                tag=f"{tag}_gm{n}_{ms}")
            P.op("tensor", lambda e: e.transpose(p_tr[:], g_t[:], identb),
                 reads=[b["g"], ctx.cbuf], writes=[b["ptr"]],
                 tag=f"{tag}_tr{n}_{ms}")
            P.op("scalar", lambda e, ms=ms: e.copy(
                gct[:, ms * 128:(ms + 1) * 128], p_tr[:]),
                reads=[b["ptr"]], writes=[b["gct"]], tag=f"{tag}_te{n}_{ms}")
        P.op("sync", lambda e, n=n: e.dma_start(
            out=gct_dram[:, n * 512:(n + 1) * 512], in_=gct[:]),
            reads=[b["gct"]], dma=True, tag=f"{tag}_st{n}")


def emit_moe1_experts(ctx, s2_dram, gct_dram, w1_dram, w2_dram, experts,
                      out_dram, accumulate, tag="m1e"):
    nc, P, a = ctx.nc, ctx.P, ctx.alloc
    ne = len(experts)
    HID = 2048
    HCH = HID // 128
    w1 = a.sb(f"{tag}_w1", [128, ne * DCH * HID], BF16)
    w2 = a.sb(f"{tag}_w2", [128, ne * HCH * D], BF16)
    stage = [a.sb(f"{tag}_st{i}", [128, 2048], F32) for i in range(1)]
    stb = [ctx.buf(f"{tag}_st{i}") for i in range(1)]
    xn_t = a.sb(f"{tag}_xn", [128, DCH * 512], BF16)
    gct_t = a.sb(f"{tag}_gct", [8, 512], BF16)
    cbc = a.sb(f"{tag}_cbc", [128, 512], BF16)
    h_sb = a.sb(f"{tag}_h", [128, ne * HCH * 512], BF16)
    rout = a.sb(f"{tag}_ro", [128, 2 * 512], F32)
    p_g = [a.ps(f"{tag}_pg{i}", [128, 512], F32) for i in range(4)]
    p_d = [a.ps(f"{tag}_pd{i}", [128, 512], F32) for i in range(3)]
    p_cb = a.ps(f"{tag}_pcb", [128, 512], F32)
    b = {n: ctx.buf(f"{tag}_{n}") for n in
         ("w1", "w2", "xn", "gct", "cbc", "h", "ro", "pcb")}
    bg = [ctx.buf(f"{tag}_pg{i}") for i in range(4)]
    bd = [ctx.buf(f"{tag}_pd{i}") for i in range(3)]
    sel8 = ctx.cb16["sel8"]

    j = 0
    for ei, ex in enumerate(experts):
        for k in range(DCH):
            st, sb_ = stage[0], stb[0]
            P.op("sync", lambda e, ex=ex, k=k, st=st: e.dma_start(
                out=st[:], in_=w1_dram[ex * 1024 + k * 128: ex * 1024 + (k + 1) * 128, :]),
                writes=[sb_], dma=True, tag=f"{tag}_w1l{ei}_{k}")
            P.op("vector", lambda e, ei=ei, k=k, st=st: e.tensor_copy(
                w1[:, (ei * DCH + k) * HID:(ei * DCH + k + 1) * HID], st[:]),
                reads=[sb_], writes=[b["w1"]], tag=f"{tag}_w1c{ei}_{k}")
        for k2 in range(HCH):
            st, sb_ = stage[0], stb[0]
            P.op("sync", lambda e, ex=ex, k2=k2, st=st: e.dma_start(
                out=st[:, 0:1024],
                in_=w2_dram[ex * 2048 + k2 * 128: ex * 2048 + (k2 + 1) * 128, :]),
                writes=[sb_], dma=True, tag=f"{tag}_w2l{ei}_{k2}")
            P.op("vector", lambda e, ei=ei, k2=k2, st=st: e.tensor_copy(
                w2[:, (ei * HCH + k2) * D:(ei * HCH + k2 + 1) * D], st[:, 0:1024]),
                reads=[sb_], writes=[b["w2"]], tag=f"{tag}_w2c{ei}_{k2}")

    for n in range(NT):
        P.op("sync", lambda e, n=n: e.dma_start(
            out=xn_t.ap().rearrange("p (c t) -> p c t", c=DCH),
            in_=s2_dram.rearrange("(c p) t -> p c t", p=128)[:, :, n * 512:(n + 1) * 512]),
            writes=[b["xn"]], dma=True, tag=f"{tag}_xld{n}")
        P.op("sync", lambda e, n=n: e.dma_start(
            out=gct_t[:], in_=gct_dram[:, n * 512:(n + 1) * 512]),
            writes=[b["gct"]], dma=True, tag=f"{tag}_gld{n}")
        for ei, ex in enumerate(experts):
            P.op("tensor", lambda e, ex=ex: e.matmul(
                p_cb[:], sel8[:, ex * 128:(ex + 1) * 128], gct_t[:],
                start=True, stop=True),
                reads=[b["gct"], ctx.cbuf], writes=[b["pcb"]],
                tag=f"{tag}_cbm{n}_{ei}")
            P.op("scalar", lambda e: e.copy(cbc[:], p_cb[:]),
                 reads=[b["pcb"]], writes=[b["cbc"]], tag=f"{tag}_cbe{n}_{ei}")
            for m in range(HCH):
                bk = m % 4
                for k in range(DCH):
                    P.op("tensor", lambda e, ei=ei, m=m, k=k, bk=bk: e.matmul(
                        p_g[bk][:],
                        w1[:, (ei * DCH + k) * HID + m * 128: (ei * DCH + k) * HID + (m + 1) * 128],
                        xn_t[:, k * 512:(k + 1) * 512],
                        start=(k == 0), stop=(k == DCH - 1)),
                        reads=[b["w1"], b["xn"]], writes=[bg[bk]],
                        tag=f"{tag}_1mm{n}_{ei}_{m}_{k}")
                P.op("scalar", lambda e, ei=ei, m=m, bk=bk: e.activation(
                    out=h_sb[:, (ei * HCH + m) * 512:(ei * HCH + m + 1) * 512],
                    in_=p_g[bk][:], func=AF.Gelu),
                    reads=[bg[bk]], writes=[b["h"]], tag=f"{tag}_gel{n}_{ei}_{m}")
                P.op("vector", lambda e, ei=ei, m=m: e.tensor_mul(
                    h_sb[:, (ei * HCH + m) * 512:(ei * HCH + m + 1) * 512],
                    h_sb[:, (ei * HCH + m) * 512:(ei * HCH + m + 1) * 512],
                    cbc[:]),
                    reads=[b["h"], b["cbc"]], writes=[b["h"]],
                    tag=f"{tag}_hcb{n}_{ei}_{m}")
        for mo in range(DCH):
            bk = mo % 3
            nacc = ne * HCH
            jj = 0
            for ei in range(ne):
                for k2 in range(HCH):
                    P.op("tensor", lambda e, ei=ei, k2=k2, mo=mo, bk=bk, jj=jj: e.matmul(
                        p_d[bk][:],
                        w2[:, (ei * HCH + k2) * D + mo * 128: (ei * HCH + k2) * D + (mo + 1) * 128],
                        h_sb[:, (ei * HCH + k2) * 512:(ei * HCH + k2 + 1) * 512],
                        start=(jj == 0), stop=(jj == nacc - 1)),
                        reads=[b["w2"], b["h"]], writes=[bd[bk]],
                        tag=f"{tag}_2mm{n}_{mo}_{ei}_{k2}")
                    jj += 1
            P.op("scalar", lambda e, mo=mo, bk=bk: e.copy(
                rout[:, (mo % 2) * 512:(mo % 2 + 1) * 512], p_d[bk][:]),
                reads=[bd[bk]], writes=[b["ro"]], tag=f"{tag}_dev{n}_{mo}")
            if mo % 2 == 1:
                q_ = "gpsimd" if accumulate else "sync"
                P.op(q_, lambda e, n=n, mo=mo: e.dma_start(
                    out=out_dram.rearrange("(c p) t -> p c t", p=128)[
                        :, mo - 1:mo + 1, n * 512:(n + 1) * 512],
                    in_=rout.ap().rearrange("p (c t) -> p c t", c=2),
                    **(dict(accum_op=ALU.add) if accumulate else {})),
                    reads=[b["ro"]], dma=True, tag=f"{tag}_ost{n}_{mo}")


def run_ln(x_cores, outs_spec, kind="ln", w=None, eps=1e-5, cache_key="ln"):
    names = ["ones_row", "ones_col"]
    specs = {"x_fm": ((D, T), F32, "ExternalInput"), **const_specs(names)}
    for nm, dt in outs_spec:
        specs[nm] = ((D, T), dt, "ExternalOutput")
    if w is not None:
        specs["lnw"] = ((128, DCH), F32, "ExternalInput")

    def ph(ctx, dr):
        emit_ln(ctx, dr["x_fm"], [(dr[nm], dt) for nm, dt in outs_spec],
                kind=kind, w_tiled_dram=dr.get("lnw"), eps=eps, tag="ln")

    base = dict(**const_inmaps(names))
    if w is not None:
        base["lnw"] = _tile_bias(w)
    im = [dict(x_fm=x, **base) for x in x_cores]
    res = _run(cache_key, specs, names, [ph], im)
    return {nm: [r[nm] for r in res] for nm, _ in outs_spec}


def run_linscan(pq, pk, pkt, vaug):
    names = ["ones_row", "triu_mask"]
    specs = {
        "pq": ((D, T), BF16, "ExternalInput"),
        "pk": ((D, T), BF16, "ExternalInput"),
        "pkt": ((T, D), BF16, "ExternalInput"),
        "vaug": ((T, 16 * 65), BF16, "ExternalInput"),
        "y": ((D, T), BF16, "ExternalOutput"),
        **const_specs(names),
    }

    def ph(ctx, dr):
        emit_linscan(ctx, dr["pq"], dr["pk"], dr["pkt"], dr["vaug"], dr["y"],
                     tag="ls")

    base = const_inmaps(names)
    im = [dict(pq=pq[c], pk=pk[c], pkt=pkt[c], vaug=vaug[c], **base)
          for c in range(N_CORES)]
    res = _run("linscan", specs, names, [ph], im)
    return [r["y"] for r in res]


def run_moe1_route(s2_cores, gate_w):
    names = ["ones_row", "ones_col", "striu"]
    specs = {
        "s2": ((D, T), BF16, "ExternalInput"),
        "gwT": ((D, 8), F32, "ExternalInput"),
        "cmb": ((T, 8), F32, "ExternalOutput"),
        "rank": ((T, 8), F32, "ExternalOutput"),
        "counts": ((1, 8), F32, "ExternalOutput"),
        **const_specs(names),
    }

    def ph(ctx, dr):
        emit_moe1_route(ctx, dr["s2"], dr["gwT"], dr["cmb"], dr["rank"],
                        dr["counts"], tag="m1r")

    base = dict(gwT=np.ascontiguousarray(gate_w), **const_inmaps(names))
    im = [dict(s2=s2, **base) for s2 in s2_cores]
    res = _run("moe1route", specs, names, [ph], im)
    return ([r["cmb"] for r in res], [r["rank"] for r in res],
            [r["counts"] for r in res])


def run_moe1_gate(cmb_cores, rank_cores, all_counts, core_masks):
    names = ["ones_row", "ident"]
    specs = {
        "cmb": ((T, 8), F32, "ExternalInput"),
        "rank": ((T, 8), F32, "ExternalInput"),
        "ca": ((8, 8), F32, "ExternalInput"),
        "cm": ((8, 1), F32, "ExternalInput"),
        "gct": ((8, T), BF16, "ExternalOutput"),
        **const_specs(names),
    }

    def ph(ctx, dr):
        emit_moe1_gate(ctx, dr["cmb"], dr["rank"], dr["ca"], dr["cm"],
                       dr["gct"], tag="m1g")

    base = const_inmaps(names)
    im = [dict(cmb=cmb_cores[c], rank=rank_cores[c], ca=all_counts,
               cm=core_masks[c], **base) for c in range(N_CORES)]
    res = _run("moe1gate", specs, names, [ph], im)
    return [r["gct"] for r in res]


def run_moe1_experts(s2_cores, gct_cores, w1, w2, experts, cache_key):
    names = ["sel8"]
    specs = {
        "s2": ((D, T), BF16, "ExternalInput"),
        "gct": ((8, T), BF16, "ExternalInput"),
        "w1": ((8 * 1024, 2048), F32, "ExternalInput"),
        "w2": ((8 * 2048, 1024), F32, "ExternalInput"),
        "out": ((D, T), F32, "ExternalOutput"),
        **const_specs(names),
    }

    def ph(ctx, dr):
        emit_moe1_experts(ctx, dr["s2"], dr["gct"], dr["w1"], dr["w2"],
                          experts, dr["out"], False, tag="m1e")

    base = dict(w1=w1.reshape(8 * 1024, 2048), w2=w2.reshape(8 * 2048, 1024),
                **const_inmaps(names))
    im = [dict(s2=s2_cores[c], gct=gct_cores[c], **base) for c in range(N_CORES)]
    res = _run(cache_key, specs, names, [ph], im)
    return [r["out"] for r in res]



def kernel(**inp):
    x = inp["L0_attn_x"]
    x_fm = [np.ascontiguousarray(x[c].T) for c in range(N_CORES)]

    xn0 = run_ln0(x_fm)
    q0 = run_mlp(xn0, inp["L0_attn_q_w1"], inp["L0_attn_q_b1"],
                 inp["L0_attn_q_w2"], inp["L0_attn_q_b2"], "fm", "q")
    k0 = run_mlp(xn0, inp["L0_attn_k_w1"], inp["L0_attn_k_b1"],
                 inp["L0_attn_k_w2"], inp["L0_attn_k_b2"], "fm", "k")
    v0 = run_mlp(xn0, inp["L0_attn_v_w1"], inp["L0_attn_v_b1"],
                 inp["L0_attn_v_w2"], inp["L0_attn_v_b2"], "tm_aug", "v")
    attn0x = run_attn(q0, k0, v0, inp["L0_attn_proj_w"], inp["L0_attn_proj_b"],
                      16, 16, False, res_cores=x_fm, cache_key="attn0")
    sg0 = run_linear(xn0, _t(inp["L0_ffn_shared_gate_w"]), None, "fm", 1024,
                     2048, act="silu", cache_key="shg")
    su0 = run_linear(xn0, _t(inp["L0_ffn_shared_up_w"]), None, "fm", 1024,
                     2048, act=None, cache_key="shu")
    ffn0 = run_linear(sg0, _t(inp["L0_ffn_shared_down_w"]), None, "fm", 2048,
                      1024, act=None, out_dtype=F32, x2_cores=su0,
                      cache_key="shd")
    cmb0 = run_ffn0_route(xn0, inp["L0_ffn_router_w"],
                          inp["L0_ffn_correction_bias"])
    rA = run_ffn0_experts(xn0, cmb0, inp["L0_ffn_gate_w"], inp["L0_ffn_up_w"],
                          inp["L0_ffn_down_w"], list(range(0, 8)), "exA")
    rB = run_ffn0_experts(xn0, cmb0, inp["L0_ffn_gate_w"], inp["L0_ffn_up_w"],
                          inp["L0_ffn_down_w"], list(range(8, 16)), "exB")
    x1 = run_add([attn0x, ffn0, rA, rB], "comb_l0")

    ln1 = run_ln(x1, [("xn1b", BF16), ("xn1f", F32)], cache_key="ln1")
    xn1b, xn1f = ln1["xn1b"], ln1["xn1f"]
    pq = run_linear(xn1b, _t(inp["L1_attn_W_q"]), None, "fm", 1024, 1024,
                    act="elu1", cache_key="l1q")
    pk = run_linear(xn1b, _t(inp["L1_attn_W_k"]), None, "fm", 1024, 1024,
                    act="elu1", cache_key="l1k")
    pkt = run_linear(xn1b, _t(inp["L1_attn_W_k"]), None, "tm", 1024, 1024,
                     act="elu1", cache_key="l1kt")
    v1 = run_linear(xn1b, _t(inp["L1_attn_W_v"]), None, "tm", 1024, 1024,
                    aug=True, cache_key="l1v")
    y1 = run_linscan(pq, pk, pkt, v1)
    attn1 = run_linear(y1, _t(inp["L1_attn_W_o"]), None, "fm", 1024, 1024,
                       act=None, out_dtype=F32, cache_key="l1o")
    qg = run_linear(xn1b, inp["L1_ffn_qw"], inp["L1_ffn_qb"], "fm", 1024,
                    1024, cache_key="gqaq")
    kg = run_linear(xn1b, inp["L1_ffn_kw"], inp["L1_ffn_kb"], "fm", 1024,
                    256, cache_key="gqak")
    vg = run_linear(xn1b, inp["L1_ffn_vw"], None, "tm", 1024, 256, aug=True,
                    b_row=inp["L1_ffn_vb"], cache_key="gqav")
    s1 = run_attn(qg, kg, vg, inp["L1_ffn_ow"], inp["L1_ffn_ob"], 16, 4, True,
                  res_cores=xn1f, cache_key="gqa_at")
    rms1 = run_ln(s1, [("s2b", BF16), ("s2f", F32)], kind="rms",
                  w=inp["L1_ffn_norm1_w"], eps=1e-6, cache_key="rms1")
    s2b, s2f = rms1["s2b"], rms1["s2f"]
    cmb1, rank1, counts = run_moe1_route(s2b, inp["L1_ffn_gate_w"])
    all_counts = np.ascontiguousarray(
        np.concatenate([counts[c] for c in range(N_CORES)], axis=0))
    core_masks = []
    for c in range(N_CORES):
        m = np.zeros((8, 1), np.float32)
        m[:c, 0] = 1.0
        core_masks.append(m)
    gct = run_moe1_gate(cmb1, rank1, all_counts, core_masks)
    mo = []
    for gi in range(4):
        mo.append(run_moe1_experts(s2b, gct, inp["L1_ffn_w1"],
                                   inp["L1_ffn_w2"],
                                   [2 * gi, 2 * gi + 1], f"m1e{gi}"))
    s3 = run_add([s2f] + mo, "comb_s3")
    rms2 = run_ln(s3, [("f1", F32)], kind="rms", w=inp["L1_ffn_norm2_w"],
                  eps=1e-6, cache_key="rms2")
    ffn1 = rms2["f1"]
    out_fm = run_add([x1, attn1, ffn1], "comb_out")
    return np.ascontiguousarray(
        np.stack([out_fm[c].T for c in range(N_CORES)]))


# version 10
# speedup vs baseline: 1.0171x; 1.0171x over previous
import math
import numpy as np
import ml_dtypes
import concourse.bass as bass
import concourse.mybir as mybir
from concourse.bass_utils import run_bass_kernel_spmd

F32 = mybir.dt.float32
BF16 = mybir.dt.bfloat16
AF = mybir.ActivationFunctionType
ALU = mybir.AluOpType
AX = mybir.AxisListType

N_CORES = 8
D = 1024
T = 4096
DCH = D // 128
NT = T // 512


ENGINES = ("sync", "scalar", "vector", "tensor", "gpsimd")


class Op:
    __slots__ = ("engine", "idx", "fn", "deps", "is_dma", "signals", "tag",
                 "sem_key")

    def __init__(self, engine, idx, fn, deps, is_dma, tag, sem_key):
        self.engine = engine
        self.idx = idx
        self.fn = fn
        self.deps = deps
        self.is_dma = is_dma
        self.signals = False
        self.tag = tag
        self.sem_key = sem_key


class Buf:
    __slots__ = ("name", "writer", "readers")

    def __init__(self, name=""):
        self.name = name
        self.writer = None
        self.readers = []


class Prog:
    def __init__(self, nc):
        self.nc = nc
        self.streams = {e: [] for e in ENGINES}
        self.order = []
        self.last_by_sem = {}
        self.pending_barrier = None
        self.barrier_fresh = set()
        self.phase = 0

    def op(self, engine, fn, reads=(), writes=(), deps=(), dma=False, tag=""):
        dl = []
        for b in reads:
            if b.writer is not None:
                dl.append(b.writer)
        for b in writes:
            if b.writer is not None:
                dl.append(b.writer)
            last = {}
            for r in b.readers:
                last[r.sem_key] = r
            dl.extend(last.values())
        dl.extend(d for d in deps if d is not None)
        if dma:
            bs = list(writes) or list(reads)
            assert bs, f"DMA op {tag} needs a read or write Buf"
            sem_key = f"dma{self.phase}_{bs[0].name}"
        else:
            sem_key = engine
        if self.pending_barrier is not None and engine in self.barrier_fresh:
            dl.append(self.pending_barrier)
            self.barrier_fresh.discard(engine)
        o = Op(engine, len(self.streams[engine]), fn, dl, dma, tag, sem_key)
        self.streams[engine].append(o)
        self.order.append(o)
        self.last_by_sem[sem_key] = o
        for b in reads:
            b.readers.append(o)
        for b in writes:
            b.writer = o
            b.readers = []
        return o

    def phase_barrier(self, tag="barrier"):
        deps = list(self.last_by_sem.values())
        o = self.op("sync", lambda e: e.nop(), deps=deps, tag=tag)
        self.pending_barrier = o
        self.barrier_fresh = set(ENGINES) - {"sync"}
        self.phase += 1
        return o

    def finalize(self, block, es, nc):
        for e in ENGINES:
            for o in self.streams[e]:
                if o.is_dma:
                    o.signals = True
                for d in o.deps:
                    if d.engine == o.engine and d.engine == "tensor" and not d.is_dma:
                        continue
                    d.signals = True
        cum = {}
        tot = {}
        for o in self.order:
            if o.signals:
                tot[o.sem_key] = tot.get(o.sem_key, 0) + (16 if o.is_dma else 1)
            cum[(o.engine, o.idx)] = tot.get(o.sem_key, 0)
        POOL = 48
        sems = {}
        base = {}
        for e in ENGINES:
            sems[e] = es.enter_context(nc.semaphore(f"s_{e}"))
            base[e] = 0
        pool = [es.enter_context(nc.semaphore(f"sd_{i}")) for i in range(POOL)]
        pool_accum = [0] * POOL
        dma_keys = [k for k in tot if k not in ENGINES]
        order_keys = []
        seen = set()
        for o in self.order:
            if o.sem_key not in seen and o.sem_key in tot and o.sem_key not in ENGINES:
                seen.add(o.sem_key)
                order_keys.append(o.sem_key)
        for ki, key in enumerate(order_keys):
            slot = ki % POOL
            sems[key] = pool[slot]
            base[key] = pool_accum[slot]
            pool_accum[slot] += tot[key]

        def emit(eng_obj, engine_name):
            waited = {}
            for o in self.streams[engine_name]:
                need = {}
                for d in o.deps:
                    if d.engine == engine_name and not d.is_dma \
                            and engine_name == "tensor":
                        continue
                    v = cum[(d.engine, d.idx)]
                    assert v > 0, f"dep not marked: {d.engine}:{d.idx} ({d.tag})"
                    need[d.sem_key] = max(need.get(d.sem_key, 0), v)
                for sk, v in need.items():
                    if waited.get(sk, 0) < v:
                        eng_obj.wait_ge(sems[sk], base[sk] + v)
                        waited[sk] = v
                inst = o.fn(eng_obj)
                if o.signals:
                    inst.then_inc(sems[o.sem_key], 16 if o.is_dma else 1)
            return waited

        @block.sync
        def _(sync):
            w = emit(sync, "sync")
            for key, v in tot.items():
                if v > 0 and w.get(key, 0) < v:
                    sync.wait_ge(sems[key], base[key] + v)

        @block.scalar
        def _(scalar):
            emit(scalar, "scalar")

        @block.vector
        def _(vector):
            emit(vector, "vector")

        @block.tensor
        def _(tensor):
            emit(tensor, "tensor")

        @block.gpsimd
        def _(gpsimd):
            emit(gpsimd, "gpsimd")



def _tile_bias(b):
    d = b.shape[0]
    return np.ascontiguousarray(b.reshape(d // 128, 128).T)


def _t(w):
    return np.ascontiguousarray(w.T)


def host_constants():
    c = {}
    c["c_ident_bf"] = np.eye(128, dtype=ml_dtypes.bfloat16)
    c["c_ones_row"] = np.ones((1, 128), np.float32)
    c["c_ones_all"] = np.ones((128, 128), np.float32)
    c["c_ones_col"] = np.ones((128, 1), np.float32)
    c["c_striu"] = np.triu(np.ones((128, 128), np.float32), 1)
    c["c_triu_mask"] = np.triu(np.ones((64, 64), np.float32), 0)
    s16 = np.zeros((16, 16 * 128), np.float32)
    for e in range(16):
        s16[e, e * 128:(e + 1) * 128] = 1.0
    c["c_sel16"] = s16
    s8 = np.zeros((8, 8 * 128), np.float32)
    for e in range(8):
        s8[e, e * 128:(e + 1) * 128] = 1.0
    c["c_sel8"] = s8
    return c



class Alloc:
    def __init__(self, nc, ctxstack):
        self.nc = nc
        self.es = ctxstack
        self.handles = {}

    def sb(self, name, shape, dtype):
        if name in self.handles:
            return self.handles[name]
        h = self.es.enter_context(self.nc.sbuf_tensor(name, shape, dtype))
        self.handles[name] = h
        return h

    def ps(self, name, shape, dtype=F32):
        if name in self.handles:
            return self.handles[name]
        h = self.es.enter_context(self.nc.psum_tensor(name, shape, dtype))
        self.handles[name] = h
        return h


class Ctx:

    def __init__(self, nc, prog, alloc):
        self.nc = nc
        self.P = prog
        self.alloc = alloc
        self.bufs = {}
        self.cb16 = {}
        self.cf32 = {}
        self.cbuf = Buf("consts")

    def buf(self, name):
        if name not in self.bufs:
            self.bufs[name] = Buf(name)
        return self.bufs[name]


CONST_SHAPES = {
    "ident": (128, 128),
    "ones_all": (128, 128),
    "ones_row": (1, 128),
    "ones_col": (128, 1),
    "striu": (128, 128),
    "triu_mask": (64, 64),
    "sel16": (16, 16 * 128),
    "sel8": (8, 8 * 128),
}


F32_CONSTS = ("ones_row", "ones_col", "ident", "ones_all")


def emit_load_constants(ctx, const_drams, names=None, stage_alloc=None):
    P, a = ctx.P, ctx.alloc
    names = names or list(CONST_SHAPES)
    for nm in names:
        shp = CONST_SHAPES[nm]
        if nm in F32_CONSTS:
            a.sb("sf_" + nm, list(shp), F32)
        a.sb("sc_" + nm, list(shp), BF16)
    stage = (stage_alloc or a).sb("c_stage", [128, 2048], F32)
    for nm in names:
        shp = CONST_SHAPES[nm]
        keep32 = nm in F32_CONSTS
        sf = a.sb("sf_" + nm, list(shp), F32) if keep32 else stage
        sb = a.sb("sc_" + nm, list(shp), BF16)
        if keep32:
            ctx.cf32[nm] = sf
        ctx.cb16[nm] = sb
        P.op("sync", lambda e, sf=sf, nm=nm, shp=shp: e.dma_start(
            out=sf[0:shp[0], 0:shp[1]], in_=const_drams[nm]),
            writes=[ctx.cbuf], dma=True, tag=f"c_{nm}_ld")
        P.op("vector", lambda e, sb=sb, sf=sf, shp=shp: e.tensor_copy(
            sb[:], sf[0:shp[0], 0:shp[1]]),
            reads=[ctx.cbuf], writes=[ctx.cbuf], tag=f"c_{nm}_cast")
    ctx.cb16 = {k: v.ap() for k, v in ctx.cb16.items()}
    ctx.cf32 = {k: v.ap() for k, v in ctx.cf32.items()}


def load_weight_bf16(ctx, w_dram, w_sb, stage_sb, stage_bufs, rows, cols,
                     tag, cast_engine="vector", dma_q="sync"):
    P = ctx.P
    wb = ctx.buf(tag)
    nk = rows // 128
    nst = len(stage_sb)
    for k in range(nk):
        st = stage_sb[k % nst]
        stb = stage_bufs[k % nst]
        P.op(dma_q, lambda e, k=k, st=st: e.dma_start(
            out=st[:, 0:cols], in_=w_dram[k * 128:(k + 1) * 128, :]),
            writes=[stb], dma=True, tag=f"{tag}_ld{k}")
        P.op(cast_engine, lambda e, k=k, st=st: e.tensor_copy(
            w_sb[:, k * cols:(k + 1) * cols], st[:, 0:cols]),
            reads=[stb], writes=[wb], tag=f"{tag}_cast{k}")
    return wb



def emit_mlp(ctx, xn_dram, w1_dram, b1_dram, w2_dram, b2_dram, out_mode,
             out_dram, b2row_dram=None, tag="mlp"):
    nc, P, a = ctx.nc, ctx.P, ctx.alloc
    HID = 4096
    HCH = HID // 128
    w1_sb = a.sb(f"{tag}_w1", [128, DCH * HID], BF16)
    w2_sb = a.sb(f"{tag}_w2", [128, HCH * D], BF16)
    stage = [a.sb(f"{tag}_st{i}", [128, 2048], F32) for i in range(1)]
    stageb = [ctx.buf(f"{tag}_st{i}") for i in range(1)]
    b1_sb = a.sb(f"{tag}_b1", [128, HCH], F32)
    xn_t = [a.sb(f"{tag}_xn{i}", [128, DCH * 512], BF16) for i in range(2)]
    h_t = a.sb(f"{tag}_h", [128, HCH * 512], BF16)
    p1 = [a.ps(f"{tag}_p1_{i}", [128, 512], F32) for i in range(4)]
    p2 = [a.ps(f"{tag}_p2_{i}", [128, 512], F32) for i in range(4)]
    b_w1 = ctx.buf(f"{tag}_w1")
    b_w2 = ctx.buf(f"{tag}_w2")
    b_b1 = ctx.buf(f"{tag}_b1")
    b_h = ctx.buf(f"{tag}_h")
    bp1 = [ctx.buf(f"{tag}_p1_{i}") for i in range(4)]
    bp2 = [ctx.buf(f"{tag}_p2_{i}") for i in range(4)]

    def load_w(w_dram, w_sb, wb, rows, cols, t2):
        nk = rows // 128
        cw = min(2048, cols)
        j = 0
        for k in range(nk):
            for half in range(cols // cw):
                st, stb = stage[0], stageb[0]
                j += 1
                P.op("sync", lambda e, k=k, h=half, st=st: e.dma_start(
                    out=st[:, 0:cw], in_=w_dram[k * 128:(k + 1) * 128,
                                                h * cw:(h + 1) * cw]),
                    writes=[stb], dma=True, tag=f"{t2}_ld{k}_{half}")
                P.op("vector", lambda e, k=k, h=half, st=st, cols=cols: e.tensor_copy(
                    w_sb[:, k * cols + h * cw: k * cols + (h + 1) * cw], st[:, 0:cw]),
                    reads=[stb], writes=[wb], tag=f"{t2}_cast{k}_{half}")

    load_w(w1_dram, w1_sb, b_w1, D, HID, tag + "w1")
    load_w(w2_dram, w2_sb, b_w2, HID, D, tag + "w2")
    P.op("sync", lambda e: e.dma_start(out=b1_sb[:], in_=b1_dram),
         writes=[b_b1], dma=True, tag=f"{tag}_b1ld")

    if out_mode == "fm":
        b2_sb = a.sb(f"{tag}_b2", [128, DCH], F32)
        b_b2 = ctx.buf(f"{tag}_b2")
        P.op("sync", lambda e: e.dma_start(out=b2_sb[:], in_=b2_dram),
             writes=[b_b2], dma=True, tag=f"{tag}_b2ld")
        out_t = a.sb(f"{tag}_out", [128, DCH * 512], BF16)
        b_out = ctx.buf(f"{tag}_out")
    else:
        b2r_sb = a.sb(f"{tag}_b2r", [1, D], F32)
        b2bc_sb = a.sb(f"{tag}_b2bc", [128, D], F32)
        b_b2r = ctx.buf(f"{tag}_b2r")
        b_b2bc = ctx.buf(f"{tag}_b2bc")
        P.op("sync", lambda e: e.dma_start(out=b2r_sb[:], in_=b2row_dram),
             writes=[b_b2r], dma=True, tag=f"{tag}_b2rld")
        b2rb_sb = a.sb(f"{tag}_b2rb", [1, D], BF16)
        P.op("vector", lambda e: e.tensor_copy(b2rb_sb[:], b2r_sb[:]),
             reads=[b_b2r], writes=[b_b2r], tag=f"{tag}_b2rcast")
        for seg in range(2):
            P.op("tensor", lambda e, s=seg: e.matmul(
                p2[3][:, 0:512], ctx.cb16["ones_row"],
                b2rb_sb[0:1, s * 512:(s + 1) * 512], start=True, stop=True),
                reads=[b_b2r, ctx.cbuf], writes=[bp2[3]], tag=f"{tag}_b2bc{seg}")
            P.op("scalar", lambda e, s=seg: e.copy(
                b2bc_sb[:, s * 512:(s + 1) * 512], p2[3][:, 0:512]),
                reads=[bp2[3]], writes=[b_b2bc], tag=f"{tag}_b2bcv{seg}")
        vaug = a.sb(f"{tag}_vaug", [128, 1040], BF16)
        b_vaug = ctx.buf(f"{tag}_vaug")
        P.op("vector", lambda e: e.memset(vaug[:], 1.0),
             writes=[b_vaug], tag=f"{tag}_vmemset")

    for n in range(NT):
        i = n % 2
        b_xn = ctx.buf(f"{tag}_xn{i}")
        P.op("sync", lambda e, n=n, i=i: e.dma_start(
            out=xn_t[i].ap().rearrange("p (c t) -> p c t", c=DCH),
            in_=xn_dram.rearrange("(c p) t -> p c t", p=128)[:, :, n * 512:(n + 1) * 512]),
            writes=[b_xn], dma=True, tag=f"{tag}_xnld{n}")
        for m in range(HCH):
            bk = m % 4
            for k in range(DCH):
                P.op("tensor", lambda e, m=m, k=k, i=i, bk=bk: e.matmul(
                    p1[bk][:], w1_sb[:, k * HID + m * 128: k * HID + (m + 1) * 128],
                    xn_t[i][:, k * 512:(k + 1) * 512],
                    start=(k == 0), stop=(k == DCH - 1)),
                    reads=[b_w1, b_xn], writes=[bp1[bk]],
                    tag=f"{tag}_mm1_{n}_{m}_{k}")
            P.op("scalar", lambda e, m=m, bk=bk: e.activation(
                out=h_t[:, m * 512:(m + 1) * 512], in_=p1[bk][:], func=AF.Gelu,
                bias=b1_sb[:, m:m + 1]),
                reads=[bp1[bk], b_b1], writes=[b_h], tag=f"{tag}_gelu{n}_{m}")
        if out_mode == "fm":
            for m2 in range(DCH):
                bk = m2 % 4
                for k2 in range(HCH):
                    P.op("tensor", lambda e, m2=m2, k2=k2, bk=bk: e.matmul(
                        p2[bk][:], w2_sb[:, k2 * D + m2 * 128: k2 * D + (m2 + 1) * 128],
                        h_t[:, k2 * 512:(k2 + 1) * 512],
                        start=(k2 == 0), stop=(k2 == HCH - 1)),
                        reads=[b_w2, b_h], writes=[bp2[bk]],
                        tag=f"{tag}_mm2_{n}_{m2}_{k2}")
                P.op("scalar", lambda e, m2=m2, bk=bk: e.activation(
                    out=out_t[:, m2 * 512:(m2 + 1) * 512], in_=p2[bk][:],
                    func=AF.Identity, bias=b2_sb[:, m2:m2 + 1]),
                    reads=[bp2[bk], b_b2], writes=[b_out], tag=f"{tag}_ev2_{n}_{m2}")
            P.op("sync", lambda e, n=n: e.dma_start(
                out=out_dram.rearrange("(c p) t -> p c t", p=128)[:, :, n * 512:(n + 1) * 512],
                in_=out_t.ap().rearrange("p (c t) -> p c t", c=DCH)),
                reads=[b_out], dma=True, tag=f"{tag}_ost{n}")
        else:
            for msub in range(4):
                for n2 in range(2):
                    bk = n2 % 2
                    for k2 in range(HCH):
                        P.op("tensor", lambda e, msub=msub, n2=n2, k2=k2, bk=bk: e.matmul(
                            p2[bk][:], h_t[:, k2 * 512 + msub * 128: k2 * 512 + (msub + 1) * 128],
                            w2_sb[:, k2 * D + n2 * 512: k2 * D + (n2 + 1) * 512],
                            start=(k2 == 0), stop=(k2 == HCH - 1)),
                            reads=[b_w2, b_h], writes=[bp2[bk]],
                            tag=f"{tag}_vmm2_{n}_{msub}_{n2}_{k2}")
                    P.op("vector", lambda e, n2=n2, bk=bk: e.tensor_add(
                        vaug.ap().rearrange("p (h s) -> p h s", s=65)[:, n2 * 8:(n2 + 1) * 8, 0:64],
                        p2[bk].ap().rearrange("p (h s) -> p h s", s=64),
                        b2bc_sb.ap().rearrange("p (h s) -> p h s", s=64)[:, n2 * 8:(n2 + 1) * 8, :]),
                        reads=[bp2[bk], b_b2bc], writes=[b_vaug],
                        tag=f"{tag}_vev_{n}_{msub}_{n2}")
                P.op("sync", lambda e, n=n, msub=msub: e.dma_start(
                    out=out_dram[n * 512 + msub * 128: n * 512 + (msub + 1) * 128, :],
                    in_=vaug[:]),
                    reads=[b_vaug], dma=True, tag=f"{tag}_vst{n}_{msub}")

def emit_ln(ctx, x_dram, outs, kind="ln", w_tiled_dram=None, eps=1e-5,
            tag="ln"):
    nc, P = ctx.nc, ctx.P
    a = ctx.alloc
    x_t = [a.sb(f"{tag}_x{i}", [128, DCH * 512], F32) for i in range(2)]
    xb_t = [a.sb(f"{tag}_xb{i}", [128, DCH * 512], BF16) for i in range(2)]
    sq_t = a.sb(f"{tag}_sq", [128, DCH * 512], BF16)
    orow = a.sb(f"{tag}_orow", [1, 512 * 4], F32)
    out_t = [a.sb(f"{tag}_o{i}", [128, DCH * 512], BF16) for i in range(2)]
    out32_t = [a.sb(f"{tag}_o32{i}", [128, DCH * 512], F32) for i in range(2)] \
        if any(dt == F32 for _, dt in outs) else None
    wln = a.sb(f"{tag}_w", [128, DCH], F32) if w_tiled_dram is not None else None
    ones_col = ctx.cb16["ones_col"]
    ones_row32 = ctx.cf32["ones_row"]

    p_stat = a.ps(f"{tag}_pstat", [1, 1024], F32)
    p_bc = a.ps(f"{tag}_pbc", [128, 1024], F32)

    if w_tiled_dram is not None:
        bw = ctx.buf(f"{tag}_w")
        P.op("sync", lambda e: e.dma_start(out=wln[:], in_=w_tiled_dram),
             writes=[bw], dma=True, tag=f"{tag}_wld")
    else:
        bw = None

    for n in range(NT):
        i = n % 2
        bx = ctx.buf(f"{tag}_x{i}")
        bxb = ctx.buf(f"{tag}_xb{i}")
        bsq = ctx.buf(f"{tag}_sq")
        bst = ctx.buf(f"{tag}_pstat")
        bbc = ctx.buf(f"{tag}_pbc")
        bor = ctx.buf(f"{tag}_orow")
        bo = ctx.buf(f"{tag}_o{i}")
        bo32 = ctx.buf(f"{tag}_o32{i}")
        P.op("sync", lambda e, n=n, i=i: e.dma_start(
            out=x_t[i].ap().rearrange("p (c t) -> p c t", c=DCH),
            in_=x_dram.rearrange("(c p) t -> p c t", p=128)[:, :, n * 512:(n + 1) * 512]),
            writes=[bx], dma=True, tag=f"{tag}_xld{n}")
        P.op("vector", lambda e, i=i: e.tensor_copy(xb_t[i][:], x_t[i][:]),
             reads=[bx], writes=[bxb], tag=f"{tag}_cast{n}")
        if kind == "ln":
            P.op("vector", lambda e, i=i: e.tensor_mul(sq_t[:], xb_t[i][:], xb_t[i][:]),
                 reads=[bxb], writes=[bsq], tag=f"{tag}_sq{n}")
        else:
            P.op("vector", lambda e, i=i: e.tensor_mul(sq_t[:], xb_t[i][:], xb_t[i][:]),
                 reads=[bxb], writes=[bsq], tag=f"{tag}_sq{n}")
        for c in range(DCH):
            if kind == "ln":
                P.op("tensor", lambda e, c=c, i=i: e.matmul(
                    p_stat[0:1, 0:512], ones_col, xb_t[i][:, c * 512:(c + 1) * 512],
                    start=(c == 0), stop=(c == DCH - 1)),
                    reads=[bxb, ctx.cbuf], writes=[bst], tag=f"{tag}_smm{n}_{c}")
            P.op("tensor", lambda e, c=c: e.matmul(
                p_stat[0:1, 512:1024], ones_col, sq_t[:, c * 512:(c + 1) * 512],
                start=(c == 0), stop=(c == DCH - 1)),
                reads=[bsq, ctx.cbuf], writes=[bst], tag=f"{tag}_qmm{n}_{c}")
        P.op("scalar", lambda e: e.copy(orow[0:1, 0:1024], p_stat[0:1, 0:1024]),
             reads=[bst], writes=[bor], tag=f"{tag}_evst{n}")
        if kind == "ln":
            P.op("vector", lambda e: e.tensor_scalar_mul(
                orow[0:1, 1024:1536], orow[0:1, 0:512], 1.0 / D),
                reads=[bor], writes=[bor], tag=f"{tag}_mean{n}")
            P.op("vector", lambda e: e.tensor_mul(
                orow[0:1, 0:512], orow[0:1, 1024:1536], orow[0:1, 1024:1536]),
                reads=[bor], writes=[bor], tag=f"{tag}_m2{n}")
            P.op("vector", lambda e: e.scalar_tensor_tensor(
                orow[0:1, 1536:2048], orow[0:1, 512:1024], 1.0 / D,
                orow[0:1, 0:512], ALU.mult, ALU.subtract),
                reads=[bor], writes=[bor], tag=f"{tag}_var{n}")
        else:
            P.op("vector", lambda e: e.tensor_scalar_mul(
                orow[0:1, 1536:2048], orow[0:1, 512:1024], 1.0 / D),
                reads=[bor], writes=[bor], tag=f"{tag}_var{n}")
        P.op("vector", lambda e: e.tensor_scalar_add(
            orow[0:1, 1536:2048], orow[0:1, 1536:2048], eps),
            reads=[bor], writes=[bor], tag=f"{tag}_eps{n}")
        P.op("scalar", lambda e: e.activation(
            out=orow[0:1, 512:1024], in_=orow[0:1, 1536:2048], func=AF.Sqrt),
            reads=[bor], writes=[bor], tag=f"{tag}_sqrt{n}")
        P.op("vector", lambda e: e.reciprocal(
            orow[0:1, 1536:2048], orow[0:1, 512:1024]),
            reads=[bor], writes=[bor], tag=f"{tag}_rcp{n}")
        if kind == "ln":
            P.op("tensor", lambda e: e.matmul(
                p_bc[:, 0:512], ones_row32, orow[0:1, 1024:1536],
                start=True, stop=True),
                reads=[bor, ctx.cbuf], writes=[bbc], tag=f"{tag}_bcm{n}")
        P.op("tensor", lambda e: e.matmul(
            p_bc[:, 512:1024], ones_row32, orow[0:1, 1536:2048],
            start=True, stop=True),
            reads=[bor, ctx.cbuf], writes=[bbc], tag=f"{tag}_bcr{n}")
        for c in range(DCH):
            if kind == "ln":
                P.op("vector", lambda e, c=c, i=i: e.tensor_sub(
                    x_t[i][:, c * 512:(c + 1) * 512],
                    x_t[i][:, c * 512:(c + 1) * 512], p_bc[:, 0:512]),
                    reads=[bx, bbc], writes=[bx], tag=f"{tag}_sub{n}_{c}")
            tgt32 = out32_t[i] if out32_t is not None else None
            if tgt32 is not None:
                P.op("vector", lambda e, c=c, i=i: e.tensor_mul(
                    out32_t[i][:, c * 512:(c + 1) * 512],
                    x_t[i][:, c * 512:(c + 1) * 512], p_bc[:, 512:1024]),
                    reads=[bx, bbc], writes=[bo32], tag=f"{tag}_mul{n}_{c}")
                src, bsrc = out32_t[i], bo32
            else:
                src, bsrc = x_t[i], bx
                P.op("vector", lambda e, c=c, i=i: e.tensor_mul(
                    x_t[i][:, c * 512:(c + 1) * 512],
                    x_t[i][:, c * 512:(c + 1) * 512], p_bc[:, 512:1024]),
                    reads=[bx, bbc], writes=[bx], tag=f"{tag}_mul{n}_{c}")
            if w_tiled_dram is not None:
                P.op("vector", lambda e, c=c, s=src: e.tensor_scalar_mul(
                    s[:, c * 512:(c + 1) * 512], s[:, c * 512:(c + 1) * 512],
                    wln[:, c:c + 1]),
                    reads=[bsrc, bw], writes=[bsrc], tag=f"{tag}_w{n}_{c}")
            P.op("vector", lambda e, c=c, i=i, s=src: e.tensor_copy(
                out_t[i][:, c * 512:(c + 1) * 512], s[:, c * 512:(c + 1) * 512]),
                reads=[bsrc], writes=[bo], tag=f"{tag}_obf{n}_{c}")
        for oi, (od, odt) in enumerate(outs):
            srcsb = out_t[i] if odt == BF16 else out32_t[i]
            srcb = bo if odt == BF16 else bo32
            P.op("sync", lambda e, n=n, s=srcsb, od=od: e.dma_start(
                out=od.rearrange("(c p) t -> p c t", p=128)[:, :, n * 512:(n + 1) * 512],
                in_=s.ap().rearrange("p (c t) -> p c t", c=DCH)),
                reads=[srcb], dma=True, tag=f"{tag}_st{n}_{oi}")



from contextlib import ExitStack


def build_launch(dram_specs, const_names, phase_fns):
    nc = bass.Bass("TRN2", target_bir_lowering=False, debug=False)
    draps = {}
    for nm, (shape, dtype, kind) in dram_specs.items():
        draps[nm] = nc.dram_tensor(nm, list(shape), dtype, kind=kind).ap()
    with ExitStack() as es:
        alloc = Alloc(nc, es)
        P = Prog(nc)
        ctx = Ctx(nc, P, alloc)
        if const_names:
            cd = {nm: draps["c_" + nm] for nm in const_names}
            emit_load_constants(ctx, cd, const_names)
        for f in phase_fns:
            f(ctx, draps)
        block = es.enter_context(nc.Block())
        P.finalize(block, es, nc)
    return nc


_NC_CACHE = {}


def _run(cache_key, dram_specs, const_names, phase_fns, in_maps):
    if cache_key not in _NC_CACHE:
        _NC_CACHE[cache_key] = build_launch(dram_specs, const_names, phase_fns)
    nc = _NC_CACHE[cache_key]
    res = run_bass_kernel_spmd(nc, in_maps, core_ids=list(range(N_CORES)))
    return res.results


CONST_HOST = None


def get_const_host():
    global CONST_HOST
    if CONST_HOST is None:
        c = host_constants()
        CONST_HOST = {k[2:]: v.astype(np.float32) for k, v in c.items()}
        CONST_HOST["ident"] = np.eye(128, dtype=np.float32)
    return CONST_HOST


def const_specs(names):
    ch = get_const_host()
    return {("c_" + n): (ch[n].shape, F32, "ExternalInput") for n in names}


def const_inmaps(names):
    ch = get_const_host()
    return {("c_" + n): ch[n] for n in names}



def run_ln0(x_fm_cores):
    names = ["ones_row", "ones_col"]
    specs = {
        "x_fm": ((D, T), F32, "ExternalInput"),
        "xn0": ((D, T), BF16, "ExternalOutput"),
        **const_specs(names),
    }

    def ph(ctx, dr):
        emit_ln(ctx, dr["x_fm"], [(dr["xn0"], BF16)], kind="ln", tag="ln0")

    im = [dict(x_fm=x, **const_inmaps(names)) for x in x_fm_cores]
    res = _run("ln0", specs, names, [ph], im)
    return [r["xn0"] for r in res]


def run_mlp(xn_cores, w1, b1, w2, b2, mode, tag):
    names = ["ones_row", "ones_col"]
    out_spec = ((D, T), BF16) if mode == "fm" else ((T, 16 * 65), BF16)
    specs = {
        "xn": ((D, T), BF16, "ExternalInput"),
        "w1": ((D, 4096), F32, "ExternalInput"),
        "b1": ((128, 32), F32, "ExternalInput"),
        "w2": ((4096, D), F32, "ExternalInput"),
        "b2": ((128, 8), F32, "ExternalInput"),
        "out": (out_spec[0], out_spec[1], "ExternalOutput"),
        **const_specs(names),
    }
    if mode != "fm":
        specs["b2row"] = ((1, D), F32, "ExternalInput")

    def ph(ctx, dr):
        emit_mlp(ctx, dr["xn"], dr["w1"], dr["b1"], dr["w2"], dr["b2"],
                 mode, dr["out"], b2row_dram=dr.get("b2row"), tag="mlp")

    base = dict(w1=np.ascontiguousarray(w1), b1=_tile_bias(b1),
                w2=np.ascontiguousarray(w2), b2=_tile_bias(b2),
                **const_inmaps(names))
    if mode != "fm":
        base["b2row"] = np.ascontiguousarray(b2.reshape(1, D))
    im = [dict(xn=xn, **base) for xn in xn_cores]
    res = _run(("mlp", mode), specs, names, [ph], im)
    return [r["out"] for r in res]


def emit_linear(ctx, x_dram, w_dram, b_dram, out_mode, out_dram, din, dout,
                act=None, out_dtype=BF16, b_row_dram=None, x2_dram=None,
                res_drams=(), aug=False, tag="lin"):
    nc, P, a = ctx.nc, ctx.P, ctx.alloc
    KCH = din // 128
    MCH = dout // 128
    w_sb = a.sb(f"{tag}_w", [128, KCH * dout], BF16)
    stage = [a.sb(f"{tag}_st{i}", [128, 2048], F32) for i in range(2)]
    stageb = [ctx.buf(f"{tag}_st{i}") for i in range(2)]
    b_w = ctx.buf(f"{tag}_w")

    cw = min(2048, dout)
    j = 0
    for k in range(KCH):
        for half in range(dout // cw):
            st, stb = stage[j % 2], stageb[j % 2]
            j += 1
            P.op("sync", lambda e, k=k, h=half, st=st: e.dma_start(
                out=st[:, 0:cw], in_=w_dram[k * 128:(k + 1) * 128,
                                            h * cw:(h + 1) * cw]),
                writes=[stb], dma=True, tag=f"{tag}_wld{k}_{half}")
            P.op("vector", lambda e, k=k, h=half, st=st: e.tensor_copy(
                w_sb[:, k * dout + h * cw: k * dout + (h + 1) * cw], st[:, 0:cw]),
                reads=[stb], writes=[b_w], tag=f"{tag}_wc{k}_{half}")

    if b_dram is not None and out_mode == "fm":
        b_sb = a.sb(f"{tag}_b", [128, MCH], F32)
        b_b = ctx.buf(f"{tag}_b")
        P.op("sync", lambda e: e.dma_start(out=b_sb[:], in_=b_dram),
             writes=[b_b], dma=True, tag=f"{tag}_bld")
    else:
        b_sb, b_b = None, None

    x_t = [a.sb(f"{tag}_x{i}", [128, KCH * 512], BF16) for i in range(2)]
    if x2_dram is not None:
        x2_t = [a.sb(f"{tag}_x2{i}", [128, KCH * 512], BF16) for i in range(2)]

    nres = len(res_drams)
    if nres:
        r_t = [a.sb(f"{tag}_r{i}", [128, DCH * 512], F32) for i in range(nres)]

    tmp = a.sb(f"{tag}_tmp", [128, 512], F32)
    b_tmp = ctx.buf(f"{tag}_tmp")
    if out_mode == "fm":
        ot = a.sb(f"{tag}_o", [128, MCH * 512],
                  F32 if (out_dtype == F32 or nres) else BF16)
        b_ot = ctx.buf(f"{tag}_o")
    else:
        nh = dout // 64
        width = nh * 65 if aug else dout
        vaug = a.sb(f"{tag}_vaug", [128, width], BF16)
        b_vaug = ctx.buf(f"{tag}_vaug")
        P.op("vector", lambda e: e.memset(vaug[:], 1.0),
             writes=[b_vaug], tag=f"{tag}_vmemset")
        if b_row_dram is not None:
            b2r_sb = a.sb(f"{tag}_b2r", [1, dout], F32)
            b2rb_sb = a.sb(f"{tag}_b2rb", [1, dout], BF16)
            b2bc_sb = a.sb(f"{tag}_b2bc", [128, dout], F32)
            b_b2r = ctx.buf(f"{tag}_b2r")
            b_b2bc = ctx.buf(f"{tag}_b2bc")
            P.op("sync", lambda e: e.dma_start(out=b2r_sb[:], in_=b_row_dram),
                 writes=[b_b2r], dma=True, tag=f"{tag}_b2rld")
            P.op("vector", lambda e: e.tensor_copy(b2rb_sb[:], b2r_sb[:]),
                 reads=[b_b2r], writes=[b_b2r], tag=f"{tag}_b2rc")
    p_mm = [a.ps(f"{tag}_p{i}", [128, 512], F32) for i in range(4)]
    bp = [ctx.buf(f"{tag}_p{i}") for i in range(4)]
    if out_mode == "tm" and b_row_dram is not None:
        segw = min(512, dout)
        for seg in range(dout // segw):
            P.op("tensor", lambda e, s=seg: e.matmul(
                p_mm[3][:, 0:segw], ctx.cb16["ones_row"],
                b2rb_sb[0:1, s * segw:(s + 1) * segw], start=True, stop=True),
                reads=[b_b2r, ctx.cbuf], writes=[bp[3]], tag=f"{tag}_bbc{seg}")
            P.op("scalar", lambda e, s=seg: e.copy(
                b2bc_sb[:, s * segw:(s + 1) * segw], p_mm[3][:, 0:segw]),
                reads=[bp[3]], writes=[b_b2bc], tag=f"{tag}_bbcv{seg}")

    def evac_act(dst_ap, p_ap, bias_ap, bsrcs, bdst, t2):
        if act is None or act == "gelu" or act == "silu":
            func = {None: AF.Identity, "gelu": AF.Gelu, "silu": AF.Silu}[act]
            if bias_ap is not None:
                P.op("scalar", lambda e: e.activation(
                    out=dst_ap, in_=p_ap, func=func, bias=bias_ap),
                    reads=bsrcs, writes=[bdst], tag=t2)
            else:
                P.op("scalar", lambda e: e.activation(
                    out=dst_ap, in_=p_ap, func=func),
                    reads=bsrcs, writes=[bdst], tag=t2)
        elif act == "elu1":
            P.op("vector", lambda e: e.tensor_scalar_min(tmp[:], p_ap, 0.0),
                 reads=bsrcs, writes=[b_tmp], tag=t2 + "_min")
            P.op("scalar", lambda e: e.activation(
                out=tmp[:], in_=tmp[:], func=AF.Exp),
                reads=[b_tmp], writes=[b_tmp], tag=t2 + "_exp")
            P.op("vector", lambda e: e.scalar_tensor_tensor(
                dst_ap, p_ap, 0.0, tmp[:], ALU.max, ALU.add),
                reads=bsrcs + [b_tmp], writes=[bdst], tag=t2 + "_add")
        else:
            raise ValueError(act)

    for n in range(NT):
        i = n % 2
        b_x = ctx.buf(f"{tag}_x{i}")
        P.op("sync", lambda e, n=n, i=i: e.dma_start(
            out=x_t[i].ap().rearrange("p (c t) -> p c t", c=KCH),
            in_=x_dram.rearrange("(c p) t -> p c t", p=128)[:, :, n * 512:(n + 1) * 512]),
            writes=[b_x], dma=True, tag=f"{tag}_xld{n}")
        if x2_dram is not None:
            b_x2 = ctx.buf(f"{tag}_x2{i}")
            P.op("sync", lambda e, n=n, i=i: e.dma_start(
                out=x2_t[i].ap().rearrange("p (c t) -> p c t", c=KCH),
                in_=x2_dram.rearrange("(c p) t -> p c t", p=128)[:, :, n * 512:(n + 1) * 512]),
                writes=[b_x2], dma=True, tag=f"{tag}_x2ld{n}")
            P.op("vector", lambda e, i=i: e.tensor_mul(
                x_t[i][:], x_t[i][:], x2_t[i][:]),
                reads=[b_x, b_x2], writes=[b_x], tag=f"{tag}_xmul{n}")
        if nres:
            for ri, rd in enumerate(res_drams):
                b_r = ctx.buf(f"{tag}_r{ri}")
                P.op("sync", lambda e, n=n, rd=rd, ri=ri: e.dma_start(
                    out=r_t[ri].ap().rearrange("p (c t) -> p c t", c=DCH),
                    in_=rd.rearrange("(c p) t -> p c t", p=128)[:, :, n * 512:(n + 1) * 512]),
                    writes=[b_r], dma=True, tag=f"{tag}_rld{n}_{ri}")

        if out_mode == "fm":
            for m in range(MCH):
                bk = m % 3
                for k in range(KCH):
                    P.op("tensor", lambda e, m=m, k=k, i=i, bk=bk: e.matmul(
                        p_mm[bk][:], w_sb[:, k * dout + m * 128: k * dout + (m + 1) * 128],
                        x_t[i][:, k * 512:(k + 1) * 512],
                        start=(k == 0), stop=(k == KCH - 1)),
                        reads=[b_w, b_x], writes=[bp[bk]],
                        tag=f"{tag}_mm{n}_{m}_{k}")
                bias_ap = b_sb[:, m:m + 1] if b_sb is not None else None
                rsrc = [bp[bk]] + ([b_b] if b_sb is not None else [])
                evac_act(ot[:, m * 512:(m + 1) * 512], p_mm[bk][:], bias_ap,
                         rsrc, b_ot, f"{tag}_ev{n}_{m}")
            if nres:
                for ri in range(nres):
                    b_r = ctx.buf(f"{tag}_r{ri}")
                    P.op("vector", lambda e, ri=ri: e.tensor_add(
                        ot[:], ot[:], r_t[ri][:]),
                        reads=[ctx.buf(f"{tag}_r{ri}"), b_ot], writes=[b_ot],
                        tag=f"{tag}_res{n}_{ri}")
            ot_is_f32 = (out_dtype == F32 or nres)
            store_q = "gpsimd" if (ot_is_f32 and out_dtype == BF16) else "sync"
            P.op(store_q, lambda e, n=n: e.dma_start(
                out=out_dram.rearrange("(c p) t -> p c t", p=128)[:, :, n * 512:(n + 1) * 512],
                in_=ot.ap().rearrange("p (c t) -> p c t", c=MCH)),
                reads=[b_ot], dma=True, tag=f"{tag}_ost{n}")
        else:
            segw = min(512, dout)
            for msub in range(4):
                for n2 in range(dout // segw):
                    bk = n2 % 2
                    for k in range(KCH):
                        P.op("tensor", lambda e, msub=msub, n2=n2, k=k, bk=bk, i=i: e.matmul(
                            p_mm[bk][:, 0:segw],
                            x_t[i][:, k * 512 + msub * 128: k * 512 + (msub + 1) * 128],
                            w_sb[:, k * dout + n2 * segw: k * dout + (n2 + 1) * segw],
                            start=(k == 0), stop=(k == KCH - 1)),
                            reads=[b_w, b_x], writes=[bp[bk]],
                            tag=f"{tag}_tmm{n}_{msub}_{n2}_{k}")
                    nh_seg = segw // 64
                    if aug:
                        dst = vaug.ap().rearrange("p (h s) -> p h s", s=65)[
                            :, n2 * nh_seg:(n2 + 1) * nh_seg, 0:64]
                    else:
                        dst = vaug[:, n2 * segw:(n2 + 1) * segw]
                    if act == "elu1":
                        evac_act(dst, p_mm[bk][:, 0:segw], None, [bp[bk]],
                                 b_vaug, f"{tag}_tev{n}_{msub}_{n2}")
                    elif b_row_dram is not None:
                        P.op("vector", lambda e, n2=n2, bk=bk, dst=dst: e.tensor_add(
                            dst,
                            p_mm[bk].ap()[:, 0:segw].rearrange("p (h s) -> p h s", s=64)
                            if aug else p_mm[bk][:, 0:segw],
                            b2bc_sb.ap()[:, n2 * segw:(n2 + 1) * segw].rearrange(
                                "p (h s) -> p h s", s=64)
                            if aug else b2bc_sb[:, n2 * segw:(n2 + 1) * segw]),
                            reads=[bp[bk], b_b2bc], writes=[b_vaug],
                            tag=f"{tag}_tev{n}_{msub}_{n2}")
                    else:
                        if aug:
                            src = p_mm[bk].ap()[:, 0:segw].rearrange(
                                "p (h s) -> p h s", s=64)
                        else:
                            src = p_mm[bk][:, 0:segw]
                        P.op("vector", lambda e, dst=dst, src=src: e.tensor_copy(
                            dst, src),
                            reads=[bp[bk]], writes=[b_vaug],
                            tag=f"{tag}_tev{n}_{msub}_{n2}")
                P.op("sync", lambda e, n=n, msub=msub: e.dma_start(
                    out=out_dram[n * 512 + msub * 128: n * 512 + (msub + 1) * 128, :],
                    in_=vaug[:]),
                    reads=[b_vaug], dma=True, tag=f"{tag}_tst{n}_{msub}")


def emit_attn(ctx, q_dram, k_dram, k_rep, vaug_dram, nheads, kvheads,
              proj_w_dram, proj_b_dram, out_dram, res_drams=(), tag="attn"):
    nc, P, a = ctx.nc, ctx.P, ctx.alloc
    W = kvheads * 65
    k_sb = a.sb(f"{tag}_k", [128, DCH * T], BF16)
    v_sb = a.sb(f"{tag}_v", [128, 32 * W], BF16)
    q_t = [a.sb(f"{tag}_q0", [128, DCH * 512], BF16)]
    pt_sb = a.sb(f"{tag}_pt", [128, 4 * 512], BF16)
    cneg_sb = a.sb(f"{tag}_cneg", [128, nheads], F32)
    mx_sb = a.sb(f"{tag}_mx", [128, 2], F32)
    csc_sb = a.sb(f"{tag}_csc", [2, 1], F32)
    trow_sb = a.sb(f"{tag}_trow", [2, 128], F32)
    diag2_sb = a.sb(f"{tag}_diag", [2, 2], F32)
    srow_sb = a.sb(f"{tag}_srow", [1, 1024], F32)
    rbc_sb = a.sb(f"{tag}_rbc", [64, 1024], F32)
    ao_sb = a.sb(f"{tag}_ao", [128, DCH * 512], BF16)
    pw_sb = a.sb(f"{tag}_pw", [128, DCH * D], BF16)
    pb_sb = a.sb(f"{tag}_pb", [128, DCH], F32)
    stage = [a.sb(f"{tag}_st{i}", [128, 1024], F32) for i in range(1)]
    stageb = [ctx.buf(f"{tag}_st{i}") for i in range(1)]
    ot = a.sb(f"{tag}_ot", [128, 2 * 512], F32)
    r_t = [a.sb(f"{tag}_r{i}", [128, 2 * 512], F32) for i in range(len(res_drams))]

    p_sc = [a.ps(f"{tag}_psc{i}", [128, 512], F32) for i in range(4)]
    p_av = [a.ps(f"{tag}_pav{i}", [65, 512], F32) for i in range(2)]
    p_bc = a.ps(f"{tag}_pbc", [128, 512], F32)
    bsc = [ctx.buf(f"{tag}_psc{i}") for i in range(4)]
    bav = [ctx.buf(f"{tag}_pav{i}") for i in range(2)]
    bbc = ctx.buf(f"{tag}_pbc")
    b_k = ctx.buf(f"{tag}_k")
    b_v = ctx.buf(f"{tag}_v")
    b_pt = ctx.buf(f"{tag}_pt")
    b_cn = ctx.buf(f"{tag}_cneg")
    b_mx = ctx.buf(f"{tag}_mx")
    b_csc = ctx.buf(f"{tag}_csc")
    b_trow = ctx.buf(f"{tag}_trow")
    b_diag = ctx.buf(f"{tag}_diag")
    b_srow = ctx.buf(f"{tag}_srow")
    b_rbc = ctx.buf(f"{tag}_rbc")
    b_ao = ctx.buf(f"{tag}_ao")
    b_pw = ctx.buf(f"{tag}_pw")
    b_pb = ctx.buf(f"{tag}_pb")
    b_ot = ctx.buf(f"{tag}_ot")

    if not k_rep:
        for c in range(DCH):
            P.op("sync", lambda e, c=c: e.dma_start(
                out=k_sb[:, c * T:(c + 1) * T],
                in_=k_dram[c * 128:(c + 1) * 128, :]),
                writes=[b_k], dma=True, tag=f"{tag}_kld{c}")
    else:
        for c in range(DCH):
            j = c // 2
            for half in range(2):
                P.op("sync", lambda e, c=c, j=j, half=half: e.dma_start(
                    out=k_sb[half * 64:(half + 1) * 64, c * T:(c + 1) * T],
                    in_=k_dram[j * 64:(j + 1) * 64, :]),
                    writes=[b_k], dma=True, tag=f"{tag}_kld{c}_{half}")
    P.op("sync", lambda e: e.dma_start(
        out=v_sb.ap().rearrange("p (t w) -> p t w", w=W),
        in_=vaug_dram.rearrange("(t p) w -> p t w", p=128)),
        writes=[b_v], dma=True, tag=f"{tag}_vld")
    for k in range(DCH):
        st, stb = stage[0], stageb[0]
        P.op("sync", lambda e, k=k, st=st: e.dma_start(
            out=st[:, 0:1024], in_=proj_w_dram[k * 128:(k + 1) * 128, :]),
            writes=[stb], dma=True, tag=f"{tag}_pwld{k}")
        P.op("vector", lambda e, k=k, st=st: e.tensor_copy(
            pw_sb[:, k * D:(k + 1) * D], st[:, 0:1024]),
            reads=[stb], writes=[b_pw], tag=f"{tag}_pwc{k}")
    P.op("sync", lambda e: e.dma_start(out=pb_sb[:], in_=proj_b_dram),
         writes=[b_pb], dma=True, tag=f"{tag}_pbld")

    ones_row32 = ctx.cf32["ones_row"]
    ones_all32 = ctx.cf32["ones_all"]
    ident32 = ctx.cf32["ident"]
    first_done = [False] * nheads

    for qt in range(NT):
        i = 0
        b_q = ctx.buf(f"{tag}_q0")
        P.op("sync", lambda e, qt=qt, i=i: e.dma_start(
            out=q_t[i].ap().rearrange("p (c t) -> p c t", c=DCH),
            in_=q_dram.rearrange("(c p) t -> p c t", p=128)[:, :, qt * 512:(qt + 1) * 512]),
            writes=[b_q], dma=True, tag=f"{tag}_qld{qt}")

        for c in range(nheads // 2):
            for hh in range(2):
                h = 2 * c + hh
                pass
            for t in range(32):
                for hh in range(2):
                    sbk = 2 * (t % 2) + hh
                    P.op("tensor", lambda e, c=c, hh=hh, t=t, i=i, sbk=sbk: e.matmul(
                        p_sc[sbk][:],
                        k_sb[hh * 64:(hh + 1) * 64, c * T + t * 128: c * T + (t + 1) * 128],
                        q_t[i][hh * 64:(hh + 1) * 64, c * 512:(c + 1) * 512],
                        start=True, stop=True),
                        reads=[b_k, b_q], writes=[bsc[sbk]],
                        tag=f"{tag}_smm{qt}_{c}_{t}_{hh}")
                if not first_done[c]:
                    first_done[c] = True
                    for hh in range(2):
                        P.op("vector", lambda e, hh=hh: e.reduce_max(
                            mx_sb[:, hh:hh + 1], p_sc[hh][:], axis=AX.X),
                            reads=[bsc[hh]], writes=[b_mx], tag=f"{tag}_cm1_{c}_{hh}")
                    P.op("tensor", lambda e: e.transpose(
                        p_bc[0:2, 0:128], mx_sb[:, 0:2], ident32),
                        reads=[b_mx, ctx.cbuf], writes=[bbc], tag=f"{tag}_ctr_{c}")
                    P.op("scalar", lambda e: e.copy(
                        srow_sb[0:2, 0:128] if False else trow_sb[:],
                        p_bc[0:2, 0:128]),
                        reads=[bbc], writes=[b_trow], tag=f"{tag}_cte_{c}")
                    P.op("vector", lambda e: e.reduce_max(
                        csc_sb[0:2, 0:1], trow_sb[:], axis=AX.X),
                        reads=[b_trow], writes=[b_csc], tag=f"{tag}_cm2_{c}")
                    P.op("vector", lambda e: e.tensor_scalar_mul(
                        csc_sb[0:2, 0:1], csc_sb[0:2, 0:1], -0.125),
                        reads=[b_csc], writes=[b_csc], tag=f"{tag}_cns_{c}")
                    P.op("vector", lambda e: e.tensor_mul(
                        diag2_sb[:], csc_sb[0:2, 0:1].to_broadcast([2, 2]),
                        ident32[0:2, 0:2]),
                        reads=[b_csc, ctx.cbuf], writes=[b_diag],
                        tag=f"{tag}_cdg_{c}")
                    P.op("tensor", lambda e: e.matmul(
                        p_bc[:, 0:2], ones_all32[0:2, 0:128], diag2_sb[:],
                        start=True, stop=True),
                        reads=[b_diag, ctx.cbuf], writes=[bbc], tag=f"{tag}_cbc_{c}")
                    P.op("scalar", lambda e, c=c: e.copy(
                        cneg_sb[:, 2 * c:2 * c + 2], p_bc[:, 0:2]),
                        reads=[bbc], writes=[b_cn], tag=f"{tag}_cev_{c}")
                for hh in range(2):
                    h = 2 * c + hh
                    sbk = 2 * (t % 2) + hh
                    P.op("scalar", lambda e, h=h, hh=hh, sbk=sbk: e.activation(
                        out=pt_sb[:, sbk * 512:(sbk + 1) * 512], in_=p_sc[sbk][:],
                        func=AF.Exp, bias=cneg_sb[:, h:h + 1], scale=0.125),
                        reads=[bsc[sbk], b_cn], writes=[b_pt],
                        tag=f"{tag}_exp{qt}_{c}_{t}_{hh}")
                    kv = h * kvheads // nheads
                    P.op("tensor", lambda e, t=t, kv=kv, hh=hh, sbk=sbk: e.matmul(
                        p_av[hh][:],
                        v_sb[:, t * W + kv * 65: t * W + kv * 65 + 65],
                        pt_sb[:, sbk * 512:(sbk + 1) * 512],
                        start=(t == 0), stop=(t == 31)),
                        reads=[b_v, b_pt], writes=[bav[hh]],
                        tag=f"{tag}_av{qt}_{c}_{t}_{hh}")
            for hh in range(2):
                h = 2 * c + hh
                P.op("scalar", lambda e, hh=hh: e.copy(
                    srow_sb[0:1, hh * 512:(hh + 1) * 512], p_av[hh][64:65, :]),
                    reads=[bav[hh]], writes=[b_srow], tag=f"{tag}_sev{qt}_{h}")
                P.op("vector", lambda e, hh=hh: e.reciprocal(
                    srow_sb[0:1, hh * 512:(hh + 1) * 512],
                    srow_sb[0:1, hh * 512:(hh + 1) * 512]),
                    reads=[b_srow], writes=[b_srow], tag=f"{tag}_rcp{qt}_{h}")
                P.op("tensor", lambda e, hh=hh: e.matmul(
                    p_bc[0:64, :], ones_row32[0:1, 0:64],
                    srow_sb[0:1, hh * 512:(hh + 1) * 512],
                    start=True, stop=True),
                    reads=[b_srow, ctx.cbuf], writes=[bbc], tag=f"{tag}_rbc{qt}_{h}")
                P.op("scalar", lambda e, hh=hh: e.copy(
                    rbc_sb[:, hh * 512:(hh + 1) * 512], p_bc[0:64, :]),
                    reads=[bbc], writes=[b_rbc], tag=f"{tag}_rbe{qt}_{h}")
                P.op("vector", lambda e, c=c, hh=hh: e.tensor_mul(
                    ao_sb[hh * 64:(hh + 1) * 64, c * 512:(c + 1) * 512],
                    p_av[hh].ap()[0:64, :],
                    rbc_sb[:, hh * 512:(hh + 1) * 512]),
                    reads=[bav[hh], b_rbc], writes=[b_ao],
                    tag=f"{tag}_nrm{qt}_{h}")
        for mg in range(DCH // 2):
            for ri, rd in enumerate(res_drams):
                b_r = ctx.buf(f"{tag}_r{ri}")
                P.op("sync", lambda e, qt=qt, rd=rd, ri=ri, mg=mg: e.dma_start(
                    out=r_t[ri].ap().rearrange("p (c t) -> p c t", c=2),
                    in_=rd.rearrange("(c p) t -> p c t", p=128)[
                        :, mg * 2:(mg + 1) * 2, qt * 512:(qt + 1) * 512]),
                    writes=[b_r], dma=True, tag=f"{tag}_rld{qt}_{mg}_{ri}")
            for mm in range(2):
                m = mg * 2 + mm
                pbk = m % 2
                for k in range(DCH):
                    P.op("tensor", lambda e, m=m, k=k, pbk=pbk: e.matmul(
                        p_sc[pbk][:], pw_sb[:, k * D + m * 128: k * D + (m + 1) * 128],
                        ao_sb[:, k * 512:(k + 1) * 512],
                        start=(k == 0), stop=(k == DCH - 1)),
                        reads=[b_pw, b_ao], writes=[bsc[pbk]],
                        tag=f"{tag}_pmm{qt}_{m}_{k}")
                P.op("scalar", lambda e, mm=mm, pbk=pbk, m=m: e.activation(
                    out=ot[:, mm * 512:(mm + 1) * 512], in_=p_sc[pbk][:],
                    func=AF.Identity, bias=pb_sb[:, m:m + 1]),
                    reads=[bsc[pbk], b_pb], writes=[b_ot], tag=f"{tag}_pev{qt}_{m}")
            for ri in range(len(res_drams)):
                P.op("vector", lambda e, ri=ri: e.tensor_add(
                    ot[:], ot[:], r_t[ri][:]),
                    reads=[ctx.buf(f"{tag}_r{ri}"), b_ot], writes=[b_ot],
                    tag=f"{tag}_pres{qt}_{mg}_{ri}")
            P.op("sync", lambda e, qt=qt, mg=mg: e.dma_start(
                out=out_dram.rearrange("(c p) t -> p c t", p=128)[
                    :, mg * 2:(mg + 1) * 2, qt * 512:(qt + 1) * 512],
                in_=ot.ap().rearrange("p (c t) -> p c t", c=2)),
                reads=[b_ot], dma=True, tag=f"{tag}_ost{qt}_{mg}")


def run_attn(q_cores, k_cores, vaug_cores, proj_w, proj_b, nheads, kvheads,
             k_rep, res_cores=None, cache_key="attn0"):
    names = ["ones_row", "ones_col", "ident", "ones_all"]
    kshape = (256, T) if k_rep else (D, T)
    specs = {
        "q": ((D, T), BF16, "ExternalInput"),
        "k": (kshape, BF16, "ExternalInput"),
        "vaug": ((T, kvheads * 65), BF16, "ExternalInput"),
        "pw": ((D, D), F32, "ExternalInput"),
        "pb": ((128, 8), F32, "ExternalInput"),
        "out": ((D, T), F32, "ExternalOutput"),
        **const_specs(names),
    }
    if res_cores is not None:
        specs["res0"] = ((D, T), F32, "ExternalInput")

    def ph(ctx, dr):
        emit_attn(ctx, dr["q"], dr["k"], k_rep, dr["vaug"], nheads, kvheads,
                  dr["pw"], dr["pb"], dr["out"],
                  res_drams=[dr["res0"]] if res_cores is not None else (),
                  tag="at")

    base = dict(pw=np.ascontiguousarray(proj_w), pb=_tile_bias(proj_b),
                **const_inmaps(names))
    im = []
    for ci in range(N_CORES):
        m = dict(q=q_cores[ci], k=k_cores[ci], vaug=vaug_cores[ci], **base)
        if res_cores is not None:
            m["res0"] = res_cores[ci]
        im.append(m)
    res = _run(cache_key, specs, names, [ph], im)
    return [r["out"] for r in res]


def emit_add(ctx, in_drams, out_dram, out_dtype=F32, tag="add"):
    nc, P, a = ctx.nc, ctx.P, ctx.alloc
    ts = [a.sb(f"{tag}_t{i}", [128, DCH * 512], F32) for i in range(2)]
    acc = a.sb(f"{tag}_acc", [128, DCH * 512], out_dtype)
    b_acc = ctx.buf(f"{tag}_acc")
    for n in range(NT):
        for ii, ind in enumerate(in_drams):
            i = ii % 2
            b_t = ctx.buf(f"{tag}_t{i}")
            P.op("sync", lambda e, n=n, ind=ind, i=i: e.dma_start(
                out=ts[i].ap().rearrange("p (c t) -> p c t", c=DCH),
                in_=ind.rearrange("(c p) t -> p c t", p=128)[:, :, n * 512:(n + 1) * 512]),
                writes=[b_t], dma=True, tag=f"{tag}_ld{n}_{ii}")
            if ii == 0:
                continue
            if ii == 1:
                P.op("vector", lambda e: e.tensor_add(acc[:], ts[0][:], ts[1][:]),
                     reads=[ctx.buf(f"{tag}_t0"), ctx.buf(f"{tag}_t1")],
                     writes=[b_acc], tag=f"{tag}_a{n}_{ii}")
            else:
                P.op("vector", lambda e, i=i: e.tensor_add(acc[:], acc[:], ts[i][:]),
                     reads=[b_acc, b_t], writes=[b_acc], tag=f"{tag}_a{n}_{ii}")
        P.op("sync", lambda e, n=n: e.dma_start(
            out=out_dram.rearrange("(c p) t -> p c t", p=128)[:, :, n * 512:(n + 1) * 512],
            in_=acc.ap().rearrange("p (c t) -> p c t", c=DCH)),
            reads=[b_acc], dma=True, tag=f"{tag}_st{n}")


def emit_ffn0_route(ctx, xn_dram, rwT_dram, cb_dram, cmb_dram, tag="rt"):
    nc, P, a = ctx.nc, ctx.P, ctx.alloc
    rw_sb = a.sb(f"{tag}_rw", [128, DCH * 16], BF16)
    rw32 = a.sb(f"{tag}_rw32", [128, DCH * 16], F32)
    cbr = a.sb(f"{tag}_cbr", [1, 16], F32)
    cbb = a.sb(f"{tag}_cbb", [128, 16], F32)
    xn_t = a.sb(f"{tag}_xn", [128, DCH * 512], BF16)
    pr_sb = a.sb(f"{tag}_pr", [128, 16], F32)
    sbv = a.sb(f"{tag}_sbv", [128, 16], F32)
    scratch = a.sb(f"{tag}_scr", [128, 8], F32)
    scratch2 = a.sb(f"{tag}_sc2", [128, 8], F32)
    pairbuf = a.sb(f"{tag}_pb", [128, 24], F32)
    gsc = a.sb(f"{tag}_gsc", [128, 4], F32)
    gmask = a.sb(f"{tag}_gm", [128, 4], F32)
    sel = a.sb(f"{tag}_sel", [128, 16], F32)
    den = a.sb(f"{tag}_den", [128, 1], F32)
    cmb = a.sb(f"{tag}_cmb", [128, 16], BF16)
    cemT = a.sb(f"{tag}_cemT", [16, 512], BF16)
    p_lg = a.ps(f"{tag}_plg", [128, 16], F32)
    p_tr = a.ps(f"{tag}_ptr", [16, 128], BF16)
    bufs = {n: ctx.buf(f"{tag}_{n}") for n in
            ("rw", "rw32", "cbr", "cbb", "xn", "pr", "sbv", "scr", "sc2",
             "pb", "gsc", "gm", "sel", "den", "cmb", "cemT", "plg", "ptr")}
    identb = ctx.cb16["ident"]

    P.op("sync", lambda e: e.dma_start(
        out=rw32.ap().rearrange("p (c o) -> p c o", c=DCH),
        in_=rwT_dram.rearrange("(c p) o -> p c o", p=128)),
        writes=[bufs["rw32"]], dma=True, tag=f"{tag}_rwld")
    P.op("vector", lambda e: e.tensor_copy(rw_sb[:], rw32[:]),
         reads=[bufs["rw32"]], writes=[bufs["rw"]], tag=f"{tag}_rwc")
    P.op("sync", lambda e: e.dma_start(out=cbr[:], in_=cb_dram),
         writes=[bufs["cbr"]], dma=True, tag=f"{tag}_cbld")
    P.op("tensor", lambda e: e.matmul(
        p_lg[:], ctx.cf32["ones_row"], cbr[:], start=True, stop=True),
        reads=[bufs["cbr"], ctx.cbuf], writes=[bufs["plg"]], tag=f"{tag}_cbbm")
    P.op("scalar", lambda e: e.copy(cbb[:], p_lg[:]),
         reads=[bufs["plg"]], writes=[bufs["cbb"]], tag=f"{tag}_cbbe")

    for n in range(NT):
        P.op("sync", lambda e, n=n: e.dma_start(
            out=xn_t.ap().rearrange("p (c t) -> p c t", c=DCH),
            in_=xn_dram.rearrange("(c p) t -> p c t", p=128)[:, :, n * 512:(n + 1) * 512]),
            writes=[bufs["xn"]], dma=True, tag=f"{tag}_xnld{n}")
        for ms in range(4):
            for k in range(DCH):
                P.op("tensor", lambda e, ms=ms, k=k: e.matmul(
                    p_lg[:], xn_t[:, k * 512 + ms * 128: k * 512 + (ms + 1) * 128],
                    rw_sb[:, k * 16:(k + 1) * 16],
                    start=(k == 0), stop=(k == DCH - 1)),
                    reads=[bufs["xn"], bufs["rw"]], writes=[bufs["plg"]],
                    tag=f"{tag}_lmm{n}_{ms}_{k}")
            P.op("scalar", lambda e: e.activation(
                out=pr_sb[:], in_=p_lg[:], func=AF.Sigmoid),
                reads=[bufs["plg"]], writes=[bufs["pr"]], tag=f"{tag}_sig{n}_{ms}")
            P.op("vector", lambda e: e.tensor_add(sbv[:], pr_sb[:], cbb[:]),
                 reads=[bufs["pr"], bufs["cbb"]], writes=[bufs["sbv"]],
                 tag=f"{tag}_sb{n}_{ms}")
            def lane(ii):
                return sbv.ap().rearrange("p (g i) -> p i g", i=4)[
                    :, ii:ii + 1, :].rearrange("p i g -> p (i g)")
            pairs = [(0, 1), (0, 2), (0, 3), (1, 2), (1, 3), (2, 3)]
            for pi, (ii, jj) in enumerate(pairs):
                P.op("vector", lambda e, ii=ii, jj=jj, pi=pi: e.tensor_add(
                    pairbuf[:, pi * 4:(pi + 1) * 4], lane(ii), lane(jj)),
                    reads=[bufs["sbv"]], writes=[bufs["pb"]],
                    tag=f"{tag}_gp{n}_{ms}_{pi}")
            for pi in range(1, 6):
                dst = gsc[:] if pi == 5 else pairbuf[:, 0:4]
                bw = bufs["gsc"] if pi == 5 else bufs["pb"]
                P.op("vector", lambda e, pi=pi, dst=dst: e.tensor_max(
                    dst, pairbuf[:, 0:4], pairbuf[:, pi * 4:(pi + 1) * 4]),
                    reads=[bufs["pb"]], writes=[bw],
                    tag=f"{tag}_gmx{n}_{ms}_{pi}")
            P.op("vector", lambda e: e.memset(scratch[:], -1e30),
                 writes=[bufs["scr"]], tag=f"{tag}_gpad{n}_{ms}")
            P.op("vector", lambda e: e.tensor_copy(scratch[:, 0:4], gsc[:]),
                 reads=[bufs["gsc"]], writes=[bufs["scr"]],
                 tag=f"{tag}_gcp{n}_{ms}")
            P.op("vector", lambda e: e.max(out=scratch2[:], in_=scratch[:]),
                 reads=[bufs["scr"]], writes=[bufs["sc2"]],
                 tag=f"{tag}_g2{n}_{ms}")
            P.op("vector", lambda e: e.tensor_single_scalar(
                gmask[:], gsc[:], scratch2[:, 1:2], ALU.is_ge),
                reads=[bufs["gsc"], bufs["sc2"]], writes=[bufs["gm"]],
                tag=f"{tag}_gmk{n}_{ms}")
            for g in range(4):
                P.op("vector", lambda e, g=g: e.tensor_copy(
                    sel[:, g * 4:(g + 1) * 4],
                    gmask[:, g:g + 1].to_broadcast([128, 4])),
                    reads=[bufs["gm"]], writes=[bufs["sel"]],
                    tag=f"{tag}_gex{n}_{ms}_{g}")
            P.op("vector", lambda e: e.scalar_tensor_tensor(
                sbv[:], sbv[:], 100.0, sel[:], ALU.add, ALU.mult),
                reads=[bufs["sbv"], bufs["sel"]], writes=[bufs["sbv"]],
                tag=f"{tag}_msk{n}_{ms}")
            P.op("vector", lambda e: e.tensor_scalar_add(sbv[:], sbv[:], -100.0),
                 reads=[bufs["sbv"]], writes=[bufs["sbv"]],
                 tag=f"{tag}_msk2{n}_{ms}")
            P.op("vector", lambda e: e.max(out=scratch2[:], in_=sbv[:]),
                 reads=[bufs["sbv"]], writes=[bufs["sc2"]],
                 tag=f"{tag}_t8{n}_{ms}")
            P.op("vector", lambda e: e.tensor_single_scalar(
                sel[:], sbv[:], scratch2[:, 3:4], ALU.is_ge),
                reads=[bufs["sbv"], bufs["sc2"]], writes=[bufs["sel"]],
                tag=f"{tag}_sel{n}_{ms}")
            P.op("vector", lambda e: e.tensor_mul(sel[:], sel[:], pr_sb[:]),
                 reads=[bufs["sel"], bufs["pr"]], writes=[bufs["sel"]],
                 tag=f"{tag}_w{n}_{ms}")
            P.op("vector", lambda e: e.reduce_sum(den[:], sel[:], axis=AX.X),
                 reads=[bufs["sel"]], writes=[bufs["den"]],
                 tag=f"{tag}_den{n}_{ms}")
            P.op("vector", lambda e: e.tensor_scalar_add(den[:], den[:], 1e-20),
                 reads=[bufs["den"]], writes=[bufs["den"]],
                 tag=f"{tag}_dene{n}_{ms}")
            P.op("vector", lambda e: e.reciprocal(den[:], den[:]),
                 reads=[bufs["den"]], writes=[bufs["den"]],
                 tag=f"{tag}_denr{n}_{ms}")
            P.op("vector", lambda e: e.tensor_scalar_mul(cmb[:], sel[:], den[:]),
                 reads=[bufs["sel"], bufs["den"]], writes=[bufs["cmb"]],
                 tag=f"{tag}_cmb{n}_{ms}")
            P.op("tensor", lambda e: e.transpose(p_tr[:], cmb[:], identb),
                 reads=[bufs["cmb"], ctx.cbuf], writes=[bufs["ptr"]],
                 tag=f"{tag}_tr{n}_{ms}")
            P.op("scalar", lambda e, ms=ms: e.copy(
                cemT[:, ms * 128:(ms + 1) * 128], p_tr[:]),
                reads=[bufs["ptr"]], writes=[bufs["cemT"]],
                tag=f"{tag}_tre{n}_{ms}")
        P.op("sync", lambda e, n=n: e.dma_start(
            out=cmb_dram[:, n * 512:(n + 1) * 512], in_=cemT[:]),
            reads=[bufs["cemT"]], dma=True, tag=f"{tag}_st{n}")


def emit_ffn0_experts(ctx, xn_dram, cmb_dram, gate_dram, up_dram, down_dram,
                      experts, out_dram, accumulate, tag="ex"):
    nc, P, a = ctx.nc, ctx.P, ctx.alloc
    ne = len(experts)
    gw = a.sb(f"{tag}_gw", [128, ne * DCH * 256], BF16)
    uw = a.sb(f"{tag}_uw", [128, ne * DCH * 256], BF16)
    dw = a.sb(f"{tag}_dw", [128, ne * 2 * 1024], BF16)
    stage = a.sb(f"{tag}_st0", [128, 2048], F32)
    stb = ctx.buf(f"{tag}_st0")
    xn_t = a.sb(f"{tag}_xn", [128, DCH * 512], BF16)
    cem_t = a.sb(f"{tag}_cem", [16, 512], BF16)
    cbc = a.sb(f"{tag}_cbc", [128, 512], BF16)
    sg = a.sb(f"{tag}_sg", [128, 512], F32)
    h_all = a.sb(f"{tag}_h", [128, ne * 2 * 512], BF16)
    rout = a.sb(f"{tag}_ro", [128, 2 * 512], F32)
    p_gu = [a.ps(f"{tag}_pgu{i}", [128, 512], F32) for i in range(4)]
    p_d = [a.ps(f"{tag}_pd{i}", [128, 512], F32) for i in range(3)]
    p_cb = a.ps(f"{tag}_pcb", [128, 512], F32)
    b = {n: ctx.buf(f"{tag}_{n}") for n in
         ("gw", "uw", "dw", "xn", "cem", "cbc", "sg", "h", "ro", "pcb")}
    bgu = [ctx.buf(f"{tag}_pgu{i}") for i in range(4)]
    bd = [ctx.buf(f"{tag}_pd{i}") for i in range(3)]
    sel16 = ctx.cb16["sel16"]

    for ei, ex in enumerate(experts):
        for k in range(DCH):
            P.op("sync", lambda e, ex=ex, k=k: e.dma_start(
                out=stage[:, 0:256],
                in_=gate_dram[ex * 1024 + k * 128: ex * 1024 + (k + 1) * 128, :]),
                writes=[stb], dma=True, tag=f"{tag}_gld{ei}_{k}")
            P.op("vector", lambda e, ei=ei, k=k: e.tensor_copy(
                gw[:, (ei * DCH + k) * 256:(ei * DCH + k + 1) * 256],
                stage[:, 0:256]),
                reads=[stb], writes=[b["gw"]], tag=f"{tag}_gc{ei}_{k}")
            P.op("sync", lambda e, ex=ex, k=k: e.dma_start(
                out=stage[:, 256:512],
                in_=up_dram[ex * 1024 + k * 128: ex * 1024 + (k + 1) * 128, :]),
                writes=[stb], dma=True, tag=f"{tag}_uld{ei}_{k}")
            P.op("vector", lambda e, ei=ei, k=k: e.tensor_copy(
                uw[:, (ei * DCH + k) * 256:(ei * DCH + k + 1) * 256],
                stage[:, 256:512]),
                reads=[stb], writes=[b["uw"]], tag=f"{tag}_uc{ei}_{k}")
        for k2 in range(2):
            P.op("sync", lambda e, ex=ex, k2=k2: e.dma_start(
                out=stage[:, 1024:2048],
                in_=down_dram[ex * 256 + k2 * 128: ex * 256 + (k2 + 1) * 128, :]),
                writes=[stb], dma=True, tag=f"{tag}_dld{ei}_{k2}")
            P.op("vector", lambda e, ei=ei, k2=k2: e.tensor_copy(
                dw[:, (ei * 2 + k2) * 1024:(ei * 2 + k2 + 1) * 1024],
                stage[:, 1024:2048]),
                reads=[stb], writes=[b["dw"]], tag=f"{tag}_dc{ei}_{k2}")

    for n in range(NT):
        P.op("sync", lambda e, n=n: e.dma_start(
            out=xn_t.ap().rearrange("p (c t) -> p c t", c=DCH),
            in_=xn_dram.rearrange("(c p) t -> p c t", p=128)[:, :, n * 512:(n + 1) * 512]),
            writes=[b["xn"]], dma=True, tag=f"{tag}_xnld{n}")
        P.op("sync", lambda e, n=n: e.dma_start(
            out=cem_t[:], in_=cmb_dram[:, n * 512:(n + 1) * 512]),
            writes=[b["cem"]], dma=True, tag=f"{tag}_cmld{n}")
        for ei, ex in enumerate(experts):
            P.op("tensor", lambda e, ex=ex: e.matmul(
                p_cb[:], sel16[:, ex * 128:(ex + 1) * 128], cem_t[:],
                start=True, stop=True),
                reads=[b["cem"], ctx.cbuf], writes=[b["pcb"]],
                tag=f"{tag}_cbm{n}_{ei}")
            P.op("scalar", lambda e: e.copy(cbc[:], p_cb[:]),
                 reads=[b["pcb"]], writes=[b["cbc"]], tag=f"{tag}_cbe{n}_{ei}")
            for m in range(2):
                bkg, bku = m, 2 + m
                for k in range(DCH):
                    P.op("tensor", lambda e, ei=ei, m=m, k=k, bkg=bkg: e.matmul(
                        p_gu[bkg][:],
                        gw[:, (ei * DCH + k) * 256 + m * 128: (ei * DCH + k) * 256 + (m + 1) * 128],
                        xn_t[:, k * 512:(k + 1) * 512],
                        start=(k == 0), stop=(k == DCH - 1)),
                        reads=[b["gw"], b["xn"]], writes=[bgu[bkg]],
                        tag=f"{tag}_gmm{n}_{ei}_{m}_{k}")
                    P.op("tensor", lambda e, ei=ei, m=m, k=k, bku=bku: e.matmul(
                        p_gu[bku][:],
                        uw[:, (ei * DCH + k) * 256 + m * 128: (ei * DCH + k) * 256 + (m + 1) * 128],
                        xn_t[:, k * 512:(k + 1) * 512],
                        start=(k == 0), stop=(k == DCH - 1)),
                        reads=[b["uw"], b["xn"]], writes=[bgu[bku]],
                        tag=f"{tag}_umm{n}_{ei}_{m}_{k}")
                P.op("scalar", lambda e, bkg=bkg: e.activation(
                    out=sg[:], in_=p_gu[bkg][:], func=AF.Silu),
                    reads=[bgu[bkg]], writes=[b["sg"]],
                    tag=f"{tag}_silu{n}_{ei}_{m}")
                P.op("vector", lambda e, ei=ei, m=m, bku=bku: e.scalar_tensor_tensor(
                    h_all[:, (ei * 2 + m) * 512:(ei * 2 + m + 1) * 512],
                    p_gu[bku][:], 1.0, sg[:], ALU.mult, ALU.mult),
                    reads=[bgu[bku], b["sg"]], writes=[b["h"]],
                    tag=f"{tag}_hmul{n}_{ei}_{m}")
                P.op("vector", lambda e, ei=ei, m=m: e.tensor_mul(
                    h_all[:, (ei * 2 + m) * 512:(ei * 2 + m + 1) * 512],
                    h_all[:, (ei * 2 + m) * 512:(ei * 2 + m + 1) * 512], cbc[:]),
                    reads=[b["h"], b["cbc"]], writes=[b["h"]],
                    tag=f"{tag}_hcb{n}_{ei}_{m}")
        for mo in range(DCH):
            bk = mo % 3
            nacc = ne * 2
            jj = 0
            for ei in range(ne):
                for k2 in range(2):
                    P.op("tensor", lambda e, ei=ei, k2=k2, mo=mo, bk=bk, jj=jj: e.matmul(
                        p_d[bk][:],
                        dw[:, (ei * 2 + k2) * 1024 + mo * 128: (ei * 2 + k2) * 1024 + (mo + 1) * 128],
                        h_all[:, (ei * 2 + k2) * 512:(ei * 2 + k2 + 1) * 512],
                        start=(jj == 0), stop=(jj == nacc - 1)),
                        reads=[b["dw"], b["h"]], writes=[bd[bk]],
                        tag=f"{tag}_dmm{n}_{mo}_{ei}_{k2}")
                    jj += 1
            P.op("scalar", lambda e, mo=mo, bk=bk: e.copy(
                rout[:, (mo % 2) * 512:(mo % 2 + 1) * 512], p_d[bk][:]),
                reads=[bd[bk]], writes=[b["ro"]], tag=f"{tag}_dev{n}_{mo}")
            if mo % 2 == 1:
                q_ = "gpsimd" if accumulate else "sync"
                P.op(q_, lambda e, n=n, mo=mo: e.dma_start(
                    out=out_dram.rearrange("(c p) t -> p c t", p=128)[
                        :, mo - 1:mo + 1, n * 512:(n + 1) * 512],
                    in_=rout.ap().rearrange("p (c t) -> p c t", c=2),
                    **(dict(accum_op=ALU.add) if accumulate else {})),
                    reads=[b["ro"]], dma=True, tag=f"{tag}_ost{n}_{mo}")



def run_linear(x_cores, w, b, mode, din, dout, act=None, out_dtype=BF16,
               b_row=None, x2_cores=None, res_cores=None, aug=False,
               cache_key="lin"):
    names = ["ones_row", "ones_col"]
    if mode == "fm":
        oshape = (dout, T)
    else:
        nh = dout // 64
        oshape = (T, nh * 65) if aug else (T, dout)
    specs = {
        "x": ((din, T), BF16, "ExternalInput"),
        "w": ((din, dout), F32, "ExternalInput"),
        "out": (oshape, out_dtype, "ExternalOutput"),
        **const_specs(names),
    }
    if b is not None:
        specs["b"] = ((128, dout // 128), F32, "ExternalInput")
    if b_row is not None:
        specs["brow"] = ((1, dout), F32, "ExternalInput")
    if x2_cores is not None:
        specs["x2"] = ((din, T), BF16, "ExternalInput")
    if res_cores:
        for ri in range(len(res_cores)):
            specs[f"res{ri}"] = ((D, T), F32, "ExternalInput")

    def ph(ctx, dr):
        emit_linear(ctx, dr["x"], dr["w"], dr.get("b"), mode, dr["out"],
                    din, dout, act=act, out_dtype=out_dtype,
                    b_row_dram=dr.get("brow"), x2_dram=dr.get("x2"),
                    res_drams=[dr[f"res{ri}"] for ri in range(len(res_cores))]
                    if res_cores else (), aug=aug, tag="li")

    base = dict(w=np.ascontiguousarray(w), **const_inmaps(names))
    if b is not None:
        base["b"] = _tile_bias(b)
    if b_row is not None:
        base["brow"] = np.ascontiguousarray(b_row.reshape(1, dout))
    im = []
    for ci in range(N_CORES):
        m = dict(x=x_cores[ci], **base)
        if x2_cores is not None:
            m["x2"] = x2_cores[ci]
        if res_cores:
            for ri in range(len(res_cores)):
                m[f"res{ri}"] = res_cores[ri][ci]
        im.append(m)
    res = _run(cache_key, specs, names, [ph], im)
    return [r["out"] for r in res]


def run_ffn0_route(xn_cores, router_w, correction_bias):
    names = ["ones_row", "ident"]
    specs = {
        "xn": ((D, T), BF16, "ExternalInput"),
        "rwT": ((D, 16), F32, "ExternalInput"),
        "cb": ((1, 16), F32, "ExternalInput"),
        "cmb": ((16, T), BF16, "ExternalOutput"),
        **const_specs(names),
    }

    def ph(ctx, dr):
        emit_ffn0_route(ctx, dr["xn"], dr["rwT"], dr["cb"], dr["cmb"], tag="rt")

    base = dict(rwT=_t(router_w), cb=correction_bias.reshape(1, 16),
                **const_inmaps(names))
    im = [dict(xn=xn, **base) for xn in xn_cores]
    res = _run("ffn0route", specs, names, [ph], im)
    return [r["cmb"] for r in res]


def run_ffn0_experts(xn_cores, cmb_cores, gate_w, up_w, down_w, experts,
                     cache_key):
    names = ["sel16"]
    specs = {
        "xn": ((D, T), BF16, "ExternalInput"),
        "cmb": ((16, T), BF16, "ExternalInput"),
        "gate": ((16 * 1024, 256), F32, "ExternalInput"),
        "up": ((16 * 1024, 256), F32, "ExternalInput"),
        "down": ((16 * 256, 1024), F32, "ExternalInput"),
        "out": ((D, T), F32, "ExternalOutput"),
        **const_specs(names),
    }

    def ph(ctx, dr):
        emit_ffn0_experts(ctx, dr["xn"], dr["cmb"], dr["gate"], dr["up"],
                          dr["down"], experts, dr["out"], False, tag="ex")

    base = dict(gate=gate_w.reshape(16 * 1024, 256),
                up=up_w.reshape(16 * 1024, 256),
                down=down_w.reshape(16 * 256, 1024), **const_inmaps(names))
    im = [dict(xn=xn, cmb=cm, **base) for xn, cm in zip(xn_cores, cmb_cores)]
    res = _run(cache_key, specs, names, [ph], im)
    return [r["out"] for r in res]


def run_add(in_cores_list, cache_key, out_dtype=F32):
    names = []
    specs = {"out": ((D, T), out_dtype, "ExternalOutput")}
    ni = len(in_cores_list)
    for ii in range(ni):
        specs[f"in{ii}"] = ((D, T), F32, "ExternalInput")

    def ph(ctx, dr):
        emit_add(ctx, [dr[f"in{ii}"] for ii in range(ni)], dr["out"],
                 out_dtype=out_dtype, tag="add")

    im = []
    for ci in range(N_CORES):
        im.append({f"in{ii}": in_cores_list[ii][ci] for ii in range(ni)})
    res = _run(cache_key, specs, names, [ph], im)
    return [r["out"] for r in res]


def emit_linscan(ctx, pq_dram, pk_dram, pkt_dram, vaug_dram, y_dram,
                 tag="ls"):
    nc, P, a = ctx.nc, ctx.P, ctx.alloc
    pq_sb = [a.sb(f"{tag}_pq{j}", [64, T], BF16) for j in range(2)]
    pk_sb = [a.sb(f"{tag}_pk{j}", [64, T], BF16) for j in range(2)]
    pkt_sb = [a.sb(f"{tag}_pkt{j}", [64, 64 * 64], BF16) for j in range(2)]
    va_sb = [a.sb(f"{tag}_va{j}", [64, 64 * 65], BF16) for j in range(2)]
    at_sb = [a.sb(f"{tag}_at{j}", [64, 64], BF16) for j in range(2)]
    s32 = [a.sb(f"{tag}_s32{j}", [64, 65], F32) for j in range(2)]
    s_bf = [a.sb(f"{tag}_sbf{j}", [64, 65], BF16) for j in range(2)]
    yt_sb = [a.sb(f"{tag}_yt{j}", [64, T], F32) for j in range(2)]
    yo_sb = a.sb(f"{tag}_yo", [64, T], BF16)
    den_sb = a.sb(f"{tag}_den", [1, 2 * T], F32)
    rbc_sb = a.sb(f"{tag}_rbc", [64, 512], F32)
    p_a = [a.ps(f"{tag}_pa{j}", [64, 64], F32) for j in range(2)]
    p_k = [a.ps(f"{tag}_pkk{j}", [64, 65], F32) for j in range(2)]
    p_y = [a.ps(f"{tag}_py{j}", [65, 64], F32) for j in range(2)]
    p_bc = a.ps(f"{tag}_pbc", [64, 512], F32)
    bq = [ctx.buf(f"{tag}_pq{j}") for j in range(2)]
    bkf = [ctx.buf(f"{tag}_pk{j}") for j in range(2)]
    bkt = [ctx.buf(f"{tag}_pkt{j}") for j in range(2)]
    bva = [ctx.buf(f"{tag}_va{j}") for j in range(2)]
    bat = [ctx.buf(f"{tag}_at{j}") for j in range(2)]
    bs32 = [ctx.buf(f"{tag}_s32{j}") for j in range(2)]
    bsbf = [ctx.buf(f"{tag}_sbf{j}") for j in range(2)]
    byt = [ctx.buf(f"{tag}_yt{j}") for j in range(2)]
    byo = ctx.buf(f"{tag}_yo")
    bden = ctx.buf(f"{tag}_den")
    brbc = ctx.buf(f"{tag}_rbc")
    ba = [ctx.buf(f"{tag}_pa{j}") for j in range(2)]
    bk = [ctx.buf(f"{tag}_pkk{j}") for j in range(2)]
    by = [ctx.buf(f"{tag}_py{j}") for j in range(2)]
    bbc = ctx.buf(f"{tag}_pbc")
    mask = ctx.cb16["triu_mask"]

    for hp in range(8):
        for hh in range(2):
            h = 2 * hp + hh
            P.op("sync", lambda e, h=h, hh=hh: e.dma_start(
                out=pq_sb[hh][:], in_=pq_dram[h * 64:(h + 1) * 64, :]),
                writes=[bq[hh]], dma=True, tag=f"{tag}_pqld{h}")
            P.op("sync", lambda e, h=h, hh=hh: e.dma_start(
                out=pk_sb[hh][:], in_=pk_dram[h * 64:(h + 1) * 64, :]),
                writes=[bkf[hh]], dma=True, tag=f"{tag}_pkld{h}")
            P.op("sync", lambda e, h=h, hh=hh: e.dma_start(
                out=pkt_sb[hh].ap().rearrange("p (n d) -> p n d", d=64),
                in_=pkt_dram[:, h * 64:(h + 1) * 64].rearrange(
                    "(n p) d -> p n d", p=64)),
                writes=[bkt[hh]], dma=True, tag=f"{tag}_pktld{h}")
            P.op("sync", lambda e, h=h, hh=hh: e.dma_start(
                out=va_sb[hh].ap().rearrange("p (n d) -> p n d", d=65),
                in_=vaug_dram[:, h * 65:(h + 1) * 65].rearrange(
                    "(n p) d -> p n d", p=64)),
                writes=[bva[hh]], dma=True, tag=f"{tag}_vald{h}")
            P.op("vector", lambda e, hh=hh: e.memset(s32[hh][:], 0.0),
                 writes=[bs32[hh]], tag=f"{tag}_s0_{hp}_{hh}")
            P.op("vector", lambda e, hh=hh: e.memset(s_bf[hh][:], 0.0),
                 writes=[bsbf[hh]], tag=f"{tag}_sb0_{hp}_{hh}")
        for n in range(64):
            for hh in range(2):
                P.op("tensor", lambda e, hh=hh, n=n: e.matmul(
                    p_a[hh][:], pk_sb[hh][:, n * 64:(n + 1) * 64],
                    pq_sb[hh][:, n * 64:(n + 1) * 64],
                    start=True, stop=True),
                    reads=[bkf[hh], bq[hh]], writes=[ba[hh]],
                    tag=f"{tag}_amm{hp}_{n}_{hh}")
                P.op("vector", lambda e, hh=hh: e.scalar_tensor_tensor(
                    at_sb[hh][:], p_a[hh][:], 1.0, mask, ALU.mult, ALU.mult),
                    reads=[ba[hh], ctx.cbuf], writes=[bat[hh]],
                    tag=f"{tag}_amsk{hp}_{n}_{hh}")
                P.op("tensor", lambda e, hh=hh, n=n: e.matmul(
                    p_y[hh][:], s_bf[hh][:], pq_sb[hh][:, n * 64:(n + 1) * 64],
                    start=True, stop=False),
                    reads=[bsbf[hh], bq[hh]], writes=[by[hh]],
                    tag=f"{tag}_imm{hp}_{n}_{hh}")
                P.op("tensor", lambda e, hh=hh, n=n: e.matmul(
                    p_y[hh][:], va_sb[hh][:, n * 65:(n + 1) * 65],
                    at_sb[hh][:], start=False, stop=True),
                    reads=[bva[hh], bat[hh]], writes=[by[hh]],
                    tag=f"{tag}_tmm{hp}_{n}_{hh}")
                P.op("tensor", lambda e, hh=hh, n=n: e.matmul(
                    p_k[hh][:], pkt_sb[hh][:, n * 64:(n + 1) * 64],
                    va_sb[hh][:, n * 65:(n + 1) * 65],
                    start=True, stop=True),
                    reads=[bkt[hh], bva[hh]], writes=[bk[hh]],
                    tag=f"{tag}_kmm{hp}_{n}_{hh}")
                P.op("vector", lambda e, hh=hh: e.tensor_add(
                    s32[hh][:], s32[hh][:], p_k[hh][:]),
                    reads=[bs32[hh], bk[hh]], writes=[bs32[hh]],
                    tag=f"{tag}_sup{hp}_{n}_{hh}")
                P.op("vector", lambda e, hh=hh: e.tensor_copy(
                    s_bf[hh][:], s32[hh][:]),
                    reads=[bs32[hh]], writes=[bsbf[hh]],
                    tag=f"{tag}_scp{hp}_{n}_{hh}")
                P.op("scalar", lambda e, hh=hh, n=n: e.copy(
                    yt_sb[hh][:, n * 64:(n + 1) * 64], p_y[hh].ap()[0:64, :]),
                    reads=[by[hh]], writes=[byt[hh]],
                    tag=f"{tag}_yev{hp}_{n}_{hh}")
                P.op("scalar", lambda e, hh=hh, n=n: e.copy(
                    den_sb[0:1, hh * T + n * 64: hh * T + (n + 1) * 64],
                    p_y[hh].ap()[64:65, :]),
                    reads=[by[hh]], writes=[bden],
                    tag=f"{tag}_dev{hp}_{n}_{hh}")
        P.op("vector", lambda e: e.tensor_scalar_max(den_sb[:], den_sb[:], 1e-6),
             reads=[bden], writes=[bden], tag=f"{tag}_dcl{hp}")
        P.op("vector", lambda e: e.reciprocal(den_sb[:], den_sb[:]),
             reads=[bden], writes=[bden], tag=f"{tag}_drc{hp}")
        for hh in range(2):
            h = 2 * hp + hh
            for tt in range(8):
                P.op("tensor", lambda e, hh=hh, tt=tt: e.matmul(
                    p_bc[:], ctx.cf32["ones_row"][0:1, 0:64],
                    den_sb[0:1, hh * T + tt * 512: hh * T + (tt + 1) * 512],
                    start=True, stop=True),
                    reads=[bden, ctx.cbuf], writes=[bbc],
                    tag=f"{tag}_nbc{hp}_{hh}_{tt}")
                P.op("scalar", lambda e: e.copy(rbc_sb[:], p_bc[:]),
                     reads=[bbc], writes=[brbc], tag=f"{tag}_nbe{hp}_{hh}_{tt}")
                P.op("vector", lambda e, hh=hh, tt=tt: e.tensor_mul(
                    yo_sb[:, tt * 512:(tt + 1) * 512],
                    yt_sb[hh][:, tt * 512:(tt + 1) * 512], rbc_sb[:]),
                    reads=[byt[hh], brbc], writes=[byo],
                    tag=f"{tag}_nml{hp}_{hh}_{tt}")
            P.op("sync", lambda e, h=h: e.dma_start(
                out=y_dram[h * 64:(h + 1) * 64, :], in_=yo_sb[:]),
                reads=[byo], dma=True, tag=f"{tag}_yst{h}")


CAPACITY1 = 10240.0


def emit_moe1_route(ctx, s2_dram, gwT_dram, cmb_dram, rank_dram, counts_dram,
                    tag="m1r"):
    nc, P, a = ctx.nc, ctx.P, ctx.alloc
    gw_sb = a.sb(f"{tag}_gw", [128, DCH * 8], BF16)
    gw32 = a.sb(f"{tag}_gw32", [128, DCH * 8], F32)
    xn_t = a.sb(f"{tag}_xn", [128, DCH * 512], BF16)
    pr = a.sb(f"{tag}_pr", [128, 8], F32)
    top = a.sb(f"{tag}_top", [128, 8], F32)
    sel = a.sb(f"{tag}_sel", [128, 8], F32)
    cnt = a.sb(f"{tag}_cnt", [128, 8], BF16)
    den = a.sb(f"{tag}_den", [128, 1], F32)
    cmb_t = a.sb(f"{tag}_cmb", [128, 4 * 8], F32)
    rank_t = a.sb(f"{tag}_rank", [128, 4 * 8], F32)
    offs = a.sb(f"{tag}_offs", [1, 8], F32)
    offb = a.sb(f"{tag}_offb", [128, 8], F32)
    tot = a.sb(f"{tag}_tot", [1, 8], F32)
    p_lg = a.ps(f"{tag}_plg", [128, 8], F32)
    p_pf = a.ps(f"{tag}_ppf", [128, 8], F32)
    p_t = a.ps(f"{tag}_pt", [1, 8], F32)
    p_ob = a.ps(f"{tag}_pob", [128, 8], F32)
    b = {n: ctx.buf(f"{tag}_{n}") for n in
         ("gw", "gw32", "xn", "pr", "top", "sel", "cnt", "den", "cmb",
          "rank", "offs", "offb", "tot", "plg", "ppf", "pt", "pob")}
    striu = ctx.cb16["striu"]
    ones_col = ctx.cb16["ones_col"]

    P.op("sync", lambda e: e.dma_start(
        out=gw32.ap().rearrange("p (c o) -> p c o", c=DCH),
        in_=gwT_dram.rearrange("(c p) o -> p c o", p=128)),
        writes=[b["gw32"]], dma=True, tag=f"{tag}_gld")
    P.op("vector", lambda e: e.tensor_copy(gw_sb[:], gw32[:]),
         reads=[b["gw32"]], writes=[b["gw"]], tag=f"{tag}_gc")
    P.op("vector", lambda e: e.memset(offs[:], 0.0),
         writes=[b["offs"]], tag=f"{tag}_o0")

    for n in range(NT):
        P.op("sync", lambda e, n=n: e.dma_start(
            out=xn_t.ap().rearrange("p (c t) -> p c t", c=DCH),
            in_=s2_dram.rearrange("(c p) t -> p c t", p=128)[:, :, n * 512:(n + 1) * 512]),
            writes=[b["xn"]], dma=True, tag=f"{tag}_xld{n}")
        for ms in range(4):
            for k in range(DCH):
                P.op("tensor", lambda e, ms=ms, k=k: e.matmul(
                    p_lg[:], xn_t[:, k * 512 + ms * 128: k * 512 + (ms + 1) * 128],
                    gw_sb[:, k * 8:(k + 1) * 8],
                    start=(k == 0), stop=(k == DCH - 1)),
                    reads=[b["xn"], b["gw"]], writes=[b["plg"]],
                    tag=f"{tag}_lmm{n}_{ms}_{k}")
            P.op("scalar", lambda e: e.activation(
                out=pr[:], in_=p_lg[:], func=AF.Sigmoid),
                reads=[b["plg"]], writes=[b["pr"]], tag=f"{tag}_sig{n}_{ms}")
            P.op("vector", lambda e: e.max(out=top[:], in_=pr[:]),
                 reads=[b["pr"]], writes=[b["top"]], tag=f"{tag}_top{n}_{ms}")
            P.op("vector", lambda e: e.tensor_single_scalar(
                sel[:], pr[:], top[:, 1:2], ALU.is_ge),
                reads=[b["pr"], b["top"]], writes=[b["sel"]],
                tag=f"{tag}_sel{n}_{ms}")
            P.op("vector", lambda e: e.tensor_add(den[:], top[:, 0:1], top[:, 1:2]),
                 reads=[b["top"]], writes=[b["den"]], tag=f"{tag}_dn{n}_{ms}")
            P.op("vector", lambda e: e.tensor_scalar_add(den[:], den[:], 1e-6),
                 reads=[b["den"]], writes=[b["den"]], tag=f"{tag}_dne{n}_{ms}")
            P.op("vector", lambda e: e.reciprocal(den[:], den[:]),
                 reads=[b["den"]], writes=[b["den"]], tag=f"{tag}_dnr{n}_{ms}")
            P.op("vector", lambda e, ms=ms: e.tensor_mul(
                cmb_t[:, ms * 8:(ms + 1) * 8], sel[:], pr[:]),
                reads=[b["sel"], b["pr"]], writes=[b["cmb"]],
                tag=f"{tag}_cw{n}_{ms}")
            P.op("vector", lambda e, ms=ms: e.tensor_scalar_mul(
                cmb_t[:, ms * 8:(ms + 1) * 8], cmb_t[:, ms * 8:(ms + 1) * 8],
                den[:]),
                reads=[b["cmb"], b["den"]], writes=[b["cmb"]],
                tag=f"{tag}_cs{n}_{ms}")
            P.op("vector", lambda e: e.tensor_copy(cnt[:], sel[:]),
                 reads=[b["sel"]], writes=[b["cnt"]], tag=f"{tag}_cb{n}_{ms}")
            P.op("tensor", lambda e: e.matmul(
                p_pf[:], striu, cnt[:], start=True, stop=True),
                reads=[b["cnt"], ctx.cbuf], writes=[b["ppf"]],
                tag=f"{tag}_pf{n}_{ms}")
            P.op("tensor", lambda e: e.matmul(
                p_t[:], ones_col, cnt[:], start=True, stop=True),
                reads=[b["cnt"], ctx.cbuf], writes=[b["pt"]],
                tag=f"{tag}_tt{n}_{ms}")
            P.op("tensor", lambda e: e.matmul(
                p_ob[:], ctx.cf32["ones_row"], offs[:], start=True, stop=True),
                reads=[b["offs"], ctx.cbuf], writes=[b["pob"]],
                tag=f"{tag}_ob{n}_{ms}")
            P.op("vector", lambda e, ms=ms: e.tensor_add(
                rank_t[:, ms * 8:(ms + 1) * 8], p_pf[:], offb[:]) if False
                else e.tensor_copy(offb[:], p_ob[:]),
                reads=[b["pob"]], writes=[b["offb"]], tag=f"{tag}_obe{n}_{ms}")
            P.op("vector", lambda e, ms=ms: e.tensor_add(
                rank_t[:, ms * 8:(ms + 1) * 8], p_pf[:], offb[:]),
                reads=[b["ppf"], b["offb"]], writes=[b["rank"]],
                tag=f"{tag}_rk{n}_{ms}")
            P.op("vector", lambda e: e.tensor_copy(tot[:], p_t[:]),
                 reads=[b["pt"]], writes=[b["tot"]], tag=f"{tag}_te{n}_{ms}")
            P.op("vector", lambda e: e.tensor_add(offs[:], offs[:], tot[:]),
                 reads=[b["offs"], b["tot"]], writes=[b["offs"]],
                 tag=f"{tag}_oa{n}_{ms}")
        P.op("sync", lambda e, n=n: e.dma_start(
            out=cmb_dram[n * 512:(n + 1) * 512, :].rearrange(
                "(ms p) e -> p ms e", p=128),
            in_=cmb_t.ap().rearrange("p (ms e) -> p ms e", e=8)),
            reads=[b["cmb"]], dma=True, tag=f"{tag}_cst{n}")
        P.op("sync", lambda e, n=n: e.dma_start(
            out=rank_dram[n * 512:(n + 1) * 512, :].rearrange(
                "(ms p) e -> p ms e", p=128),
            in_=rank_t.ap().rearrange("p (ms e) -> p ms e", e=8)),
            reads=[b["rank"]], dma=True, tag=f"{tag}_rst{n}")
    P.op("sync", lambda e: e.dma_start(out=counts_dram, in_=offs[:]),
         reads=[b["offs"]], dma=True, tag=f"{tag}_tst")


def emit_moe1_gate(ctx, cmb_dram, rank_dram, counts_all_dram, coremask_dram,
                   gct_dram, tag="m1g"):
    nc, P, a = ctx.nc, ctx.P, ctx.alloc
    ca = a.sb(f"{tag}_ca", [8, 8], F32)
    cm = a.sb(f"{tag}_cm", [8, 1], F32)
    offb = a.sb(f"{tag}_offb", [128, 8], F32)
    cmb_t = a.sb(f"{tag}_cmb", [128, 4 * 8], F32)
    rank_t = a.sb(f"{tag}_rank", [128, 4 * 8], F32)
    g_t = a.sb(f"{tag}_g", [128, 8], BF16)
    gct = a.sb(f"{tag}_gct", [8, 512], BF16)
    p_o = a.ps(f"{tag}_po", [128, 8], F32)
    p_tr = a.ps(f"{tag}_ptr", [8, 128], BF16)
    b = {n: ctx.buf(f"{tag}_{n}") for n in
         ("ca", "cm", "offb", "cmb", "rank", "g", "gct", "po", "ptr")}
    identb = ctx.cb16["ident"]

    P.op("sync", lambda e: e.dma_start(out=ca[:], in_=counts_all_dram),
         writes=[b["ca"]], dma=True, tag=f"{tag}_cald")
    P.op("sync", lambda e: e.dma_start(out=cm[:], in_=coremask_dram),
         writes=[b["cm"]], dma=True, tag=f"{tag}_cmld")
    P.op("tensor", lambda e: e.matmul(
        p_o[0:1, :], cm[:], ca[:], start=True, stop=True),
        reads=[b["ca"], b["cm"]], writes=[b["po"]], tag=f"{tag}_offmm")
    P.op("scalar", lambda e: e.copy(offb[0:1, :], p_o[0:1, :]),
         reads=[b["po"]], writes=[b["offb"]], tag=f"{tag}_offe")
    P.op("tensor", lambda e: e.matmul(
        p_o[:], ctx.cf32["ones_row"], offb[0:1, :], start=True, stop=True),
        reads=[b["offb"], ctx.cbuf], writes=[b["po"]], tag=f"{tag}_obc")
    P.op("scalar", lambda e: e.copy(offb[:], p_o[:]),
         reads=[b["po"]], writes=[b["offb"]], tag=f"{tag}_obce")

    for n in range(NT):
        P.op("sync", lambda e, n=n: e.dma_start(
            out=cmb_t.ap().rearrange("p (ms e) -> p ms e", e=8),
            in_=cmb_dram[n * 512:(n + 1) * 512, :].rearrange(
                "(ms p) e -> p ms e", p=128)),
            writes=[b["cmb"]], dma=True, tag=f"{tag}_cld{n}")
        P.op("sync", lambda e, n=n: e.dma_start(
            out=rank_t.ap().rearrange("p (ms e) -> p ms e", e=8),
            in_=rank_dram[n * 512:(n + 1) * 512, :].rearrange(
                "(ms p) e -> p ms e", p=128)),
            writes=[b["rank"]], dma=True, tag=f"{tag}_rld{n}")
        for ms in range(4):
            P.op("vector", lambda e, ms=ms: e.tensor_add(
                rank_t[:, ms * 8:(ms + 1) * 8], rank_t[:, ms * 8:(ms + 1) * 8],
                offb[:]),
                reads=[b["rank"], b["offb"]], writes=[b["rank"]],
                tag=f"{tag}_ga{n}_{ms}")
            P.op("vector", lambda e, ms=ms: e.tensor_single_scalar(
                rank_t[:, ms * 8:(ms + 1) * 8], rank_t[:, ms * 8:(ms + 1) * 8],
                CAPACITY1, ALU.is_lt),
                reads=[b["rank"]], writes=[b["rank"]], tag=f"{tag}_gv{n}_{ms}")
            P.op("vector", lambda e, ms=ms: e.tensor_mul(
                g_t[:], rank_t[:, ms * 8:(ms + 1) * 8],
                cmb_t[:, ms * 8:(ms + 1) * 8]),
                reads=[b["rank"], b["cmb"]], writes=[b["g"]],
                tag=f"{tag}_gm{n}_{ms}")
            P.op("tensor", lambda e: e.transpose(p_tr[:], g_t[:], identb),
                 reads=[b["g"], ctx.cbuf], writes=[b["ptr"]],
                 tag=f"{tag}_tr{n}_{ms}")
            P.op("scalar", lambda e, ms=ms: e.copy(
                gct[:, ms * 128:(ms + 1) * 128], p_tr[:]),
                reads=[b["ptr"]], writes=[b["gct"]], tag=f"{tag}_te{n}_{ms}")
        P.op("sync", lambda e, n=n: e.dma_start(
            out=gct_dram[:, n * 512:(n + 1) * 512], in_=gct[:]),
            reads=[b["gct"]], dma=True, tag=f"{tag}_st{n}")


def emit_moe1_experts(ctx, s2_dram, gct_dram, w1_dram, w2_dram, experts,
                      out_dram, accumulate, tag="m1e"):
    nc, P, a = ctx.nc, ctx.P, ctx.alloc
    ne = len(experts)
    HID = 2048
    HCH = HID // 128
    w1 = a.sb(f"{tag}_w1", [128, ne * DCH * HID], BF16)
    w2 = a.sb(f"{tag}_w2", [128, ne * HCH * D], BF16)
    stage = [a.sb(f"{tag}_st{i}", [128, 2048], F32) for i in range(1)]
    stb = [ctx.buf(f"{tag}_st{i}") for i in range(1)]
    xn_t = a.sb(f"{tag}_xn", [128, DCH * 512], BF16)
    gct_t = a.sb(f"{tag}_gct", [8, 512], BF16)
    cbc = a.sb(f"{tag}_cbc", [128, 512], BF16)
    h_sb = a.sb(f"{tag}_h", [128, ne * HCH * 512], BF16)
    rout = a.sb(f"{tag}_ro", [128, 2 * 512], F32)
    p_g = [a.ps(f"{tag}_pg{i}", [128, 512], F32) for i in range(4)]
    p_d = [a.ps(f"{tag}_pd{i}", [128, 512], F32) for i in range(3)]
    p_cb = a.ps(f"{tag}_pcb", [128, 512], F32)
    b = {n: ctx.buf(f"{tag}_{n}") for n in
         ("w1", "w2", "xn", "gct", "cbc", "h", "ro", "pcb")}
    bg = [ctx.buf(f"{tag}_pg{i}") for i in range(4)]
    bd = [ctx.buf(f"{tag}_pd{i}") for i in range(3)]
    sel8 = ctx.cb16["sel8"]

    j = 0
    for ei, ex in enumerate(experts):
        for k in range(DCH):
            st, sb_ = stage[0], stb[0]
            P.op("sync", lambda e, ex=ex, k=k, st=st: e.dma_start(
                out=st[:], in_=w1_dram[ex * 1024 + k * 128: ex * 1024 + (k + 1) * 128, :]),
                writes=[sb_], dma=True, tag=f"{tag}_w1l{ei}_{k}")
            P.op("vector", lambda e, ei=ei, k=k, st=st: e.tensor_copy(
                w1[:, (ei * DCH + k) * HID:(ei * DCH + k + 1) * HID], st[:]),
                reads=[sb_], writes=[b["w1"]], tag=f"{tag}_w1c{ei}_{k}")
        for k2 in range(HCH):
            st, sb_ = stage[0], stb[0]
            P.op("sync", lambda e, ex=ex, k2=k2, st=st: e.dma_start(
                out=st[:, 0:1024],
                in_=w2_dram[ex * 2048 + k2 * 128: ex * 2048 + (k2 + 1) * 128, :]),
                writes=[sb_], dma=True, tag=f"{tag}_w2l{ei}_{k2}")
            P.op("vector", lambda e, ei=ei, k2=k2, st=st: e.tensor_copy(
                w2[:, (ei * HCH + k2) * D:(ei * HCH + k2 + 1) * D], st[:, 0:1024]),
                reads=[sb_], writes=[b["w2"]], tag=f"{tag}_w2c{ei}_{k2}")

    for n in range(NT):
        P.op("sync", lambda e, n=n: e.dma_start(
            out=xn_t.ap().rearrange("p (c t) -> p c t", c=DCH),
            in_=s2_dram.rearrange("(c p) t -> p c t", p=128)[:, :, n * 512:(n + 1) * 512]),
            writes=[b["xn"]], dma=True, tag=f"{tag}_xld{n}")
        P.op("sync", lambda e, n=n: e.dma_start(
            out=gct_t[:], in_=gct_dram[:, n * 512:(n + 1) * 512]),
            writes=[b["gct"]], dma=True, tag=f"{tag}_gld{n}")
        for ei, ex in enumerate(experts):
            P.op("tensor", lambda e, ex=ex: e.matmul(
                p_cb[:], sel8[:, ex * 128:(ex + 1) * 128], gct_t[:],
                start=True, stop=True),
                reads=[b["gct"], ctx.cbuf], writes=[b["pcb"]],
                tag=f"{tag}_cbm{n}_{ei}")
            P.op("scalar", lambda e: e.copy(cbc[:], p_cb[:]),
                 reads=[b["pcb"]], writes=[b["cbc"]], tag=f"{tag}_cbe{n}_{ei}")
            for m in range(HCH):
                bk = m % 4
                for k in range(DCH):
                    P.op("tensor", lambda e, ei=ei, m=m, k=k, bk=bk: e.matmul(
                        p_g[bk][:],
                        w1[:, (ei * DCH + k) * HID + m * 128: (ei * DCH + k) * HID + (m + 1) * 128],
                        xn_t[:, k * 512:(k + 1) * 512],
                        start=(k == 0), stop=(k == DCH - 1)),
                        reads=[b["w1"], b["xn"]], writes=[bg[bk]],
                        tag=f"{tag}_1mm{n}_{ei}_{m}_{k}")
                P.op("scalar", lambda e, ei=ei, m=m, bk=bk: e.activation(
                    out=h_sb[:, (ei * HCH + m) * 512:(ei * HCH + m + 1) * 512],
                    in_=p_g[bk][:], func=AF.Gelu),
                    reads=[bg[bk]], writes=[b["h"]], tag=f"{tag}_gel{n}_{ei}_{m}")
                P.op("vector", lambda e, ei=ei, m=m: e.tensor_mul(
                    h_sb[:, (ei * HCH + m) * 512:(ei * HCH + m + 1) * 512],
                    h_sb[:, (ei * HCH + m) * 512:(ei * HCH + m + 1) * 512],
                    cbc[:]),
                    reads=[b["h"], b["cbc"]], writes=[b["h"]],
                    tag=f"{tag}_hcb{n}_{ei}_{m}")
        for mo in range(DCH):
            bk = mo % 3
            nacc = ne * HCH
            jj = 0
            for ei in range(ne):
                for k2 in range(HCH):
                    P.op("tensor", lambda e, ei=ei, k2=k2, mo=mo, bk=bk, jj=jj: e.matmul(
                        p_d[bk][:],
                        w2[:, (ei * HCH + k2) * D + mo * 128: (ei * HCH + k2) * D + (mo + 1) * 128],
                        h_sb[:, (ei * HCH + k2) * 512:(ei * HCH + k2 + 1) * 512],
                        start=(jj == 0), stop=(jj == nacc - 1)),
                        reads=[b["w2"], b["h"]], writes=[bd[bk]],
                        tag=f"{tag}_2mm{n}_{mo}_{ei}_{k2}")
                    jj += 1
            P.op("scalar", lambda e, mo=mo, bk=bk: e.copy(
                rout[:, (mo % 2) * 512:(mo % 2 + 1) * 512], p_d[bk][:]),
                reads=[bd[bk]], writes=[b["ro"]], tag=f"{tag}_dev{n}_{mo}")
            if mo % 2 == 1:
                q_ = "gpsimd" if accumulate else "sync"
                P.op(q_, lambda e, n=n, mo=mo: e.dma_start(
                    out=out_dram.rearrange("(c p) t -> p c t", p=128)[
                        :, mo - 1:mo + 1, n * 512:(n + 1) * 512],
                    in_=rout.ap().rearrange("p (c t) -> p c t", c=2),
                    **(dict(accum_op=ALU.add) if accumulate else {})),
                    reads=[b["ro"]], dma=True, tag=f"{tag}_ost{n}_{mo}")


def run_ln(x_cores, outs_spec, kind="ln", w=None, eps=1e-5, cache_key="ln"):
    names = ["ones_row", "ones_col"]
    specs = {"x_fm": ((D, T), F32, "ExternalInput"), **const_specs(names)}
    for nm, dt in outs_spec:
        specs[nm] = ((D, T), dt, "ExternalOutput")
    if w is not None:
        specs["lnw"] = ((128, DCH), F32, "ExternalInput")

    def ph(ctx, dr):
        emit_ln(ctx, dr["x_fm"], [(dr[nm], dt) for nm, dt in outs_spec],
                kind=kind, w_tiled_dram=dr.get("lnw"), eps=eps, tag="ln")

    base = dict(**const_inmaps(names))
    if w is not None:
        base["lnw"] = _tile_bias(w)
    im = [dict(x_fm=x, **base) for x in x_cores]
    res = _run(cache_key, specs, names, [ph], im)
    return {nm: [r[nm] for r in res] for nm, _ in outs_spec}


def run_linscan(pq, pk, pkt, vaug):
    names = ["ones_row", "triu_mask"]
    specs = {
        "pq": ((D, T), BF16, "ExternalInput"),
        "pk": ((D, T), BF16, "ExternalInput"),
        "pkt": ((T, D), BF16, "ExternalInput"),
        "vaug": ((T, 16 * 65), BF16, "ExternalInput"),
        "y": ((D, T), BF16, "ExternalOutput"),
        **const_specs(names),
    }

    def ph(ctx, dr):
        emit_linscan(ctx, dr["pq"], dr["pk"], dr["pkt"], dr["vaug"], dr["y"],
                     tag="ls")

    base = const_inmaps(names)
    im = [dict(pq=pq[c], pk=pk[c], pkt=pkt[c], vaug=vaug[c], **base)
          for c in range(N_CORES)]
    res = _run("linscan", specs, names, [ph], im)
    return [r["y"] for r in res]


def run_moe1_route(s2_cores, gate_w):
    names = ["ones_row", "ones_col", "striu"]
    specs = {
        "s2": ((D, T), BF16, "ExternalInput"),
        "gwT": ((D, 8), F32, "ExternalInput"),
        "cmb": ((T, 8), F32, "ExternalOutput"),
        "rank": ((T, 8), F32, "ExternalOutput"),
        "counts": ((1, 8), F32, "ExternalOutput"),
        **const_specs(names),
    }

    def ph(ctx, dr):
        emit_moe1_route(ctx, dr["s2"], dr["gwT"], dr["cmb"], dr["rank"],
                        dr["counts"], tag="m1r")

    base = dict(gwT=np.ascontiguousarray(gate_w), **const_inmaps(names))
    im = [dict(s2=s2, **base) for s2 in s2_cores]
    res = _run("moe1route", specs, names, [ph], im)
    return ([r["cmb"] for r in res], [r["rank"] for r in res],
            [r["counts"] for r in res])


def run_moe1_gate(cmb_cores, rank_cores, all_counts, core_masks):
    names = ["ones_row", "ident"]
    specs = {
        "cmb": ((T, 8), F32, "ExternalInput"),
        "rank": ((T, 8), F32, "ExternalInput"),
        "ca": ((8, 8), F32, "ExternalInput"),
        "cm": ((8, 1), F32, "ExternalInput"),
        "gct": ((8, T), BF16, "ExternalOutput"),
        **const_specs(names),
    }

    def ph(ctx, dr):
        emit_moe1_gate(ctx, dr["cmb"], dr["rank"], dr["ca"], dr["cm"],
                       dr["gct"], tag="m1g")

    base = const_inmaps(names)
    im = [dict(cmb=cmb_cores[c], rank=rank_cores[c], ca=all_counts,
               cm=core_masks[c], **base) for c in range(N_CORES)]
    res = _run("moe1gate", specs, names, [ph], im)
    return [r["gct"] for r in res]


def run_moe1_experts(s2_cores, gct_cores, w1, w2, experts, cache_key):
    names = ["sel8"]
    specs = {
        "s2": ((D, T), BF16, "ExternalInput"),
        "gct": ((8, T), BF16, "ExternalInput"),
        "w1": ((8 * 1024, 2048), F32, "ExternalInput"),
        "w2": ((8 * 2048, 1024), F32, "ExternalInput"),
        "out": ((D, T), F32, "ExternalOutput"),
        **const_specs(names),
    }

    def ph(ctx, dr):
        emit_moe1_experts(ctx, dr["s2"], dr["gct"], dr["w1"], dr["w2"],
                          experts, dr["out"], False, tag="m1e")

    base = dict(w1=w1.reshape(8 * 1024, 2048), w2=w2.reshape(8 * 2048, 1024),
                **const_inmaps(names))
    im = [dict(s2=s2_cores[c], gct=gct_cores[c], **base) for c in range(N_CORES)]
    res = _run(cache_key, specs, names, [ph], im)
    return [r["out"] for r in res]



def kernel(**inp):
    x = inp["L0_attn_x"]
    x_fm = [np.ascontiguousarray(x[c].T) for c in range(N_CORES)]

    xn0 = run_ln0(x_fm)
    q0 = run_mlp(xn0, inp["L0_attn_q_w1"], inp["L0_attn_q_b1"],
                 inp["L0_attn_q_w2"], inp["L0_attn_q_b2"], "fm", "q")
    k0 = run_mlp(xn0, inp["L0_attn_k_w1"], inp["L0_attn_k_b1"],
                 inp["L0_attn_k_w2"], inp["L0_attn_k_b2"], "fm", "k")
    v0 = run_mlp(xn0, inp["L0_attn_v_w1"], inp["L0_attn_v_b1"],
                 inp["L0_attn_v_w2"], inp["L0_attn_v_b2"], "tm_aug", "v")
    attn0x = run_attn(q0, k0, v0, inp["L0_attn_proj_w"], inp["L0_attn_proj_b"],
                      16, 16, False, res_cores=x_fm, cache_key="attn0")
    sg0 = run_linear(xn0, _t(inp["L0_ffn_shared_gate_w"]), None, "fm", 1024,
                     2048, act="silu", cache_key="shg")
    su0 = run_linear(xn0, _t(inp["L0_ffn_shared_up_w"]), None, "fm", 1024,
                     2048, act=None, cache_key="shu")
    ffn0 = run_linear(sg0, _t(inp["L0_ffn_shared_down_w"]), None, "fm", 2048,
                      1024, act=None, out_dtype=F32, x2_cores=su0,
                      cache_key="shd")
    cmb0 = run_ffn0_route(xn0, inp["L0_ffn_router_w"],
                          inp["L0_ffn_correction_bias"])
    rA = run_ffn0_experts(xn0, cmb0, inp["L0_ffn_gate_w"], inp["L0_ffn_up_w"],
                          inp["L0_ffn_down_w"], list(range(0, 8)), "exA")
    rB = run_ffn0_experts(xn0, cmb0, inp["L0_ffn_gate_w"], inp["L0_ffn_up_w"],
                          inp["L0_ffn_down_w"], list(range(8, 16)), "exB")
    x1 = run_add([attn0x, ffn0, rA, rB], "comb_l0")

    ln1 = run_ln(x1, [("xn1b", BF16), ("xn1f", F32)], cache_key="ln1")
    xn1b, xn1f = ln1["xn1b"], ln1["xn1f"]
    pq = run_linear(xn1b, _t(inp["L1_attn_W_q"]), None, "fm", 1024, 1024,
                    act="elu1", cache_key="l1q")
    pk = run_linear(xn1b, _t(inp["L1_attn_W_k"]), None, "fm", 1024, 1024,
                    act="elu1", cache_key="l1k")
    pkt = run_linear(xn1b, _t(inp["L1_attn_W_k"]), None, "tm", 1024, 1024,
                     act="elu1", cache_key="l1kt")
    v1 = run_linear(xn1b, _t(inp["L1_attn_W_v"]), None, "tm", 1024, 1024,
                    aug=True, cache_key="l1v")
    y1 = run_linscan(pq, pk, pkt, v1)
    attn1 = run_linear(y1, _t(inp["L1_attn_W_o"]), None, "fm", 1024, 1024,
                       act=None, out_dtype=F32, cache_key="l1o")
    qg = run_linear(xn1b, inp["L1_ffn_qw"], inp["L1_ffn_qb"], "fm", 1024,
                    1024, cache_key="gqaq")
    kg = run_linear(xn1b, inp["L1_ffn_kw"], inp["L1_ffn_kb"], "fm", 1024,
                    256, cache_key="gqak")
    vg = run_linear(xn1b, inp["L1_ffn_vw"], None, "tm", 1024, 256, aug=True,
                    b_row=inp["L1_ffn_vb"], cache_key="gqav")
    s1 = run_attn(qg, kg, vg, inp["L1_ffn_ow"], inp["L1_ffn_ob"], 16, 4, True,
                  res_cores=xn1f, cache_key="gqa_at")
    rms1 = run_ln(s1, [("s2b", BF16), ("s2f", F32)], kind="rms",
                  w=inp["L1_ffn_norm1_w"], eps=1e-6, cache_key="rms1")
    s2b, s2f = rms1["s2b"], rms1["s2f"]
    cmb1, rank1, counts = run_moe1_route(s2b, inp["L1_ffn_gate_w"])
    all_counts = np.ascontiguousarray(
        np.concatenate([counts[c] for c in range(N_CORES)], axis=0))
    core_masks = []
    for c in range(N_CORES):
        m = np.zeros((8, 1), np.float32)
        m[:c, 0] = 1.0
        core_masks.append(m)
    gct = run_moe1_gate(cmb1, rank1, all_counts, core_masks)
    mo = []
    for gi in range(4):
        mo.append(run_moe1_experts(s2b, gct, inp["L1_ffn_w1"],
                                   inp["L1_ffn_w2"],
                                   [2 * gi, 2 * gi + 1], f"m1e{gi}"))
    s3 = run_add([s2f] + mo, "comb_s3")
    rms2 = run_ln(s3, [("f1", F32)], kind="rms", w=inp["L1_ffn_norm2_w"],
                  eps=1e-6, cache_key="rms2")
    ffn1 = rms2["f1"]
    out_fm = run_add([x1, attn1, ffn1], "comb_out")
    return np.ascontiguousarray(
        np.stack([out_fm[c].T for c in range(N_CORES)]))



ALL_CONSTS = ["ident", "ones_all", "ones_row", "ones_col", "striu",
              "triu_mask", "sel16", "sel8"]


def build_fused(part="all"):
    nc = bass.Bass("TRN2", target_bir_lowering=False, debug=False)
    dr = {}

    def din(nm, shape, dtype=F32):
        dr[nm] = nc.dram_tensor(nm, list(shape), dtype, kind="ExternalInput").ap()

    import os
    taps = set(os.environ.get("DEBUG_TAPS", "").split(",")) - {""}
    HANDOFF = ("x1", "attn1", "s2b", "s2f", "cmb1", "rank1", "cnts")
    if part == "C":
        HANDOFF = HANDOFF + ("s3",)

    def dint(nm, shape, dtype=F32):
        kind = "ExternalOutput" if nm in taps else "Internal"
        if part == "A" and nm in HANDOFF:
            kind = "ExternalOutput"
        if part in ("B", "C") and nm in HANDOFF:
            kind = "ExternalInput"
        dr[nm] = nc.dram_tensor(nm, list(shape), dtype, kind=kind).ap()

    din("x_fm", (D, T))
    for nm in ALL_CONSTS:
        din("c_" + nm, CONST_SHAPES[nm])
    for p_ in ("q", "k", "v"):
        din(f"{p_}w1", (D, 4096)); din(f"{p_}b1", (128, 32))
        din(f"{p_}w2", (4096, D)); din(f"{p_}b2", (128, 8))
    din("vb2row", (1, D))
    din("pjw", (D, D)); din("pjb", (128, 8))
    din("sgwT", (D, 2048)); din("suwT", (D, 2048)); din("sdwT", (2048, D))
    din("rwT", (D, 16)); din("rcb", (1, 16))
    din("egw", (16 * 1024, 256)); din("euw", (16 * 1024, 256))
    din("edw", (16 * 256, 1024))
    din("wqT", (D, D)); din("wkT", (D, D)); din("wvT", (D, D)); din("woT", (D, D))
    din("gqw", (D, D)); din("gqb", (128, 8))
    din("gkw", (D, 256)); din("gkb", (128, 2))
    din("gvw", (D, 256)); din("gvbrow", (1, 256))
    din("gow", (D, D)); din("gob", (128, 8))
    din("n1w", (128, DCH)); din("n2w", (128, DCH))
    din("mgw", (D, 8))
    din("mw1", (8 * 1024, 2048)); din("mw2", (8 * 2048, 1024))
    din("ca8", (8, 8)); din("cmask", (8, 1))
    out = nc.dram_tensor("out_fm", [D, T], F32, kind="ExternalOutput").ap()

    dint("xn0", (D, T), BF16)
    dint("q0", (D, T), BF16); dint("k0", (D, T), BF16)
    dint("v0", (T, 16 * 65), BF16)
    dint("attn0x", (D, T), F32)
    dint("sg", (D2 := 2048, T), BF16); dint("su", (2048, T), BF16)
    dint("ffn0", (D, T), F32)
    dint("cmb0", (16, T), BF16)
    dint("x1", (D, T), F32)
    dint("xn1b", (D, T), BF16); dint("xn1f", (D, T), F32)
    dint("pq", (D, T), BF16); dint("pk", (D, T), BF16)
    dint("pkt", (T, D), BF16); dint("v1", (T, 16 * 65), BF16)
    dint("y1", (D, T), BF16)
    dint("attn1", (D, T), F32)
    dint("qg", (D, T), BF16); dint("kg", (256, T), BF16)
    dint("vg", (T, 4 * 65), BF16)
    dint("s1", (D, T), F32)
    dint("s2b", (D, T), BF16); dint("s2f", (D, T), F32)
    dint("cmb1", (T, 8), F32); dint("rank1", (T, 8), F32)
    dint("cnts", (1, 8), F32)
    dint("gct", (8, T), BF16)
    dint("moe", (D, T), F32)
    dint("s3", (D, T), F32)
    dint("ffn1", (D, T), F32)

    from contextlib import ExitStack
    with ExitStack() as outer:
        P = Prog(nc)
        ctx = Ctx(nc, P, Alloc(nc, outer))
        cd = {nm: dr["c_" + nm] for nm in ALL_CONSTS}
        with ExitStack() as ces:
            emit_load_constants(ctx, cd, ALL_CONSTS,
                                stage_alloc=Alloc(nc, ces))
        P.phase_barrier()
        if os.environ.get("DEBUG_CONSTS"):
            dbgc = nc.dram_tensor("dbgc", [128, 256], F32,
                                  kind="ExternalOutput").ap()
            dsb = outer.enter_context(nc.sbuf_tensor("dbgc_sb", [128, 256], F32))
            bdb = ctx.buf("dbgc_sb")
            P.op("vector", lambda e: e.tensor_copy(
                dsb[:, 0:1], ctx.cb16["ones_col"]),
                reads=[ctx.cbuf], writes=[bdb], tag="dbgc1")
            P.op("vector", lambda e: e.tensor_copy(
                dsb[0:1, 1:129], ctx.cb16["ones_row"]),
                reads=[ctx.cbuf], writes=[bdb], tag="dbgc2")
            P.op("vector", lambda e: e.tensor_copy(
                dsb[0:1, 129:130], ctx.cf32["ones_row"][0:1, 0:1]),
                reads=[ctx.cbuf], writes=[bdb], tag="dbgc3")
            P.op("sync", lambda e: e.dma_start(out=dbgc, in_=dsb[:]),
                 reads=[bdb], dma=True, tag="dbgc_st")
            P.phase_barrier()

        def phase(fn):
            with ExitStack() as pes:
                ctx.alloc = Alloc(nc, pes)
                fn()
            P.phase_barrier()

        PA = part in ("all", "A")
        PB = part in ("all", "B")
        PC = part == "C"
        if PA:
            phase(lambda: emit_ln(ctx, dr["x_fm"], [(dr["xn0"], BF16)],
                                  kind="ln", tag="Pln0"))
        if PA:
            phase(lambda: emit_mlp(ctx, dr["xn0"], dr["qw1"], dr["qb1"],
                                   dr["qw2"], dr["qb2"], "fm", dr["q0"], tag="Pmq"))
            phase(lambda: emit_mlp(ctx, dr["xn0"], dr["kw1"], dr["kb1"],
                                   dr["kw2"], dr["kb2"], "fm", dr["k0"], tag="Pmk"))
            phase(lambda: emit_mlp(ctx, dr["xn0"], dr["vw1"], dr["vb1"],
                                   dr["vw2"], dr["vb2"], "tm_aug", dr["v0"],
                                   b2row_dram=dr["vb2row"], tag="Pmv"))
            phase(lambda: emit_attn(ctx, dr["q0"], dr["k0"], False, dr["v0"],
                                    16, 16, dr["pjw"], dr["pjb"], dr["attn0x"],
                                    res_drams=[dr["x_fm"]], tag="Pat0"))
            phase(lambda: emit_linear(ctx, dr["xn0"], dr["sgwT"], None, "fm",
                                      dr["sg"], 1024, 2048, act="silu", tag="Psg"))
            phase(lambda: emit_linear(ctx, dr["xn0"], dr["suwT"], None, "fm",
                                      dr["su"], 1024, 2048, act=None, tag="Psu"))
            phase(lambda: emit_linear(ctx, dr["sg"], dr["sdwT"], None, "fm",
                                      dr["ffn0"], 2048, 1024, act=None,
                                      out_dtype=F32, x2_dram=dr["su"], tag="Psd"))
            phase(lambda: emit_ffn0_route(ctx, dr["xn0"], dr["rwT"], dr["rcb"],
                                          dr["cmb0"], tag="Prt"))
            phase(lambda: emit_ffn0_experts(ctx, dr["xn0"], dr["cmb0"], dr["egw"],
                                            dr["euw"], dr["edw"], list(range(8)),
                                            dr["ffn0"], True, tag="PxA"))
            phase(lambda: emit_ffn0_experts(ctx, dr["xn0"], dr["cmb0"], dr["egw"],
                                            dr["euw"], dr["edw"], list(range(8, 16)),
                                            dr["ffn0"], True, tag="PxB"))
            phase(lambda: emit_add(ctx, [dr["attn0x"], dr["ffn0"]], dr["x1"],
                                   tag="Pc0"))
            phase(lambda: emit_ln(ctx, dr["x1"], [(dr["xn1b"], BF16),
                                                  (dr["xn1f"], F32)],
                                  kind="ln", tag="Pln1"))
            phase(lambda: emit_linear(ctx, dr["xn1b"], dr["wqT"], None, "fm",
                                      dr["pq"], 1024, 1024, act="elu1", tag="Plq"))
            phase(lambda: emit_linear(ctx, dr["xn1b"], dr["wkT"], None, "fm",
                                      dr["pk"], 1024, 1024, act="elu1", tag="Plk"))
            phase(lambda: emit_linear(ctx, dr["xn1b"], dr["wkT"], None, "tm",
                                      dr["pkt"], 1024, 1024, act="elu1", tag="Plkt"))
            phase(lambda: emit_linear(ctx, dr["xn1b"], dr["wvT"], None, "tm",
                                      dr["v1"], 1024, 1024, aug=True, tag="Plv"))
            phase(lambda: emit_linscan(ctx, dr["pq"], dr["pk"], dr["pkt"],
                                       dr["v1"], dr["y1"], tag="Pls"))
            phase(lambda: emit_linear(ctx, dr["y1"], dr["woT"], None, "fm",
                                      dr["attn1"], 1024, 1024, act=None,
                                      out_dtype=F32, tag="Plo"))
            phase(lambda: emit_linear(ctx, dr["xn1b"], dr["gqw"], dr["gqb"], "fm",
                                      dr["qg"], 1024, 1024, tag="Pgq"))
            phase(lambda: emit_linear(ctx, dr["xn1b"], dr["gkw"], dr["gkb"], "fm",
                                      dr["kg"], 1024, 256, tag="Pgk"))
            phase(lambda: emit_linear(ctx, dr["xn1b"], dr["gvw"], None, "tm",
                                      dr["vg"], 1024, 256, aug=True,
                                      b_row_dram=dr["gvbrow"], tag="Pgv"))
            phase(lambda: emit_attn(ctx, dr["qg"], dr["kg"], True, dr["vg"],
                                    16, 4, dr["gow"], dr["gob"], dr["s1"],
                                    res_drams=[dr["xn1f"]], tag="Pga"))
            phase(lambda: emit_ln(ctx, dr["s1"], [(dr["s2b"], BF16),
                                                  (dr["s2f"], F32)],
                                  kind="rms", w_tiled_dram=dr["n1w"], eps=1e-6,
                                  tag="Pr1"))
            phase(lambda: emit_moe1_route(ctx, dr["s2b"], dr["mgw"], dr["cmb1"],
                                          dr["rank1"], dr["cnts"], tag="Pm1r"))
        if PB:
            phase(lambda: emit_moe1_gate(ctx, dr["cmb1"], dr["rank1"], dr["ca8"],
                                         dr["cmask"], dr["gct"], tag="Pm1g"))
            for gi in range(4):
                phase(lambda gi=gi: emit_moe1_experts(
                    ctx, dr["s2b"], dr["gct"], dr["mw1"], dr["mw2"],
                    [2 * gi, 2 * gi + 1], dr["moe"], gi > 0, tag=f"Pme{gi}"))
        if PB:
            phase(lambda: emit_add(ctx, [dr["s2f"], dr["moe"]], dr["s3"],
                                   tag="Pc3"))
        if PB or PC:
            phase(lambda: emit_ln(ctx, dr["s3"], [(dr["ffn1"], F32)],
                                  kind="rms", w_tiled_dram=dr["n2w"], eps=1e-6,
                                  tag="Pr2"))
        if PB:
            phase(lambda: emit_add(ctx, [dr["x1"], dr["attn1"], dr["ffn1"]],
                                   out, tag="Pco"))

        block = outer.enter_context(nc.Block())
        P.finalize(block, outer, nc)
    return nc


_FUSED_NC = None


def fused_in_maps(inp):
    x = inp["L0_attn_x"]
    ch = get_const_host()
    base = {
        "qw1": np.ascontiguousarray(inp["L0_attn_q_w1"]),
        "qb1": _tile_bias(inp["L0_attn_q_b1"]),
        "qw2": np.ascontiguousarray(inp["L0_attn_q_w2"]),
        "qb2": _tile_bias(inp["L0_attn_q_b2"]),
        "kw1": np.ascontiguousarray(inp["L0_attn_k_w1"]),
        "kb1": _tile_bias(inp["L0_attn_k_b1"]),
        "kw2": np.ascontiguousarray(inp["L0_attn_k_w2"]),
        "kb2": _tile_bias(inp["L0_attn_k_b2"]),
        "vw1": np.ascontiguousarray(inp["L0_attn_v_w1"]),
        "vb1": _tile_bias(inp["L0_attn_v_b1"]),
        "vw2": np.ascontiguousarray(inp["L0_attn_v_w2"]),
        "vb2": _tile_bias(inp["L0_attn_v_b2"]),
        "vb2row": np.ascontiguousarray(inp["L0_attn_v_b2"].reshape(1, D)),
        "pjw": np.ascontiguousarray(inp["L0_attn_proj_w"]),
        "pjb": _tile_bias(inp["L0_attn_proj_b"]),
        "sgwT": _t(inp["L0_ffn_shared_gate_w"]),
        "suwT": _t(inp["L0_ffn_shared_up_w"]),
        "sdwT": _t(inp["L0_ffn_shared_down_w"]),
        "rwT": _t(inp["L0_ffn_router_w"]),
        "rcb": np.ascontiguousarray(inp["L0_ffn_correction_bias"].reshape(1, 16)),
        "egw": np.ascontiguousarray(inp["L0_ffn_gate_w"].reshape(16 * 1024, 256)),
        "euw": np.ascontiguousarray(inp["L0_ffn_up_w"].reshape(16 * 1024, 256)),
        "edw": np.ascontiguousarray(inp["L0_ffn_down_w"].reshape(16 * 256, 1024)),
        "wqT": _t(inp["L1_attn_W_q"]),
        "wkT": _t(inp["L1_attn_W_k"]),
        "wvT": _t(inp["L1_attn_W_v"]),
        "woT": _t(inp["L1_attn_W_o"]),
        "gqw": np.ascontiguousarray(inp["L1_ffn_qw"]),
        "gqb": _tile_bias(inp["L1_ffn_qb"]),
        "gkw": np.ascontiguousarray(inp["L1_ffn_kw"]),
        "gkb": _tile_bias(inp["L1_ffn_kb"]),
        "gvw": np.ascontiguousarray(inp["L1_ffn_vw"]),
        "gvbrow": np.ascontiguousarray(inp["L1_ffn_vb"].reshape(1, 256)),
        "gow": np.ascontiguousarray(inp["L1_ffn_ow"]),
        "gob": _tile_bias(inp["L1_ffn_ob"]),
        "n1w": _tile_bias(inp["L1_ffn_norm1_w"]),
        "n2w": _tile_bias(inp["L1_ffn_norm2_w"]),
        "mgw": np.ascontiguousarray(inp["L1_ffn_gate_w"]),
        "mw1": np.ascontiguousarray(inp["L1_ffn_w1"].reshape(8 * 1024, 2048)),
        "mw2": np.ascontiguousarray(inp["L1_ffn_w2"].reshape(8 * 2048, 1024)),
        "ca8": np.zeros((8, 8), np.float32),
    }
    for nm in ALL_CONSTS:
        base["c_" + nm] = ch[nm]
    im = []
    for c in range(N_CORES):
        m = dict(base)
        m["x_fm"] = np.ascontiguousarray(x[c].T)
        cmk = np.zeros((8, 1), np.float32)
        cmk[:c, 0] = 1.0
        m["cmask"] = cmk
        im.append(m)
    return im


def kernel_fused(**inp):
    global _FUSED_NC
    if _FUSED_NC is None:
        _FUSED_NC = build_fused()
    nc = _FUSED_NC
    im = fused_in_maps(inp)
    res = run_bass_kernel_spmd(nc, im, core_ids=list(range(N_CORES)))
    return np.ascontiguousarray(
        np.stack([res.results[c]["out_fm"].T for c in range(N_CORES)]))


kernel_multi = kernel
kernel = kernel_fused


def _input_names(nc):
    names = []
    for alloc in nc.m.functions[0].allocations:
        if isinstance(alloc, mybir.MemoryLocationSet) and alloc.kind == "ExternalInput":
            nm = alloc.name
            if nm.endswith("_set"):
                nm = nm[:-4]
            if nm != "partition_id":
                names.append(nm)
    return set(names)


_AB_NC = {}


def kernel_ab(**inp):
    if "A" not in _AB_NC:
        _AB_NC["A"] = build_fused("A")
        _AB_NC["B"] = build_fused("B")
    ncA, ncB = _AB_NC["A"], _AB_NC["B"]
    namesA, namesB = _input_names(ncA), _input_names(ncB)
    im = fused_in_maps(inp)
    imA = [{k: v for k, v in m.items() if k in namesA} for m in im]
    resA = run_bass_kernel_spmd(ncA, imA, core_ids=list(range(N_CORES)))
    all_counts = np.ascontiguousarray(np.concatenate(
        [np.asarray(resA.results[c]["cnts"]) for c in range(N_CORES)], axis=0))
    imB = []
    for c in range(N_CORES):
        m = {k: v for k, v in im[c].items() if k in namesB}
        for nm in ("x1", "attn1", "s2b", "s2f", "cmb1", "rank1", "cnts"):
            if nm in namesB:
                m[nm] = np.asarray(resA.results[c][nm])
        m["ca8"] = all_counts
        imB.append(m)
    resB = run_bass_kernel_spmd(ncB, imB, core_ids=list(range(N_CORES)))
    return np.ascontiguousarray(
        np.stack([resB.results[c]["out_fm"].T for c in range(N_CORES)]))


kernel = kernel_ab


# version 11 (final)
# speedup vs baseline: 1.0576x; 1.0398x over previous
import math
import numpy as np
import ml_dtypes
import concourse.bass as bass
import concourse.mybir as mybir
from concourse.bass_utils import run_bass_kernel_spmd

F32 = mybir.dt.float32
BF16 = mybir.dt.bfloat16
AF = mybir.ActivationFunctionType
ALU = mybir.AluOpType
AX = mybir.AxisListType

N_CORES = 8
D = 1024
T = 4096
DCH = D // 128
NT = T // 512


ENGINES = ("sync", "scalar", "vector", "tensor", "gpsimd")


class Op:
    __slots__ = ("engine", "idx", "fn", "deps", "is_dma", "signals", "tag",
                 "sem_key")

    def __init__(self, engine, idx, fn, deps, is_dma, tag, sem_key):
        self.engine = engine
        self.idx = idx
        self.fn = fn
        self.deps = deps
        self.is_dma = is_dma
        self.signals = False
        self.tag = tag
        self.sem_key = sem_key


class Buf:
    __slots__ = ("name", "writer", "readers")

    def __init__(self, name=""):
        self.name = name
        self.writer = None
        self.readers = []


class Prog:
    def __init__(self, nc):
        self.nc = nc
        self.streams = {e: [] for e in ENGINES}
        self.order = []
        self.last_by_sem = {}
        self.pending_barrier = None
        self.barrier_fresh = set()
        self.phase = 0

    def op(self, engine, fn, reads=(), writes=(), deps=(), dma=False, tag=""):
        dl = []
        for b in reads:
            if b.writer is not None:
                dl.append(b.writer)
        for b in writes:
            if b.writer is not None:
                dl.append(b.writer)
            last = {}
            for r in b.readers:
                last[r.sem_key] = r
            dl.extend(last.values())
        dl.extend(d for d in deps if d is not None)
        if dma:
            bs = list(writes) or list(reads)
            assert bs, f"DMA op {tag} needs a read or write Buf"
            sem_key = f"dma{self.phase}_{bs[0].name}"
        else:
            sem_key = engine
        if self.pending_barrier is not None and engine in self.barrier_fresh:
            dl.append(self.pending_barrier)
            self.barrier_fresh.discard(engine)
        o = Op(engine, len(self.streams[engine]), fn, dl, dma, tag, sem_key)
        self.streams[engine].append(o)
        self.order.append(o)
        self.last_by_sem[sem_key] = o
        for b in reads:
            b.readers.append(o)
        for b in writes:
            b.writer = o
            b.readers = []
        return o

    def phase_barrier(self, tag="barrier"):
        deps = list(self.last_by_sem.values())
        o = self.op("sync", lambda e: e.nop(), deps=deps, tag=tag)
        self.pending_barrier = o
        self.barrier_fresh = set(ENGINES) - {"sync"}
        self.phase += 1
        return o

    def finalize(self, block, es, nc):
        for e in ENGINES:
            for o in self.streams[e]:
                if o.is_dma:
                    o.signals = True
                for d in o.deps:
                    if d.engine == o.engine and d.engine == "tensor" and not d.is_dma:
                        continue
                    d.signals = True
        cum = {}
        tot = {}
        for o in self.order:
            if o.signals:
                tot[o.sem_key] = tot.get(o.sem_key, 0) + (16 if o.is_dma else 1)
            cum[(o.engine, o.idx)] = tot.get(o.sem_key, 0)
        POOL = 48
        sems = {}
        base = {}
        for e in ENGINES:
            sems[e] = es.enter_context(nc.semaphore(f"s_{e}"))
            base[e] = 0
        pool = [es.enter_context(nc.semaphore(f"sd_{i}")) for i in range(POOL)]
        pool_accum = [0] * POOL
        dma_keys = [k for k in tot if k not in ENGINES]
        order_keys = []
        seen = set()
        for o in self.order:
            if o.sem_key not in seen and o.sem_key in tot and o.sem_key not in ENGINES:
                seen.add(o.sem_key)
                order_keys.append(o.sem_key)
        for ki, key in enumerate(order_keys):
            slot = ki % POOL
            sems[key] = pool[slot]
            base[key] = pool_accum[slot]
            pool_accum[slot] += tot[key]

        def emit(eng_obj, engine_name):
            waited = {}
            for o in self.streams[engine_name]:
                need = {}
                for d in o.deps:
                    if d.engine == engine_name and not d.is_dma \
                            and engine_name == "tensor":
                        continue
                    v = cum[(d.engine, d.idx)]
                    assert v > 0, f"dep not marked: {d.engine}:{d.idx} ({d.tag})"
                    need[d.sem_key] = max(need.get(d.sem_key, 0), v)
                for sk, v in need.items():
                    if waited.get(sk, 0) < v:
                        eng_obj.wait_ge(sems[sk], base[sk] + v)
                        waited[sk] = v
                inst = o.fn(eng_obj)
                if o.signals:
                    inst.then_inc(sems[o.sem_key], 16 if o.is_dma else 1)
            return waited

        @block.sync
        def _(sync):
            w = emit(sync, "sync")
            for key, v in tot.items():
                if v > 0 and w.get(key, 0) < v:
                    sync.wait_ge(sems[key], base[key] + v)

        @block.scalar
        def _(scalar):
            emit(scalar, "scalar")

        @block.vector
        def _(vector):
            emit(vector, "vector")

        @block.tensor
        def _(tensor):
            emit(tensor, "tensor")

        @block.gpsimd
        def _(gpsimd):
            emit(gpsimd, "gpsimd")



def _tile_bias(b):
    d = b.shape[0]
    return np.ascontiguousarray(b.reshape(d // 128, 128).T)


def _t(w):
    return np.ascontiguousarray(w.T)


def host_constants():
    c = {}
    c["c_ident_bf"] = np.eye(128, dtype=ml_dtypes.bfloat16)
    c["c_ones_row"] = np.ones((1, 128), np.float32)
    c["c_ones_all"] = np.ones((128, 128), np.float32)
    c["c_ones_col"] = np.ones((128, 1), np.float32)
    c["c_striu"] = np.triu(np.ones((128, 128), np.float32), 1)
    c["c_triu_mask"] = np.triu(np.ones((64, 64), np.float32), 0)
    s16 = np.zeros((16, 16 * 128), np.float32)
    for e in range(16):
        s16[e, e * 128:(e + 1) * 128] = 1.0
    c["c_sel16"] = s16
    s8 = np.zeros((8, 8 * 128), np.float32)
    for e in range(8):
        s8[e, e * 128:(e + 1) * 128] = 1.0
    c["c_sel8"] = s8
    return c



class Alloc:
    def __init__(self, nc, ctxstack):
        self.nc = nc
        self.es = ctxstack
        self.handles = {}

    def sb(self, name, shape, dtype):
        if name in self.handles:
            return self.handles[name]
        h = self.es.enter_context(self.nc.sbuf_tensor(name, shape, dtype))
        self.handles[name] = h
        return h

    def ps(self, name, shape, dtype=F32):
        if name in self.handles:
            return self.handles[name]
        h = self.es.enter_context(self.nc.psum_tensor(name, shape, dtype))
        self.handles[name] = h
        return h


class Ctx:

    def __init__(self, nc, prog, alloc):
        self.nc = nc
        self.P = prog
        self.alloc = alloc
        self.bufs = {}
        self.cb16 = {}
        self.cf32 = {}
        self.cbuf = Buf("consts")

    def buf(self, name):
        if name not in self.bufs:
            self.bufs[name] = Buf(name)
        return self.bufs[name]


CONST_SHAPES = {
    "ident": (128, 128),
    "ones_all": (128, 128),
    "ones_row": (1, 128),
    "ones_col": (128, 1),
    "striu": (128, 128),
    "triu_mask": (64, 64),
    "sel16": (16, 16 * 128),
    "sel8": (8, 8 * 128),
}


F32_CONSTS = ("ones_row", "ones_col", "ident", "ones_all")


def emit_load_constants(ctx, const_drams, names=None, stage_alloc=None):
    P, a = ctx.P, ctx.alloc
    names = names or list(CONST_SHAPES)
    for nm in names:
        shp = CONST_SHAPES[nm]
        if nm in F32_CONSTS:
            a.sb("sf_" + nm, list(shp), F32)
        a.sb("sc_" + nm, list(shp), BF16)
    stage = (stage_alloc or a).sb("c_stage", [128, 2048], F32)
    for nm in names:
        shp = CONST_SHAPES[nm]
        keep32 = nm in F32_CONSTS
        sf = a.sb("sf_" + nm, list(shp), F32) if keep32 else stage
        sb = a.sb("sc_" + nm, list(shp), BF16)
        if keep32:
            ctx.cf32[nm] = sf
        ctx.cb16[nm] = sb
        P.op("sync", lambda e, sf=sf, nm=nm, shp=shp: e.dma_start(
            out=sf[0:shp[0], 0:shp[1]], in_=const_drams[nm]),
            writes=[ctx.cbuf], dma=True, tag=f"c_{nm}_ld")
        P.op("vector", lambda e, sb=sb, sf=sf, shp=shp: e.tensor_copy(
            sb[:], sf[0:shp[0], 0:shp[1]]),
            reads=[ctx.cbuf], writes=[ctx.cbuf], tag=f"c_{nm}_cast")
    ctx.cb16 = {k: v.ap() for k, v in ctx.cb16.items()}
    ctx.cf32 = {k: v.ap() for k, v in ctx.cf32.items()}


def load_weight_bf16(ctx, w_dram, w_sb, stage_sb, stage_bufs, rows, cols,
                     tag, cast_engine="vector", dma_q="sync"):
    P = ctx.P
    wb = ctx.buf(tag)
    nk = rows // 128
    nst = len(stage_sb)
    for k in range(nk):
        st = stage_sb[k % nst]
        stb = stage_bufs[k % nst]
        P.op(dma_q, lambda e, k=k, st=st: e.dma_start(
            out=st[:, 0:cols], in_=w_dram[k * 128:(k + 1) * 128, :]),
            writes=[stb], dma=True, tag=f"{tag}_ld{k}")
        P.op(cast_engine, lambda e, k=k, st=st: e.tensor_copy(
            w_sb[:, k * cols:(k + 1) * cols], st[:, 0:cols]),
            reads=[stb], writes=[wb], tag=f"{tag}_cast{k}")
    return wb



def emit_mlp(ctx, xn_dram, w1_dram, b1_dram, w2_dram, b2_dram, out_mode,
             out_dram, b2row_dram=None, tag="mlp"):
    nc, P, a = ctx.nc, ctx.P, ctx.alloc
    HID = 4096
    HCH = HID // 128
    w1_sb = a.sb(f"{tag}_w1", [128, DCH * HID], BF16)
    w2_sb = a.sb(f"{tag}_w2", [128, HCH * D], BF16)
    stage = [a.sb(f"{tag}_st{i}", [128, 2048], F32) for i in range(1)]
    stageb = [ctx.buf(f"{tag}_st{i}") for i in range(1)]
    b1_sb = a.sb(f"{tag}_b1", [128, HCH], F32)
    xn_t = [a.sb(f"{tag}_xn{i}", [128, DCH * 512], BF16) for i in range(2)]
    h_t = a.sb(f"{tag}_h", [128, HCH * 512], BF16)
    p1 = [a.ps(f"{tag}_p1_{i}", [128, 512], F32) for i in range(4)]
    p2 = [a.ps(f"{tag}_p2_{i}", [128, 512], F32) for i in range(4)]
    b_w1 = ctx.buf(f"{tag}_w1")
    b_w2 = ctx.buf(f"{tag}_w2")
    b_b1 = ctx.buf(f"{tag}_b1")
    b_h = ctx.buf(f"{tag}_h")
    bp1 = [ctx.buf(f"{tag}_p1_{i}") for i in range(4)]
    bp2 = [ctx.buf(f"{tag}_p2_{i}") for i in range(4)]

    def load_w(w_dram, w_sb, wb, rows, cols, t2):
        nk = rows // 128
        cw = min(2048, cols)
        j = 0
        for k in range(nk):
            for half in range(cols // cw):
                st, stb = stage[0], stageb[0]
                j += 1
                P.op("sync", lambda e, k=k, h=half, st=st: e.dma_start(
                    out=st[:, 0:cw], in_=w_dram[k * 128:(k + 1) * 128,
                                                h * cw:(h + 1) * cw]),
                    writes=[stb], dma=True, tag=f"{t2}_ld{k}_{half}")
                P.op("vector", lambda e, k=k, h=half, st=st, cols=cols: e.tensor_copy(
                    w_sb[:, k * cols + h * cw: k * cols + (h + 1) * cw], st[:, 0:cw]),
                    reads=[stb], writes=[wb], tag=f"{t2}_cast{k}_{half}")

    load_w(w1_dram, w1_sb, b_w1, D, HID, tag + "w1")
    load_w(w2_dram, w2_sb, b_w2, HID, D, tag + "w2")
    P.op("sync", lambda e: e.dma_start(out=b1_sb[:], in_=b1_dram),
         writes=[b_b1], dma=True, tag=f"{tag}_b1ld")

    if out_mode == "fm":
        b2_sb = a.sb(f"{tag}_b2", [128, DCH], F32)
        b_b2 = ctx.buf(f"{tag}_b2")
        P.op("sync", lambda e: e.dma_start(out=b2_sb[:], in_=b2_dram),
             writes=[b_b2], dma=True, tag=f"{tag}_b2ld")
        out_t = a.sb(f"{tag}_out", [128, DCH * 512], BF16)
        b_out = ctx.buf(f"{tag}_out")
    else:
        b2r_sb = a.sb(f"{tag}_b2r", [1, D], F32)
        b2bc_sb = a.sb(f"{tag}_b2bc", [128, D], F32)
        b_b2r = ctx.buf(f"{tag}_b2r")
        b_b2bc = ctx.buf(f"{tag}_b2bc")
        P.op("sync", lambda e: e.dma_start(out=b2r_sb[:], in_=b2row_dram),
             writes=[b_b2r], dma=True, tag=f"{tag}_b2rld")
        b2rb_sb = a.sb(f"{tag}_b2rb", [1, D], BF16)
        P.op("vector", lambda e: e.tensor_copy(b2rb_sb[:], b2r_sb[:]),
             reads=[b_b2r], writes=[b_b2r], tag=f"{tag}_b2rcast")
        for seg in range(2):
            P.op("tensor", lambda e, s=seg: e.matmul(
                p2[3][:, 0:512], ctx.cb16["ones_row"],
                b2rb_sb[0:1, s * 512:(s + 1) * 512], start=True, stop=True),
                reads=[b_b2r, ctx.cbuf], writes=[bp2[3]], tag=f"{tag}_b2bc{seg}")
            P.op("scalar", lambda e, s=seg: e.copy(
                b2bc_sb[:, s * 512:(s + 1) * 512], p2[3][:, 0:512]),
                reads=[bp2[3]], writes=[b_b2bc], tag=f"{tag}_b2bcv{seg}")
        vaug = a.sb(f"{tag}_vaug", [128, 1040], BF16)
        b_vaug = ctx.buf(f"{tag}_vaug")
        P.op("vector", lambda e: e.memset(vaug[:], 1.0),
             writes=[b_vaug], tag=f"{tag}_vmemset")

    for n in range(NT):
        i = n % 2
        b_xn = ctx.buf(f"{tag}_xn{i}")
        P.op("sync", lambda e, n=n, i=i: e.dma_start(
            out=xn_t[i].ap().rearrange("p (c t) -> p c t", c=DCH),
            in_=xn_dram.rearrange("(c p) t -> p c t", p=128)[:, :, n * 512:(n + 1) * 512]),
            writes=[b_xn], dma=True, tag=f"{tag}_xnld{n}")
        for m in range(HCH):
            bk = m % 4
            for k in range(DCH):
                P.op("tensor", lambda e, m=m, k=k, i=i, bk=bk: e.matmul(
                    p1[bk][:], w1_sb[:, k * HID + m * 128: k * HID + (m + 1) * 128],
                    xn_t[i][:, k * 512:(k + 1) * 512],
                    start=(k == 0), stop=(k == DCH - 1)),
                    reads=[b_w1, b_xn], writes=[bp1[bk]],
                    tag=f"{tag}_mm1_{n}_{m}_{k}")
            P.op("scalar", lambda e, m=m, bk=bk: e.activation(
                out=h_t[:, m * 512:(m + 1) * 512], in_=p1[bk][:], func=AF.Gelu,
                bias=b1_sb[:, m:m + 1]),
                reads=[bp1[bk], b_b1], writes=[b_h], tag=f"{tag}_gelu{n}_{m}")
        if out_mode == "fm":
            for m2 in range(DCH):
                bk = m2 % 4
                for k2 in range(HCH):
                    P.op("tensor", lambda e, m2=m2, k2=k2, bk=bk: e.matmul(
                        p2[bk][:], w2_sb[:, k2 * D + m2 * 128: k2 * D + (m2 + 1) * 128],
                        h_t[:, k2 * 512:(k2 + 1) * 512],
                        start=(k2 == 0), stop=(k2 == HCH - 1)),
                        reads=[b_w2, b_h], writes=[bp2[bk]],
                        tag=f"{tag}_mm2_{n}_{m2}_{k2}")
                P.op("scalar", lambda e, m2=m2, bk=bk: e.activation(
                    out=out_t[:, m2 * 512:(m2 + 1) * 512], in_=p2[bk][:],
                    func=AF.Identity, bias=b2_sb[:, m2:m2 + 1]),
                    reads=[bp2[bk], b_b2], writes=[b_out], tag=f"{tag}_ev2_{n}_{m2}")
            P.op("sync", lambda e, n=n: e.dma_start(
                out=out_dram.rearrange("(c p) t -> p c t", p=128)[:, :, n * 512:(n + 1) * 512],
                in_=out_t.ap().rearrange("p (c t) -> p c t", c=DCH)),
                reads=[b_out], dma=True, tag=f"{tag}_ost{n}")
        else:
            for msub in range(4):
                for n2 in range(2):
                    bk = n2 % 2
                    for k2 in range(HCH):
                        P.op("tensor", lambda e, msub=msub, n2=n2, k2=k2, bk=bk: e.matmul(
                            p2[bk][:], h_t[:, k2 * 512 + msub * 128: k2 * 512 + (msub + 1) * 128],
                            w2_sb[:, k2 * D + n2 * 512: k2 * D + (n2 + 1) * 512],
                            start=(k2 == 0), stop=(k2 == HCH - 1)),
                            reads=[b_w2, b_h], writes=[bp2[bk]],
                            tag=f"{tag}_vmm2_{n}_{msub}_{n2}_{k2}")
                    P.op("vector", lambda e, n2=n2, bk=bk: e.tensor_add(
                        vaug.ap().rearrange("p (h s) -> p h s", s=65)[:, n2 * 8:(n2 + 1) * 8, 0:64],
                        p2[bk].ap().rearrange("p (h s) -> p h s", s=64),
                        b2bc_sb.ap().rearrange("p (h s) -> p h s", s=64)[:, n2 * 8:(n2 + 1) * 8, :]),
                        reads=[bp2[bk], b_b2bc], writes=[b_vaug],
                        tag=f"{tag}_vev_{n}_{msub}_{n2}")
                P.op("sync", lambda e, n=n, msub=msub: e.dma_start(
                    out=out_dram[n * 512 + msub * 128: n * 512 + (msub + 1) * 128, :],
                    in_=vaug[:]),
                    reads=[b_vaug], dma=True, tag=f"{tag}_vst{n}_{msub}")

def emit_ln(ctx, x_dram, outs, kind="ln", w_tiled_dram=None, eps=1e-5,
            tag="ln"):
    nc, P = ctx.nc, ctx.P
    a = ctx.alloc
    x_t = [a.sb(f"{tag}_x{i}", [128, DCH * 512], F32) for i in range(2)]
    xb_t = [a.sb(f"{tag}_xb{i}", [128, DCH * 512], BF16) for i in range(2)]
    sq_t = a.sb(f"{tag}_sq", [128, DCH * 512], BF16)
    orow = a.sb(f"{tag}_orow", [1, 512 * 4], F32)
    out_t = [a.sb(f"{tag}_o{i}", [128, DCH * 512], BF16) for i in range(2)]
    out32_t = [a.sb(f"{tag}_o32{i}", [128, DCH * 512], F32) for i in range(2)] \
        if any(dt == F32 for _, dt in outs) else None
    wln = a.sb(f"{tag}_w", [128, DCH], F32) if w_tiled_dram is not None else None
    ones_col = ctx.cb16["ones_col"]
    ones_row32 = ctx.cf32["ones_row"]

    p_stat = a.ps(f"{tag}_pstat", [1, 1024], F32)
    p_bc = a.ps(f"{tag}_pbc", [128, 1024], F32)

    if w_tiled_dram is not None:
        bw = ctx.buf(f"{tag}_w")
        P.op("sync", lambda e: e.dma_start(out=wln[:], in_=w_tiled_dram),
             writes=[bw], dma=True, tag=f"{tag}_wld")
    else:
        bw = None

    for n in range(NT):
        i = n % 2
        bx = ctx.buf(f"{tag}_x{i}")
        bxb = ctx.buf(f"{tag}_xb{i}")
        bsq = ctx.buf(f"{tag}_sq")
        bst = ctx.buf(f"{tag}_pstat")
        bbc = ctx.buf(f"{tag}_pbc")
        bor = ctx.buf(f"{tag}_orow")
        bo = ctx.buf(f"{tag}_o{i}")
        bo32 = ctx.buf(f"{tag}_o32{i}")
        P.op("sync", lambda e, n=n, i=i: e.dma_start(
            out=x_t[i].ap().rearrange("p (c t) -> p c t", c=DCH),
            in_=x_dram.rearrange("(c p) t -> p c t", p=128)[:, :, n * 512:(n + 1) * 512]),
            writes=[bx], dma=True, tag=f"{tag}_xld{n}")
        P.op("vector", lambda e, i=i: e.tensor_copy(xb_t[i][:], x_t[i][:]),
             reads=[bx], writes=[bxb], tag=f"{tag}_cast{n}")
        if kind == "ln":
            P.op("vector", lambda e, i=i: e.tensor_mul(sq_t[:], xb_t[i][:], xb_t[i][:]),
                 reads=[bxb], writes=[bsq], tag=f"{tag}_sq{n}")
        else:
            P.op("vector", lambda e, i=i: e.tensor_mul(sq_t[:], xb_t[i][:], xb_t[i][:]),
                 reads=[bxb], writes=[bsq], tag=f"{tag}_sq{n}")
        for c in range(DCH):
            if kind == "ln":
                P.op("tensor", lambda e, c=c, i=i: e.matmul(
                    p_stat[0:1, 0:512], ones_col, xb_t[i][:, c * 512:(c + 1) * 512],
                    start=(c == 0), stop=(c == DCH - 1)),
                    reads=[bxb, ctx.cbuf], writes=[bst], tag=f"{tag}_smm{n}_{c}")
            P.op("tensor", lambda e, c=c: e.matmul(
                p_stat[0:1, 512:1024], ones_col, sq_t[:, c * 512:(c + 1) * 512],
                start=(c == 0), stop=(c == DCH - 1)),
                reads=[bsq, ctx.cbuf], writes=[bst], tag=f"{tag}_qmm{n}_{c}")
        P.op("scalar", lambda e: e.copy(orow[0:1, 0:1024], p_stat[0:1, 0:1024]),
             reads=[bst], writes=[bor], tag=f"{tag}_evst{n}")
        if kind == "ln":
            P.op("vector", lambda e: e.tensor_scalar_mul(
                orow[0:1, 1024:1536], orow[0:1, 0:512], 1.0 / D),
                reads=[bor], writes=[bor], tag=f"{tag}_mean{n}")
            P.op("vector", lambda e: e.tensor_mul(
                orow[0:1, 0:512], orow[0:1, 1024:1536], orow[0:1, 1024:1536]),
                reads=[bor], writes=[bor], tag=f"{tag}_m2{n}")
            P.op("vector", lambda e: e.scalar_tensor_tensor(
                orow[0:1, 1536:2048], orow[0:1, 512:1024], 1.0 / D,
                orow[0:1, 0:512], ALU.mult, ALU.subtract),
                reads=[bor], writes=[bor], tag=f"{tag}_var{n}")
        else:
            P.op("vector", lambda e: e.tensor_scalar_mul(
                orow[0:1, 1536:2048], orow[0:1, 512:1024], 1.0 / D),
                reads=[bor], writes=[bor], tag=f"{tag}_var{n}")
        P.op("vector", lambda e: e.tensor_scalar_add(
            orow[0:1, 1536:2048], orow[0:1, 1536:2048], eps),
            reads=[bor], writes=[bor], tag=f"{tag}_eps{n}")
        P.op("scalar", lambda e: e.activation(
            out=orow[0:1, 512:1024], in_=orow[0:1, 1536:2048], func=AF.Sqrt),
            reads=[bor], writes=[bor], tag=f"{tag}_sqrt{n}")
        P.op("vector", lambda e: e.reciprocal(
            orow[0:1, 1536:2048], orow[0:1, 512:1024]),
            reads=[bor], writes=[bor], tag=f"{tag}_rcp{n}")
        if kind == "ln":
            P.op("tensor", lambda e: e.matmul(
                p_bc[:, 0:512], ones_row32, orow[0:1, 1024:1536],
                start=True, stop=True),
                reads=[bor, ctx.cbuf], writes=[bbc], tag=f"{tag}_bcm{n}")
        P.op("tensor", lambda e: e.matmul(
            p_bc[:, 512:1024], ones_row32, orow[0:1, 1536:2048],
            start=True, stop=True),
            reads=[bor, ctx.cbuf], writes=[bbc], tag=f"{tag}_bcr{n}")
        for c in range(DCH):
            if kind == "ln":
                P.op("vector", lambda e, c=c, i=i: e.tensor_sub(
                    x_t[i][:, c * 512:(c + 1) * 512],
                    x_t[i][:, c * 512:(c + 1) * 512], p_bc[:, 0:512]),
                    reads=[bx, bbc], writes=[bx], tag=f"{tag}_sub{n}_{c}")
            tgt32 = out32_t[i] if out32_t is not None else None
            if tgt32 is not None:
                P.op("vector", lambda e, c=c, i=i: e.tensor_mul(
                    out32_t[i][:, c * 512:(c + 1) * 512],
                    x_t[i][:, c * 512:(c + 1) * 512], p_bc[:, 512:1024]),
                    reads=[bx, bbc], writes=[bo32], tag=f"{tag}_mul{n}_{c}")
                src, bsrc = out32_t[i], bo32
            else:
                src, bsrc = x_t[i], bx
                P.op("vector", lambda e, c=c, i=i: e.tensor_mul(
                    x_t[i][:, c * 512:(c + 1) * 512],
                    x_t[i][:, c * 512:(c + 1) * 512], p_bc[:, 512:1024]),
                    reads=[bx, bbc], writes=[bx], tag=f"{tag}_mul{n}_{c}")
            if w_tiled_dram is not None:
                P.op("vector", lambda e, c=c, s=src: e.tensor_scalar_mul(
                    s[:, c * 512:(c + 1) * 512], s[:, c * 512:(c + 1) * 512],
                    wln[:, c:c + 1]),
                    reads=[bsrc, bw], writes=[bsrc], tag=f"{tag}_w{n}_{c}")
            P.op("vector", lambda e, c=c, i=i, s=src: e.tensor_copy(
                out_t[i][:, c * 512:(c + 1) * 512], s[:, c * 512:(c + 1) * 512]),
                reads=[bsrc], writes=[bo], tag=f"{tag}_obf{n}_{c}")
        for oi, (od, odt) in enumerate(outs):
            srcsb = out_t[i] if odt == BF16 else out32_t[i]
            srcb = bo if odt == BF16 else bo32
            P.op("sync", lambda e, n=n, s=srcsb, od=od: e.dma_start(
                out=od.rearrange("(c p) t -> p c t", p=128)[:, :, n * 512:(n + 1) * 512],
                in_=s.ap().rearrange("p (c t) -> p c t", c=DCH)),
                reads=[srcb], dma=True, tag=f"{tag}_st{n}_{oi}")



from contextlib import ExitStack


def build_launch(dram_specs, const_names, phase_fns):
    nc = bass.Bass("TRN2", target_bir_lowering=False, debug=False)
    draps = {}
    for nm, (shape, dtype, kind) in dram_specs.items():
        draps[nm] = nc.dram_tensor(nm, list(shape), dtype, kind=kind).ap()
    with ExitStack() as es:
        alloc = Alloc(nc, es)
        P = Prog(nc)
        ctx = Ctx(nc, P, alloc)
        if const_names:
            cd = {nm: draps["c_" + nm] for nm in const_names}
            emit_load_constants(ctx, cd, const_names)
        for f in phase_fns:
            f(ctx, draps)
        block = es.enter_context(nc.Block())
        P.finalize(block, es, nc)
    return nc


_NC_CACHE = {}


def _run(cache_key, dram_specs, const_names, phase_fns, in_maps):
    if cache_key not in _NC_CACHE:
        _NC_CACHE[cache_key] = build_launch(dram_specs, const_names, phase_fns)
    nc = _NC_CACHE[cache_key]
    res = run_bass_kernel_spmd(nc, in_maps, core_ids=list(range(N_CORES)))
    return res.results


CONST_HOST = None


def get_const_host():
    global CONST_HOST
    if CONST_HOST is None:
        c = host_constants()
        CONST_HOST = {k[2:]: v.astype(np.float32) for k, v in c.items()}
        CONST_HOST["ident"] = np.eye(128, dtype=np.float32)
    return CONST_HOST


def const_specs(names):
    ch = get_const_host()
    return {("c_" + n): (ch[n].shape, F32, "ExternalInput") for n in names}


def const_inmaps(names):
    ch = get_const_host()
    return {("c_" + n): ch[n] for n in names}



def run_ln0(x_fm_cores):
    names = ["ones_row", "ones_col"]
    specs = {
        "x_fm": ((D, T), F32, "ExternalInput"),
        "xn0": ((D, T), BF16, "ExternalOutput"),
        **const_specs(names),
    }

    def ph(ctx, dr):
        emit_ln(ctx, dr["x_fm"], [(dr["xn0"], BF16)], kind="ln", tag="ln0")

    im = [dict(x_fm=x, **const_inmaps(names)) for x in x_fm_cores]
    res = _run("ln0", specs, names, [ph], im)
    return [r["xn0"] for r in res]


def run_mlp(xn_cores, w1, b1, w2, b2, mode, tag):
    names = ["ones_row", "ones_col"]
    out_spec = ((D, T), BF16) if mode == "fm" else ((T, 16 * 65), BF16)
    specs = {
        "xn": ((D, T), BF16, "ExternalInput"),
        "w1": ((D, 4096), F32, "ExternalInput"),
        "b1": ((128, 32), F32, "ExternalInput"),
        "w2": ((4096, D), F32, "ExternalInput"),
        "b2": ((128, 8), F32, "ExternalInput"),
        "out": (out_spec[0], out_spec[1], "ExternalOutput"),
        **const_specs(names),
    }
    if mode != "fm":
        specs["b2row"] = ((1, D), F32, "ExternalInput")

    def ph(ctx, dr):
        emit_mlp(ctx, dr["xn"], dr["w1"], dr["b1"], dr["w2"], dr["b2"],
                 mode, dr["out"], b2row_dram=dr.get("b2row"), tag="mlp")

    base = dict(w1=np.ascontiguousarray(w1), b1=_tile_bias(b1),
                w2=np.ascontiguousarray(w2), b2=_tile_bias(b2),
                **const_inmaps(names))
    if mode != "fm":
        base["b2row"] = np.ascontiguousarray(b2.reshape(1, D))
    im = [dict(xn=xn, **base) for xn in xn_cores]
    res = _run(("mlp", mode), specs, names, [ph], im)
    return [r["out"] for r in res]


def emit_linear(ctx, x_dram, w_dram, b_dram, out_mode, out_dram, din, dout,
                act=None, out_dtype=BF16, b_row_dram=None, x2_dram=None,
                res_drams=(), aug=False, tag="lin"):
    nc, P, a = ctx.nc, ctx.P, ctx.alloc
    KCH = din // 128
    MCH = dout // 128
    w_sb = a.sb(f"{tag}_w", [128, KCH * dout], BF16)
    stage = [a.sb(f"{tag}_st{i}", [128, 2048], F32) for i in range(2)]
    stageb = [ctx.buf(f"{tag}_st{i}") for i in range(2)]
    b_w = ctx.buf(f"{tag}_w")

    cw = min(2048, dout)
    j = 0
    for k in range(KCH):
        for half in range(dout // cw):
            st, stb = stage[j % 2], stageb[j % 2]
            j += 1
            P.op("sync", lambda e, k=k, h=half, st=st: e.dma_start(
                out=st[:, 0:cw], in_=w_dram[k * 128:(k + 1) * 128,
                                            h * cw:(h + 1) * cw]),
                writes=[stb], dma=True, tag=f"{tag}_wld{k}_{half}")
            P.op("vector", lambda e, k=k, h=half, st=st: e.tensor_copy(
                w_sb[:, k * dout + h * cw: k * dout + (h + 1) * cw], st[:, 0:cw]),
                reads=[stb], writes=[b_w], tag=f"{tag}_wc{k}_{half}")

    if b_dram is not None and out_mode == "fm":
        b_sb = a.sb(f"{tag}_b", [128, MCH], F32)
        b_b = ctx.buf(f"{tag}_b")
        P.op("sync", lambda e: e.dma_start(out=b_sb[:], in_=b_dram),
             writes=[b_b], dma=True, tag=f"{tag}_bld")
    else:
        b_sb, b_b = None, None

    x_t = [a.sb(f"{tag}_x{i}", [128, KCH * 512], BF16) for i in range(2)]
    if x2_dram is not None:
        x2_t = [a.sb(f"{tag}_x2{i}", [128, KCH * 512], BF16) for i in range(2)]

    nres = len(res_drams)
    if nres:
        r_t = [a.sb(f"{tag}_r{i}", [128, DCH * 512], F32) for i in range(nres)]

    tmp = a.sb(f"{tag}_tmp", [128, 512], F32)
    b_tmp = ctx.buf(f"{tag}_tmp")
    if out_mode == "fm":
        ot = a.sb(f"{tag}_o", [128, MCH * 512],
                  F32 if (out_dtype == F32 or nres) else BF16)
        b_ot = ctx.buf(f"{tag}_o")
    else:
        nh = dout // 64
        width = nh * 65 if aug else dout
        vaug = a.sb(f"{tag}_vaug", [128, width], BF16)
        b_vaug = ctx.buf(f"{tag}_vaug")
        P.op("vector", lambda e: e.memset(vaug[:], 1.0),
             writes=[b_vaug], tag=f"{tag}_vmemset")
        if b_row_dram is not None:
            b2r_sb = a.sb(f"{tag}_b2r", [1, dout], F32)
            b2rb_sb = a.sb(f"{tag}_b2rb", [1, dout], BF16)
            b2bc_sb = a.sb(f"{tag}_b2bc", [128, dout], F32)
            b_b2r = ctx.buf(f"{tag}_b2r")
            b_b2bc = ctx.buf(f"{tag}_b2bc")
            P.op("sync", lambda e: e.dma_start(out=b2r_sb[:], in_=b_row_dram),
                 writes=[b_b2r], dma=True, tag=f"{tag}_b2rld")
            P.op("vector", lambda e: e.tensor_copy(b2rb_sb[:], b2r_sb[:]),
                 reads=[b_b2r], writes=[b_b2r], tag=f"{tag}_b2rc")
    p_mm = [a.ps(f"{tag}_p{i}", [128, 512], F32) for i in range(4)]
    bp = [ctx.buf(f"{tag}_p{i}") for i in range(4)]
    if out_mode == "tm" and b_row_dram is not None:
        segw = min(512, dout)
        for seg in range(dout // segw):
            P.op("tensor", lambda e, s=seg: e.matmul(
                p_mm[3][:, 0:segw], ctx.cb16["ones_row"],
                b2rb_sb[0:1, s * segw:(s + 1) * segw], start=True, stop=True),
                reads=[b_b2r, ctx.cbuf], writes=[bp[3]], tag=f"{tag}_bbc{seg}")
            P.op("scalar", lambda e, s=seg: e.copy(
                b2bc_sb[:, s * segw:(s + 1) * segw], p_mm[3][:, 0:segw]),
                reads=[bp[3]], writes=[b_b2bc], tag=f"{tag}_bbcv{seg}")

    def evac_act(dst_ap, p_ap, bias_ap, bsrcs, bdst, t2):
        if act is None or act == "gelu" or act == "silu":
            func = {None: AF.Identity, "gelu": AF.Gelu, "silu": AF.Silu}[act]
            if bias_ap is not None:
                P.op("scalar", lambda e: e.activation(
                    out=dst_ap, in_=p_ap, func=func, bias=bias_ap),
                    reads=bsrcs, writes=[bdst], tag=t2)
            else:
                P.op("scalar", lambda e: e.activation(
                    out=dst_ap, in_=p_ap, func=func),
                    reads=bsrcs, writes=[bdst], tag=t2)
        elif act == "elu1":
            P.op("vector", lambda e: e.tensor_scalar_min(tmp[:], p_ap, 0.0),
                 reads=bsrcs, writes=[b_tmp], tag=t2 + "_min")
            P.op("scalar", lambda e: e.activation(
                out=tmp[:], in_=tmp[:], func=AF.Exp),
                reads=[b_tmp], writes=[b_tmp], tag=t2 + "_exp")
            P.op("vector", lambda e: e.scalar_tensor_tensor(
                dst_ap, p_ap, 0.0, tmp[:], ALU.max, ALU.add),
                reads=bsrcs + [b_tmp], writes=[bdst], tag=t2 + "_add")
        else:
            raise ValueError(act)

    for n in range(NT):
        i = n % 2
        b_x = ctx.buf(f"{tag}_x{i}")
        P.op("sync", lambda e, n=n, i=i: e.dma_start(
            out=x_t[i].ap().rearrange("p (c t) -> p c t", c=KCH),
            in_=x_dram.rearrange("(c p) t -> p c t", p=128)[:, :, n * 512:(n + 1) * 512]),
            writes=[b_x], dma=True, tag=f"{tag}_xld{n}")
        if x2_dram is not None:
            b_x2 = ctx.buf(f"{tag}_x2{i}")
            P.op("sync", lambda e, n=n, i=i: e.dma_start(
                out=x2_t[i].ap().rearrange("p (c t) -> p c t", c=KCH),
                in_=x2_dram.rearrange("(c p) t -> p c t", p=128)[:, :, n * 512:(n + 1) * 512]),
                writes=[b_x2], dma=True, tag=f"{tag}_x2ld{n}")
            P.op("vector", lambda e, i=i: e.tensor_mul(
                x_t[i][:], x_t[i][:], x2_t[i][:]),
                reads=[b_x, b_x2], writes=[b_x], tag=f"{tag}_xmul{n}")
        if nres:
            for ri, rd in enumerate(res_drams):
                b_r = ctx.buf(f"{tag}_r{ri}")
                P.op("sync", lambda e, n=n, rd=rd, ri=ri: e.dma_start(
                    out=r_t[ri].ap().rearrange("p (c t) -> p c t", c=DCH),
                    in_=rd.rearrange("(c p) t -> p c t", p=128)[:, :, n * 512:(n + 1) * 512]),
                    writes=[b_r], dma=True, tag=f"{tag}_rld{n}_{ri}")

        if out_mode == "fm":
            for m in range(MCH):
                bk = m % 3
                for k in range(KCH):
                    P.op("tensor", lambda e, m=m, k=k, i=i, bk=bk: e.matmul(
                        p_mm[bk][:], w_sb[:, k * dout + m * 128: k * dout + (m + 1) * 128],
                        x_t[i][:, k * 512:(k + 1) * 512],
                        start=(k == 0), stop=(k == KCH - 1)),
                        reads=[b_w, b_x], writes=[bp[bk]],
                        tag=f"{tag}_mm{n}_{m}_{k}")
                bias_ap = b_sb[:, m:m + 1] if b_sb is not None else None
                rsrc = [bp[bk]] + ([b_b] if b_sb is not None else [])
                evac_act(ot[:, m * 512:(m + 1) * 512], p_mm[bk][:], bias_ap,
                         rsrc, b_ot, f"{tag}_ev{n}_{m}")
            if nres:
                for ri in range(nres):
                    b_r = ctx.buf(f"{tag}_r{ri}")
                    P.op("vector", lambda e, ri=ri: e.tensor_add(
                        ot[:], ot[:], r_t[ri][:]),
                        reads=[ctx.buf(f"{tag}_r{ri}"), b_ot], writes=[b_ot],
                        tag=f"{tag}_res{n}_{ri}")
            ot_is_f32 = (out_dtype == F32 or nres)
            store_q = "gpsimd" if (ot_is_f32 and out_dtype == BF16) else "sync"
            P.op(store_q, lambda e, n=n: e.dma_start(
                out=out_dram.rearrange("(c p) t -> p c t", p=128)[:, :, n * 512:(n + 1) * 512],
                in_=ot.ap().rearrange("p (c t) -> p c t", c=MCH)),
                reads=[b_ot], dma=True, tag=f"{tag}_ost{n}")
        else:
            segw = min(512, dout)
            for msub in range(4):
                for n2 in range(dout // segw):
                    bk = n2 % 2
                    for k in range(KCH):
                        P.op("tensor", lambda e, msub=msub, n2=n2, k=k, bk=bk, i=i: e.matmul(
                            p_mm[bk][:, 0:segw],
                            x_t[i][:, k * 512 + msub * 128: k * 512 + (msub + 1) * 128],
                            w_sb[:, k * dout + n2 * segw: k * dout + (n2 + 1) * segw],
                            start=(k == 0), stop=(k == KCH - 1)),
                            reads=[b_w, b_x], writes=[bp[bk]],
                            tag=f"{tag}_tmm{n}_{msub}_{n2}_{k}")
                    nh_seg = segw // 64
                    if aug:
                        dst = vaug.ap().rearrange("p (h s) -> p h s", s=65)[
                            :, n2 * nh_seg:(n2 + 1) * nh_seg, 0:64]
                    else:
                        dst = vaug[:, n2 * segw:(n2 + 1) * segw]
                    if act == "elu1":
                        evac_act(dst, p_mm[bk][:, 0:segw], None, [bp[bk]],
                                 b_vaug, f"{tag}_tev{n}_{msub}_{n2}")
                    elif b_row_dram is not None:
                        P.op("vector", lambda e, n2=n2, bk=bk, dst=dst: e.tensor_add(
                            dst,
                            p_mm[bk].ap()[:, 0:segw].rearrange("p (h s) -> p h s", s=64)
                            if aug else p_mm[bk][:, 0:segw],
                            b2bc_sb.ap()[:, n2 * segw:(n2 + 1) * segw].rearrange(
                                "p (h s) -> p h s", s=64)
                            if aug else b2bc_sb[:, n2 * segw:(n2 + 1) * segw]),
                            reads=[bp[bk], b_b2bc], writes=[b_vaug],
                            tag=f"{tag}_tev{n}_{msub}_{n2}")
                    else:
                        if aug:
                            src = p_mm[bk].ap()[:, 0:segw].rearrange(
                                "p (h s) -> p h s", s=64)
                        else:
                            src = p_mm[bk][:, 0:segw]
                        P.op("vector", lambda e, dst=dst, src=src: e.tensor_copy(
                            dst, src),
                            reads=[bp[bk]], writes=[b_vaug],
                            tag=f"{tag}_tev{n}_{msub}_{n2}")
                P.op("sync", lambda e, n=n, msub=msub: e.dma_start(
                    out=out_dram[n * 512 + msub * 128: n * 512 + (msub + 1) * 128, :],
                    in_=vaug[:]),
                    reads=[b_vaug], dma=True, tag=f"{tag}_tst{n}_{msub}")


def emit_attn(ctx, q_dram, k_dram, k_rep, vaug_dram, nheads, kvheads,
              proj_w_dram, proj_b_dram, out_dram, res_drams=(), tag="attn"):
    nc, P, a = ctx.nc, ctx.P, ctx.alloc
    W = kvheads * 65
    k_sb = a.sb(f"{tag}_k", [128, DCH * T], BF16)
    v_sb = a.sb(f"{tag}_v", [128, 32 * W], BF16)
    q_t = [a.sb(f"{tag}_q0", [128, DCH * 512], BF16)]
    pt_sb = a.sb(f"{tag}_pt", [128, 4 * 1024], BF16)
    cneg_sb = a.sb(f"{tag}_cneg", [128, nheads], F32)
    mx_sb = a.sb(f"{tag}_mx", [128, 2], F32)
    csc_sb = a.sb(f"{tag}_csc", [2, 1], F32)
    trow_sb = a.sb(f"{tag}_trow", [2, 128], F32)
    diag2_sb = a.sb(f"{tag}_diag", [2, 2], F32)
    srow_sb = a.sb(f"{tag}_srow", [1, 1024], F32)
    rbc_sb = a.sb(f"{tag}_rbc", [64, 1024], F32)
    ao_sb = a.sb(f"{tag}_ao", [128, DCH * 512], BF16)
    pw_sb = a.sb(f"{tag}_pw", [128, DCH * D], BF16)
    pb_sb = a.sb(f"{tag}_pb", [128, DCH], F32)
    stage = [a.sb(f"{tag}_st{i}", [128, 1024], F32) for i in range(2)]
    stageb = [ctx.buf(f"{tag}_st{i}") for i in range(2)]
    ot = a.sb(f"{tag}_ot", [128, 2 * 512], F32)
    r_t = [a.sb(f"{tag}_r{i}", [128, 2 * 512], F32) for i in range(len(res_drams))]

    p_big = [a.ps(f"{tag}_pb{i}", [128, 1024], F32) for i in range(2)]
    p_av = [a.ps(f"{tag}_pav{i}", [65, 512], F32) for i in range(2)]
    p_bc = a.ps(f"{tag}_pbc", [128, 512], F32)
    bbg = [ctx.buf(f"{tag}_pb{i}") for i in range(2)]
    bav = [ctx.buf(f"{tag}_pav{i}") for i in range(2)]
    bbc = ctx.buf(f"{tag}_pbc")
    b_k = ctx.buf(f"{tag}_k")
    b_v = ctx.buf(f"{tag}_v")
    b_pt = [ctx.buf(f"{tag}_pt{i}") for i in range(4)]
    b_cn = ctx.buf(f"{tag}_cneg")
    b_mx = ctx.buf(f"{tag}_mx")
    b_csc = ctx.buf(f"{tag}_csc")
    b_trow = ctx.buf(f"{tag}_trow")
    b_diag = ctx.buf(f"{tag}_diag")
    b_srow = ctx.buf(f"{tag}_srow")
    b_rbc = ctx.buf(f"{tag}_rbc")
    b_ao = ctx.buf(f"{tag}_ao")
    b_pw = ctx.buf(f"{tag}_pw")
    b_pb2 = ctx.buf(f"{tag}_pbias")
    b_ot = ctx.buf(f"{tag}_ot")

    if not k_rep:
        for c in range(DCH):
            P.op("sync", lambda e, c=c: e.dma_start(
                out=k_sb[:, c * T:(c + 1) * T],
                in_=k_dram[c * 128:(c + 1) * 128, :]),
                writes=[b_k], dma=True, tag=f"{tag}_kld{c}")
    else:
        for c in range(DCH):
            j = c // 2
            for half in range(2):
                P.op("sync", lambda e, c=c, j=j, half=half: e.dma_start(
                    out=k_sb[half * 64:(half + 1) * 64, c * T:(c + 1) * T],
                    in_=k_dram[j * 64:(j + 1) * 64, :]),
                    writes=[b_k], dma=True, tag=f"{tag}_kld{c}_{half}")
    P.op("sync", lambda e: e.dma_start(
        out=v_sb.ap().rearrange("p (t w) -> p t w", w=W),
        in_=vaug_dram.rearrange("(t p) w -> p t w", p=128)),
        writes=[b_v], dma=True, tag=f"{tag}_vld")
    for k in range(DCH):
        st, stb = stage[k % 2], stageb[k % 2]
        P.op("sync", lambda e, k=k, st=st: e.dma_start(
            out=st[:, 0:1024], in_=proj_w_dram[k * 128:(k + 1) * 128, :]),
            writes=[stb], dma=True, tag=f"{tag}_pwld{k}")
        P.op("vector", lambda e, k=k, st=st: e.tensor_copy(
            pw_sb[:, k * D:(k + 1) * D], st[:, 0:1024]),
            reads=[stb], writes=[b_pw], tag=f"{tag}_pwc{k}")
    P.op("sync", lambda e: e.dma_start(out=pb_sb[:], in_=proj_b_dram),
         writes=[b_pb2], dma=True, tag=f"{tag}_pbld")

    ones_row32 = ctx.cf32["ones_row"]
    ones_all32 = ctx.cf32["ones_all"]
    ident32 = ctx.cf32["ident"]
    first_done = [False] * nheads

    for qt in range(NT):
        i = 0
        b_q = ctx.buf(f"{tag}_q0")
        P.op("sync", lambda e, qt=qt: e.dma_start(
            out=q_t[0].ap().rearrange("p (c t) -> p c t", c=DCH),
            in_=q_dram.rearrange("(c p) t -> p c t", p=128)[:, :, qt * 512:(qt + 1) * 512]),
            writes=[b_q], dma=True, tag=f"{tag}_qld{qt}")
        for h in range(nheads):
            c = h // 2
            pb0 = (h % 2) * 64
            kv = h * kvheads // nheads
            av = p_av[h % 2]
            bavh = bav[h % 2]
            for tp in range(16):
                ps = p_big[tp % 2]
                bps = bbg[tp % 2]
                for u in range(2):
                    t = 2 * tp + u
                    P.op("tensor", lambda e, c=c, pb0=pb0, t=t, u=u, ps=ps: e.matmul(
                        ps[:, u * 512:(u + 1) * 512],
                        k_sb[pb0:pb0 + 64, c * T + t * 128: c * T + (t + 1) * 128],
                        q_t[0][pb0:pb0 + 64, c * 512:(c + 1) * 512],
                        start=True, stop=True),
                        reads=[b_k, b_q], writes=[bps],
                        tag=f"{tag}_smm{qt}_{h}_{t}")
                if not first_done[h]:
                    first_done[h] = True
                    P.op("vector", lambda e, ps=ps: e.reduce_max(
                        mx_sb[:, 0:1], ps[:], axis=AX.X),
                        reads=[bps], writes=[b_mx], tag=f"{tag}_cm1_{h}")
                    P.op("vector", lambda e: e.tensor_copy(
                        mx_sb[:, 1:2], mx_sb[:, 0:1]),
                        reads=[b_mx], writes=[b_mx], tag=f"{tag}_cm1b_{h}")
                    P.op("tensor", lambda e: e.transpose(
                        p_bc[0:2, 0:128], mx_sb[:, 0:2], ident32),
                        reads=[b_mx, ctx.cbuf], writes=[bbc], tag=f"{tag}_ctr_{h}")
                    P.op("scalar", lambda e: e.copy(trow_sb[:], p_bc[0:2, 0:128]),
                        reads=[bbc], writes=[b_trow], tag=f"{tag}_cte_{h}")
                    P.op("vector", lambda e: e.reduce_max(
                        csc_sb[0:2, 0:1], trow_sb[:], axis=AX.X),
                        reads=[b_trow], writes=[b_csc], tag=f"{tag}_cm2_{h}")
                    P.op("vector", lambda e: e.tensor_scalar_mul(
                        csc_sb[0:2, 0:1], csc_sb[0:2, 0:1], -0.125),
                        reads=[b_csc], writes=[b_csc], tag=f"{tag}_cns_{h}")
                    P.op("vector", lambda e: e.tensor_mul(
                        diag2_sb[:], csc_sb[0:2, 0:1].to_broadcast([2, 2]),
                        ident32[0:2, 0:2]),
                        reads=[b_csc, ctx.cbuf], writes=[b_diag],
                        tag=f"{tag}_cdg_{h}")
                    P.op("tensor", lambda e: e.matmul(
                        p_bc[:, 0:2], ones_all32[0:2, 0:128], diag2_sb[:],
                        start=True, stop=True),
                        reads=[b_diag, ctx.cbuf], writes=[bbc], tag=f"{tag}_cbc_{h}")
                    P.op("scalar", lambda e, h=h: e.copy(
                        cneg_sb[:, h:h + 1], p_bc[:, 0:2][:, 0:1]),
                        reads=[bbc], writes=[b_cn], tag=f"{tag}_cev_{h}")
                slot = tp % 4
                P.op("scalar", lambda e, h=h, ps=ps, slot=slot: e.activation(
                    out=pt_sb[:, slot * 1024:(slot + 1) * 1024], in_=ps[:],
                    func=AF.Exp, bias=cneg_sb[:, h:h + 1], scale=0.125),
                    reads=[bps, b_cn], writes=[b_pt[slot]],
                    tag=f"{tag}_exp{qt}_{h}_{tp}")
                for u in range(2):
                    t = 2 * tp + u
                    P.op("tensor", lambda e, t=t, kv=kv, u=u, slot=slot, av=av, tp=tp: e.matmul(
                        av[:],
                        v_sb[:, t * W + kv * 65: t * W + kv * 65 + 65],
                        pt_sb[:, slot * 1024 + u * 512: slot * 1024 + (u + 1) * 512],
                        start=(tp == 0 and u == 0), stop=(tp == 15 and u == 1)),
                        reads=[b_v, b_pt[slot]], writes=[bavh],
                        tag=f"{tag}_av{qt}_{h}_{t}")
            hh = h % 2
            P.op("scalar", lambda e, hh=hh, av=av: e.copy(
                srow_sb[0:1, hh * 512:(hh + 1) * 512], av.ap()[64:65, :]),
                reads=[bavh], writes=[b_srow], tag=f"{tag}_sev{qt}_{h}")
            P.op("vector", lambda e, hh=hh: e.reciprocal(
                srow_sb[0:1, hh * 512:(hh + 1) * 512],
                srow_sb[0:1, hh * 512:(hh + 1) * 512]),
                reads=[b_srow], writes=[b_srow], tag=f"{tag}_rcp{qt}_{h}")
            P.op("tensor", lambda e, hh=hh: e.matmul(
                p_bc[0:64, :], ones_row32[0:1, 0:64],
                srow_sb[0:1, hh * 512:(hh + 1) * 512],
                start=True, stop=True),
                reads=[b_srow, ctx.cbuf], writes=[bbc], tag=f"{tag}_rbc{qt}_{h}")
            P.op("scalar", lambda e, hh=hh: e.copy(
                rbc_sb[:, hh * 512:(hh + 1) * 512], p_bc[0:64, :]),
                reads=[bbc], writes=[b_rbc], tag=f"{tag}_rbe{qt}_{h}")
            P.op("vector", lambda e, c=c, hh=hh, av=av: e.tensor_mul(
                ao_sb[hh * 64:(hh + 1) * 64, c * 512:(c + 1) * 512],
                av.ap()[0:64, :], rbc_sb[:, hh * 512:(hh + 1) * 512]),
                reads=[bavh, b_rbc], writes=[b_ao],
                tag=f"{tag}_nrm{qt}_{h}")
        for mg in range(DCH // 2):
            for ri, rd in enumerate(res_drams):
                b_r = ctx.buf(f"{tag}_r{ri}")
                P.op("sync", lambda e, qt=qt, rd=rd, ri=ri, mg=mg: e.dma_start(
                    out=r_t[ri].ap().rearrange("p (c t) -> p c t", c=2),
                    in_=rd.rearrange("(c p) t -> p c t", p=128)[
                        :, mg * 2:(mg + 1) * 2, qt * 512:(qt + 1) * 512]),
                    writes=[b_r], dma=True, tag=f"{tag}_rld{qt}_{mg}_{ri}")
            for mm in range(2):
                m = mg * 2 + mm
                pbk = m % 2
                for k in range(DCH):
                    P.op("tensor", lambda e, m=m, k=k, pbk=pbk, mm=mm: e.matmul(
                        p_big[pbk][:, (mm % 2) * 512:(mm % 2 + 1) * 512],
                        pw_sb[:, k * D + m * 128: k * D + (m + 1) * 128],
                        ao_sb[:, k * 512:(k + 1) * 512],
                        start=(k == 0), stop=(k == DCH - 1)),
                        reads=[b_pw, b_ao], writes=[bbg[pbk]],
                        tag=f"{tag}_pmm{qt}_{m}_{k}")
                P.op("scalar", lambda e, mm=mm, pbk=pbk, m=m: e.activation(
                    out=ot[:, mm * 512:(mm + 1) * 512],
                    in_=p_big[pbk][:, (mm % 2) * 512:(mm % 2 + 1) * 512],
                    func=AF.Identity, bias=pb_sb[:, m:m + 1]),
                    reads=[bbg[pbk], b_pb2], writes=[b_ot], tag=f"{tag}_pev{qt}_{m}")
            for ri in range(len(res_drams)):
                P.op("vector", lambda e, ri=ri: e.tensor_add(
                    ot[:], ot[:], r_t[ri][:]),
                    reads=[ctx.buf(f"{tag}_r{ri}"), b_ot], writes=[b_ot],
                    tag=f"{tag}_pres{qt}_{mg}_{ri}")
            P.op("sync", lambda e, qt=qt, mg=mg: e.dma_start(
                out=out_dram.rearrange("(c p) t -> p c t", p=128)[
                    :, mg * 2:(mg + 1) * 2, qt * 512:(qt + 1) * 512],
                in_=ot.ap().rearrange("p (c t) -> p c t", c=2)),
                reads=[b_ot], dma=True, tag=f"{tag}_ost{qt}_{mg}")


def run_attn(q_cores, k_cores, vaug_cores, proj_w, proj_b, nheads, kvheads,
             k_rep, res_cores=None, cache_key="attn0"):
    names = ["ones_row", "ones_col", "ident", "ones_all"]
    kshape = (256, T) if k_rep else (D, T)
    specs = {
        "q": ((D, T), BF16, "ExternalInput"),
        "k": (kshape, BF16, "ExternalInput"),
        "vaug": ((T, kvheads * 65), BF16, "ExternalInput"),
        "pw": ((D, D), F32, "ExternalInput"),
        "pb": ((128, 8), F32, "ExternalInput"),
        "out": ((D, T), F32, "ExternalOutput"),
        **const_specs(names),
    }
    if res_cores is not None:
        specs["res0"] = ((D, T), F32, "ExternalInput")

    def ph(ctx, dr):
        emit_attn(ctx, dr["q"], dr["k"], k_rep, dr["vaug"], nheads, kvheads,
                  dr["pw"], dr["pb"], dr["out"],
                  res_drams=[dr["res0"]] if res_cores is not None else (),
                  tag="at")

    base = dict(pw=np.ascontiguousarray(proj_w), pb=_tile_bias(proj_b),
                **const_inmaps(names))
    im = []
    for ci in range(N_CORES):
        m = dict(q=q_cores[ci], k=k_cores[ci], vaug=vaug_cores[ci], **base)
        if res_cores is not None:
            m["res0"] = res_cores[ci]
        im.append(m)
    res = _run(cache_key, specs, names, [ph], im)
    return [r["out"] for r in res]


def emit_add(ctx, in_drams, out_dram, out_dtype=F32, tag="add"):
    nc, P, a = ctx.nc, ctx.P, ctx.alloc
    ts = [a.sb(f"{tag}_t{i}", [128, DCH * 512], F32) for i in range(2)]
    acc = a.sb(f"{tag}_acc", [128, DCH * 512], out_dtype)
    b_acc = ctx.buf(f"{tag}_acc")
    for n in range(NT):
        for ii, ind in enumerate(in_drams):
            i = ii % 2
            b_t = ctx.buf(f"{tag}_t{i}")
            P.op("sync", lambda e, n=n, ind=ind, i=i: e.dma_start(
                out=ts[i].ap().rearrange("p (c t) -> p c t", c=DCH),
                in_=ind.rearrange("(c p) t -> p c t", p=128)[:, :, n * 512:(n + 1) * 512]),
                writes=[b_t], dma=True, tag=f"{tag}_ld{n}_{ii}")
            if ii == 0:
                continue
            if ii == 1:
                P.op("vector", lambda e: e.tensor_add(acc[:], ts[0][:], ts[1][:]),
                     reads=[ctx.buf(f"{tag}_t0"), ctx.buf(f"{tag}_t1")],
                     writes=[b_acc], tag=f"{tag}_a{n}_{ii}")
            else:
                P.op("vector", lambda e, i=i: e.tensor_add(acc[:], acc[:], ts[i][:]),
                     reads=[b_acc, b_t], writes=[b_acc], tag=f"{tag}_a{n}_{ii}")
        P.op("sync", lambda e, n=n: e.dma_start(
            out=out_dram.rearrange("(c p) t -> p c t", p=128)[:, :, n * 512:(n + 1) * 512],
            in_=acc.ap().rearrange("p (c t) -> p c t", c=DCH)),
            reads=[b_acc], dma=True, tag=f"{tag}_st{n}")


def emit_ffn0_route(ctx, xn_dram, rwT_dram, cb_dram, cmb_dram, tag="rt"):
    nc, P, a = ctx.nc, ctx.P, ctx.alloc
    rw_sb = a.sb(f"{tag}_rw", [128, DCH * 16], BF16)
    rw32 = a.sb(f"{tag}_rw32", [128, DCH * 16], F32)
    cbr = a.sb(f"{tag}_cbr", [1, 16], F32)
    cbb = a.sb(f"{tag}_cbb", [128, 16], F32)
    xn_t = a.sb(f"{tag}_xn", [128, DCH * 512], BF16)
    pr_sb = a.sb(f"{tag}_pr", [128, 16], F32)
    sbv = a.sb(f"{tag}_sbv", [128, 16], F32)
    scratch = a.sb(f"{tag}_scr", [128, 8], F32)
    scratch2 = a.sb(f"{tag}_sc2", [128, 8], F32)
    pairbuf = a.sb(f"{tag}_pb", [128, 24], F32)
    gsc = a.sb(f"{tag}_gsc", [128, 4], F32)
    gmask = a.sb(f"{tag}_gm", [128, 4], F32)
    sel = a.sb(f"{tag}_sel", [128, 16], F32)
    den = a.sb(f"{tag}_den", [128, 1], F32)
    cmb = a.sb(f"{tag}_cmb", [128, 16], BF16)
    cemT = a.sb(f"{tag}_cemT", [16, 512], BF16)
    p_lg = a.ps(f"{tag}_plg", [128, 16], F32)
    p_tr = a.ps(f"{tag}_ptr", [16, 128], BF16)
    bufs = {n: ctx.buf(f"{tag}_{n}") for n in
            ("rw", "rw32", "cbr", "cbb", "xn", "pr", "sbv", "scr", "sc2",
             "pb", "gsc", "gm", "sel", "den", "cmb", "cemT", "plg", "ptr")}
    identb = ctx.cb16["ident"]

    P.op("sync", lambda e: e.dma_start(
        out=rw32.ap().rearrange("p (c o) -> p c o", c=DCH),
        in_=rwT_dram.rearrange("(c p) o -> p c o", p=128)),
        writes=[bufs["rw32"]], dma=True, tag=f"{tag}_rwld")
    P.op("vector", lambda e: e.tensor_copy(rw_sb[:], rw32[:]),
         reads=[bufs["rw32"]], writes=[bufs["rw"]], tag=f"{tag}_rwc")
    P.op("sync", lambda e: e.dma_start(out=cbr[:], in_=cb_dram),
         writes=[bufs["cbr"]], dma=True, tag=f"{tag}_cbld")
    P.op("tensor", lambda e: e.matmul(
        p_lg[:], ctx.cf32["ones_row"], cbr[:], start=True, stop=True),
        reads=[bufs["cbr"], ctx.cbuf], writes=[bufs["plg"]], tag=f"{tag}_cbbm")
    P.op("scalar", lambda e: e.copy(cbb[:], p_lg[:]),
         reads=[bufs["plg"]], writes=[bufs["cbb"]], tag=f"{tag}_cbbe")

    for n in range(NT):
        P.op("sync", lambda e, n=n: e.dma_start(
            out=xn_t.ap().rearrange("p (c t) -> p c t", c=DCH),
            in_=xn_dram.rearrange("(c p) t -> p c t", p=128)[:, :, n * 512:(n + 1) * 512]),
            writes=[bufs["xn"]], dma=True, tag=f"{tag}_xnld{n}")
        for ms in range(4):
            for k in range(DCH):
                P.op("tensor", lambda e, ms=ms, k=k: e.matmul(
                    p_lg[:], xn_t[:, k * 512 + ms * 128: k * 512 + (ms + 1) * 128],
                    rw_sb[:, k * 16:(k + 1) * 16],
                    start=(k == 0), stop=(k == DCH - 1)),
                    reads=[bufs["xn"], bufs["rw"]], writes=[bufs["plg"]],
                    tag=f"{tag}_lmm{n}_{ms}_{k}")
            P.op("scalar", lambda e: e.activation(
                out=pr_sb[:], in_=p_lg[:], func=AF.Sigmoid),
                reads=[bufs["plg"]], writes=[bufs["pr"]], tag=f"{tag}_sig{n}_{ms}")
            P.op("vector", lambda e: e.tensor_add(sbv[:], pr_sb[:], cbb[:]),
                 reads=[bufs["pr"], bufs["cbb"]], writes=[bufs["sbv"]],
                 tag=f"{tag}_sb{n}_{ms}")
            def lane(ii):
                return sbv.ap().rearrange("p (g i) -> p i g", i=4)[
                    :, ii:ii + 1, :].rearrange("p i g -> p (i g)")
            pairs = [(0, 1), (0, 2), (0, 3), (1, 2), (1, 3), (2, 3)]
            for pi, (ii, jj) in enumerate(pairs):
                P.op("vector", lambda e, ii=ii, jj=jj, pi=pi: e.tensor_add(
                    pairbuf[:, pi * 4:(pi + 1) * 4], lane(ii), lane(jj)),
                    reads=[bufs["sbv"]], writes=[bufs["pb"]],
                    tag=f"{tag}_gp{n}_{ms}_{pi}")
            for pi in range(1, 6):
                dst = gsc[:] if pi == 5 else pairbuf[:, 0:4]
                bw = bufs["gsc"] if pi == 5 else bufs["pb"]
                P.op("vector", lambda e, pi=pi, dst=dst: e.tensor_max(
                    dst, pairbuf[:, 0:4], pairbuf[:, pi * 4:(pi + 1) * 4]),
                    reads=[bufs["pb"]], writes=[bw],
                    tag=f"{tag}_gmx{n}_{ms}_{pi}")
            P.op("vector", lambda e: e.memset(scratch[:], -1e30),
                 writes=[bufs["scr"]], tag=f"{tag}_gpad{n}_{ms}")
            P.op("vector", lambda e: e.tensor_copy(scratch[:, 0:4], gsc[:]),
                 reads=[bufs["gsc"]], writes=[bufs["scr"]],
                 tag=f"{tag}_gcp{n}_{ms}")
            P.op("vector", lambda e: e.max(out=scratch2[:], in_=scratch[:]),
                 reads=[bufs["scr"]], writes=[bufs["sc2"]],
                 tag=f"{tag}_g2{n}_{ms}")
            P.op("vector", lambda e: e.tensor_single_scalar(
                gmask[:], gsc[:], scratch2[:, 1:2], ALU.is_ge),
                reads=[bufs["gsc"], bufs["sc2"]], writes=[bufs["gm"]],
                tag=f"{tag}_gmk{n}_{ms}")
            for g in range(4):
                P.op("vector", lambda e, g=g: e.tensor_copy(
                    sel[:, g * 4:(g + 1) * 4],
                    gmask[:, g:g + 1].to_broadcast([128, 4])),
                    reads=[bufs["gm"]], writes=[bufs["sel"]],
                    tag=f"{tag}_gex{n}_{ms}_{g}")
            P.op("vector", lambda e: e.scalar_tensor_tensor(
                sbv[:], sbv[:], 100.0, sel[:], ALU.add, ALU.mult),
                reads=[bufs["sbv"], bufs["sel"]], writes=[bufs["sbv"]],
                tag=f"{tag}_msk{n}_{ms}")
            P.op("vector", lambda e: e.tensor_scalar_add(sbv[:], sbv[:], -100.0),
                 reads=[bufs["sbv"]], writes=[bufs["sbv"]],
                 tag=f"{tag}_msk2{n}_{ms}")
            P.op("vector", lambda e: e.max(out=scratch2[:], in_=sbv[:]),
                 reads=[bufs["sbv"]], writes=[bufs["sc2"]],
                 tag=f"{tag}_t8{n}_{ms}")
            P.op("vector", lambda e: e.tensor_single_scalar(
                sel[:], sbv[:], scratch2[:, 3:4], ALU.is_ge),
                reads=[bufs["sbv"], bufs["sc2"]], writes=[bufs["sel"]],
                tag=f"{tag}_sel{n}_{ms}")
            P.op("vector", lambda e: e.tensor_mul(sel[:], sel[:], pr_sb[:]),
                 reads=[bufs["sel"], bufs["pr"]], writes=[bufs["sel"]],
                 tag=f"{tag}_w{n}_{ms}")
            P.op("vector", lambda e: e.reduce_sum(den[:], sel[:], axis=AX.X),
                 reads=[bufs["sel"]], writes=[bufs["den"]],
                 tag=f"{tag}_den{n}_{ms}")
            P.op("vector", lambda e: e.tensor_scalar_add(den[:], den[:], 1e-20),
                 reads=[bufs["den"]], writes=[bufs["den"]],
                 tag=f"{tag}_dene{n}_{ms}")
            P.op("vector", lambda e: e.reciprocal(den[:], den[:]),
                 reads=[bufs["den"]], writes=[bufs["den"]],
                 tag=f"{tag}_denr{n}_{ms}")
            P.op("vector", lambda e: e.tensor_scalar_mul(cmb[:], sel[:], den[:]),
                 reads=[bufs["sel"], bufs["den"]], writes=[bufs["cmb"]],
                 tag=f"{tag}_cmb{n}_{ms}")
            P.op("tensor", lambda e: e.transpose(p_tr[:], cmb[:], identb),
                 reads=[bufs["cmb"], ctx.cbuf], writes=[bufs["ptr"]],
                 tag=f"{tag}_tr{n}_{ms}")
            P.op("scalar", lambda e, ms=ms: e.copy(
                cemT[:, ms * 128:(ms + 1) * 128], p_tr[:]),
                reads=[bufs["ptr"]], writes=[bufs["cemT"]],
                tag=f"{tag}_tre{n}_{ms}")
        P.op("sync", lambda e, n=n: e.dma_start(
            out=cmb_dram[:, n * 512:(n + 1) * 512], in_=cemT[:]),
            reads=[bufs["cemT"]], dma=True, tag=f"{tag}_st{n}")


def emit_ffn0_experts(ctx, xn_dram, cmb_dram, gate_dram, up_dram, down_dram,
                      experts, out_dram, accumulate, tag="ex"):
    nc, P, a = ctx.nc, ctx.P, ctx.alloc
    ne = len(experts)
    gw = a.sb(f"{tag}_gw", [128, ne * DCH * 256], BF16)
    uw = a.sb(f"{tag}_uw", [128, ne * DCH * 256], BF16)
    dw = a.sb(f"{tag}_dw", [128, ne * 2 * 1024], BF16)
    stage = a.sb(f"{tag}_st0", [128, 2048], F32)
    stb = ctx.buf(f"{tag}_st0")
    xn_t = a.sb(f"{tag}_xn", [128, DCH * 512], BF16)
    cem_t = a.sb(f"{tag}_cem", [16, 512], BF16)
    cbc = a.sb(f"{tag}_cbc", [128, 512], BF16)
    sg = a.sb(f"{tag}_sg", [128, 512], F32)
    h_all = a.sb(f"{tag}_h", [128, ne * 2 * 512], BF16)
    rout = a.sb(f"{tag}_ro", [128, 2 * 512], F32)
    p_gu = [a.ps(f"{tag}_pgu{i}", [128, 512], F32) for i in range(4)]
    p_d = [a.ps(f"{tag}_pd{i}", [128, 512], F32) for i in range(3)]
    p_cb = a.ps(f"{tag}_pcb", [128, 512], F32)
    b = {n: ctx.buf(f"{tag}_{n}") for n in
         ("gw", "uw", "dw", "xn", "cem", "cbc", "sg", "h", "ro", "pcb")}
    bgu = [ctx.buf(f"{tag}_pgu{i}") for i in range(4)]
    bd = [ctx.buf(f"{tag}_pd{i}") for i in range(3)]
    sel16 = ctx.cb16["sel16"]

    for ei, ex in enumerate(experts):
        for k in range(DCH):
            P.op("sync", lambda e, ex=ex, k=k: e.dma_start(
                out=stage[:, 0:256],
                in_=gate_dram[ex * 1024 + k * 128: ex * 1024 + (k + 1) * 128, :]),
                writes=[stb], dma=True, tag=f"{tag}_gld{ei}_{k}")
            P.op("vector", lambda e, ei=ei, k=k: e.tensor_copy(
                gw[:, (ei * DCH + k) * 256:(ei * DCH + k + 1) * 256],
                stage[:, 0:256]),
                reads=[stb], writes=[b["gw"]], tag=f"{tag}_gc{ei}_{k}")
            P.op("sync", lambda e, ex=ex, k=k: e.dma_start(
                out=stage[:, 256:512],
                in_=up_dram[ex * 1024 + k * 128: ex * 1024 + (k + 1) * 128, :]),
                writes=[stb], dma=True, tag=f"{tag}_uld{ei}_{k}")
            P.op("vector", lambda e, ei=ei, k=k: e.tensor_copy(
                uw[:, (ei * DCH + k) * 256:(ei * DCH + k + 1) * 256],
                stage[:, 256:512]),
                reads=[stb], writes=[b["uw"]], tag=f"{tag}_uc{ei}_{k}")
        for k2 in range(2):
            P.op("sync", lambda e, ex=ex, k2=k2: e.dma_start(
                out=stage[:, 1024:2048],
                in_=down_dram[ex * 256 + k2 * 128: ex * 256 + (k2 + 1) * 128, :]),
                writes=[stb], dma=True, tag=f"{tag}_dld{ei}_{k2}")
            P.op("vector", lambda e, ei=ei, k2=k2: e.tensor_copy(
                dw[:, (ei * 2 + k2) * 1024:(ei * 2 + k2 + 1) * 1024],
                stage[:, 1024:2048]),
                reads=[stb], writes=[b["dw"]], tag=f"{tag}_dc{ei}_{k2}")

    for n in range(NT):
        P.op("sync", lambda e, n=n: e.dma_start(
            out=xn_t.ap().rearrange("p (c t) -> p c t", c=DCH),
            in_=xn_dram.rearrange("(c p) t -> p c t", p=128)[:, :, n * 512:(n + 1) * 512]),
            writes=[b["xn"]], dma=True, tag=f"{tag}_xnld{n}")
        P.op("sync", lambda e, n=n: e.dma_start(
            out=cem_t[:], in_=cmb_dram[:, n * 512:(n + 1) * 512]),
            writes=[b["cem"]], dma=True, tag=f"{tag}_cmld{n}")
        for ei, ex in enumerate(experts):
            P.op("tensor", lambda e, ex=ex: e.matmul(
                p_cb[:], sel16[:, ex * 128:(ex + 1) * 128], cem_t[:],
                start=True, stop=True),
                reads=[b["cem"], ctx.cbuf], writes=[b["pcb"]],
                tag=f"{tag}_cbm{n}_{ei}")
            P.op("scalar", lambda e: e.copy(cbc[:], p_cb[:]),
                 reads=[b["pcb"]], writes=[b["cbc"]], tag=f"{tag}_cbe{n}_{ei}")
            for m in range(2):
                bkg, bku = m, 2 + m
                for k in range(DCH):
                    P.op("tensor", lambda e, ei=ei, m=m, k=k, bkg=bkg: e.matmul(
                        p_gu[bkg][:],
                        gw[:, (ei * DCH + k) * 256 + m * 128: (ei * DCH + k) * 256 + (m + 1) * 128],
                        xn_t[:, k * 512:(k + 1) * 512],
                        start=(k == 0), stop=(k == DCH - 1)),
                        reads=[b["gw"], b["xn"]], writes=[bgu[bkg]],
                        tag=f"{tag}_gmm{n}_{ei}_{m}_{k}")
                    P.op("tensor", lambda e, ei=ei, m=m, k=k, bku=bku: e.matmul(
                        p_gu[bku][:],
                        uw[:, (ei * DCH + k) * 256 + m * 128: (ei * DCH + k) * 256 + (m + 1) * 128],
                        xn_t[:, k * 512:(k + 1) * 512],
                        start=(k == 0), stop=(k == DCH - 1)),
                        reads=[b["uw"], b["xn"]], writes=[bgu[bku]],
                        tag=f"{tag}_umm{n}_{ei}_{m}_{k}")
                P.op("scalar", lambda e, bkg=bkg: e.activation(
                    out=sg[:], in_=p_gu[bkg][:], func=AF.Silu),
                    reads=[bgu[bkg]], writes=[b["sg"]],
                    tag=f"{tag}_silu{n}_{ei}_{m}")
                P.op("vector", lambda e, ei=ei, m=m, bku=bku: e.scalar_tensor_tensor(
                    h_all[:, (ei * 2 + m) * 512:(ei * 2 + m + 1) * 512],
                    p_gu[bku][:], 1.0, sg[:], ALU.mult, ALU.mult),
                    reads=[bgu[bku], b["sg"]], writes=[b["h"]],
                    tag=f"{tag}_hmul{n}_{ei}_{m}")
                P.op("vector", lambda e, ei=ei, m=m: e.tensor_mul(
                    h_all[:, (ei * 2 + m) * 512:(ei * 2 + m + 1) * 512],
                    h_all[:, (ei * 2 + m) * 512:(ei * 2 + m + 1) * 512], cbc[:]),
                    reads=[b["h"], b["cbc"]], writes=[b["h"]],
                    tag=f"{tag}_hcb{n}_{ei}_{m}")
        for mo in range(DCH):
            bk = mo % 3
            nacc = ne * 2
            jj = 0
            for ei in range(ne):
                for k2 in range(2):
                    P.op("tensor", lambda e, ei=ei, k2=k2, mo=mo, bk=bk, jj=jj: e.matmul(
                        p_d[bk][:],
                        dw[:, (ei * 2 + k2) * 1024 + mo * 128: (ei * 2 + k2) * 1024 + (mo + 1) * 128],
                        h_all[:, (ei * 2 + k2) * 512:(ei * 2 + k2 + 1) * 512],
                        start=(jj == 0), stop=(jj == nacc - 1)),
                        reads=[b["dw"], b["h"]], writes=[bd[bk]],
                        tag=f"{tag}_dmm{n}_{mo}_{ei}_{k2}")
                    jj += 1
            P.op("scalar", lambda e, mo=mo, bk=bk: e.copy(
                rout[:, (mo % 2) * 512:(mo % 2 + 1) * 512], p_d[bk][:]),
                reads=[bd[bk]], writes=[b["ro"]], tag=f"{tag}_dev{n}_{mo}")
            if mo % 2 == 1:
                q_ = "gpsimd" if accumulate else "sync"
                P.op(q_, lambda e, n=n, mo=mo: e.dma_start(
                    out=out_dram.rearrange("(c p) t -> p c t", p=128)[
                        :, mo - 1:mo + 1, n * 512:(n + 1) * 512],
                    in_=rout.ap().rearrange("p (c t) -> p c t", c=2),
                    **(dict(accum_op=ALU.add) if accumulate else {})),
                    reads=[b["ro"]], dma=True, tag=f"{tag}_ost{n}_{mo}")



def run_linear(x_cores, w, b, mode, din, dout, act=None, out_dtype=BF16,
               b_row=None, x2_cores=None, res_cores=None, aug=False,
               cache_key="lin"):
    names = ["ones_row", "ones_col"]
    if mode == "fm":
        oshape = (dout, T)
    else:
        nh = dout // 64
        oshape = (T, nh * 65) if aug else (T, dout)
    specs = {
        "x": ((din, T), BF16, "ExternalInput"),
        "w": ((din, dout), F32, "ExternalInput"),
        "out": (oshape, out_dtype, "ExternalOutput"),
        **const_specs(names),
    }
    if b is not None:
        specs["b"] = ((128, dout // 128), F32, "ExternalInput")
    if b_row is not None:
        specs["brow"] = ((1, dout), F32, "ExternalInput")
    if x2_cores is not None:
        specs["x2"] = ((din, T), BF16, "ExternalInput")
    if res_cores:
        for ri in range(len(res_cores)):
            specs[f"res{ri}"] = ((D, T), F32, "ExternalInput")

    def ph(ctx, dr):
        emit_linear(ctx, dr["x"], dr["w"], dr.get("b"), mode, dr["out"],
                    din, dout, act=act, out_dtype=out_dtype,
                    b_row_dram=dr.get("brow"), x2_dram=dr.get("x2"),
                    res_drams=[dr[f"res{ri}"] for ri in range(len(res_cores))]
                    if res_cores else (), aug=aug, tag="li")

    base = dict(w=np.ascontiguousarray(w), **const_inmaps(names))
    if b is not None:
        base["b"] = _tile_bias(b)
    if b_row is not None:
        base["brow"] = np.ascontiguousarray(b_row.reshape(1, dout))
    im = []
    for ci in range(N_CORES):
        m = dict(x=x_cores[ci], **base)
        if x2_cores is not None:
            m["x2"] = x2_cores[ci]
        if res_cores:
            for ri in range(len(res_cores)):
                m[f"res{ri}"] = res_cores[ri][ci]
        im.append(m)
    res = _run(cache_key, specs, names, [ph], im)
    return [r["out"] for r in res]


def run_ffn0_route(xn_cores, router_w, correction_bias):
    names = ["ones_row", "ident"]
    specs = {
        "xn": ((D, T), BF16, "ExternalInput"),
        "rwT": ((D, 16), F32, "ExternalInput"),
        "cb": ((1, 16), F32, "ExternalInput"),
        "cmb": ((16, T), BF16, "ExternalOutput"),
        **const_specs(names),
    }

    def ph(ctx, dr):
        emit_ffn0_route(ctx, dr["xn"], dr["rwT"], dr["cb"], dr["cmb"], tag="rt")

    base = dict(rwT=_t(router_w), cb=correction_bias.reshape(1, 16),
                **const_inmaps(names))
    im = [dict(xn=xn, **base) for xn in xn_cores]
    res = _run("ffn0route", specs, names, [ph], im)
    return [r["cmb"] for r in res]


def run_ffn0_experts(xn_cores, cmb_cores, gate_w, up_w, down_w, experts,
                     cache_key):
    names = ["sel16"]
    specs = {
        "xn": ((D, T), BF16, "ExternalInput"),
        "cmb": ((16, T), BF16, "ExternalInput"),
        "gate": ((16 * 1024, 256), F32, "ExternalInput"),
        "up": ((16 * 1024, 256), F32, "ExternalInput"),
        "down": ((16 * 256, 1024), F32, "ExternalInput"),
        "out": ((D, T), F32, "ExternalOutput"),
        **const_specs(names),
    }

    def ph(ctx, dr):
        emit_ffn0_experts(ctx, dr["xn"], dr["cmb"], dr["gate"], dr["up"],
                          dr["down"], experts, dr["out"], False, tag="ex")

    base = dict(gate=gate_w.reshape(16 * 1024, 256),
                up=up_w.reshape(16 * 1024, 256),
                down=down_w.reshape(16 * 256, 1024), **const_inmaps(names))
    im = [dict(xn=xn, cmb=cm, **base) for xn, cm in zip(xn_cores, cmb_cores)]
    res = _run(cache_key, specs, names, [ph], im)
    return [r["out"] for r in res]


def run_add(in_cores_list, cache_key, out_dtype=F32):
    names = []
    specs = {"out": ((D, T), out_dtype, "ExternalOutput")}
    ni = len(in_cores_list)
    for ii in range(ni):
        specs[f"in{ii}"] = ((D, T), F32, "ExternalInput")

    def ph(ctx, dr):
        emit_add(ctx, [dr[f"in{ii}"] for ii in range(ni)], dr["out"],
                 out_dtype=out_dtype, tag="add")

    im = []
    for ci in range(N_CORES):
        im.append({f"in{ii}": in_cores_list[ii][ci] for ii in range(ni)})
    res = _run(cache_key, specs, names, [ph], im)
    return [r["out"] for r in res]


def emit_linscan(ctx, pq_dram, pk_dram, pkt_dram, vaug_dram, y_dram,
                 tag="ls"):
    nc, P, a = ctx.nc, ctx.P, ctx.alloc
    pq_sb = [a.sb(f"{tag}_pq{j}", [64, T], BF16) for j in range(2)]
    pk_sb = [a.sb(f"{tag}_pk{j}", [64, T], BF16) for j in range(2)]
    pkt_sb = [a.sb(f"{tag}_pkt{j}", [64, 64 * 64], BF16) for j in range(2)]
    va_sb = [a.sb(f"{tag}_va{j}", [64, 64 * 65], BF16) for j in range(2)]
    at_sb = [a.sb(f"{tag}_at{j}", [64, 64], BF16) for j in range(2)]
    s32 = [a.sb(f"{tag}_s32{j}", [64, 65], F32) for j in range(2)]
    s_bf = [a.sb(f"{tag}_sbf{j}", [64, 65], BF16) for j in range(2)]
    yt_sb = [a.sb(f"{tag}_yt{j}", [64, T], F32) for j in range(2)]
    yo_sb = a.sb(f"{tag}_yo", [64, T], BF16)
    den_sb = a.sb(f"{tag}_den", [1, 2 * T], F32)
    rbc_sb = a.sb(f"{tag}_rbc", [64, 512], F32)
    p_a = [a.ps(f"{tag}_pa{j}", [64, 64], F32) for j in range(2)]
    p_k = [a.ps(f"{tag}_pkk{j}", [64, 65], F32) for j in range(2)]
    p_y = [a.ps(f"{tag}_py{j}", [65, 64], F32) for j in range(2)]
    p_bc = a.ps(f"{tag}_pbc", [64, 512], F32)
    bq = [ctx.buf(f"{tag}_pq{j}") for j in range(2)]
    bkf = [ctx.buf(f"{tag}_pk{j}") for j in range(2)]
    bkt = [ctx.buf(f"{tag}_pkt{j}") for j in range(2)]
    bva = [ctx.buf(f"{tag}_va{j}") for j in range(2)]
    bat = [ctx.buf(f"{tag}_at{j}") for j in range(2)]
    bs32 = [ctx.buf(f"{tag}_s32{j}") for j in range(2)]
    bsbf = [ctx.buf(f"{tag}_sbf{j}") for j in range(2)]
    byt = [ctx.buf(f"{tag}_yt{j}") for j in range(2)]
    byo = ctx.buf(f"{tag}_yo")
    bden = ctx.buf(f"{tag}_den")
    brbc = ctx.buf(f"{tag}_rbc")
    ba = [ctx.buf(f"{tag}_pa{j}") for j in range(2)]
    bk = [ctx.buf(f"{tag}_pkk{j}") for j in range(2)]
    by = [ctx.buf(f"{tag}_py{j}") for j in range(2)]
    bbc = ctx.buf(f"{tag}_pbc")
    mask = ctx.cb16["triu_mask"]

    for hp in range(8):
        for hh in range(2):
            h = 2 * hp + hh
            P.op("sync", lambda e, h=h, hh=hh: e.dma_start(
                out=pq_sb[hh][:], in_=pq_dram[h * 64:(h + 1) * 64, :]),
                writes=[bq[hh]], dma=True, tag=f"{tag}_pqld{h}")
            P.op("sync", lambda e, h=h, hh=hh: e.dma_start(
                out=pk_sb[hh][:], in_=pk_dram[h * 64:(h + 1) * 64, :]),
                writes=[bkf[hh]], dma=True, tag=f"{tag}_pkld{h}")
            P.op("sync", lambda e, h=h, hh=hh: e.dma_start(
                out=pkt_sb[hh].ap().rearrange("p (n d) -> p n d", d=64),
                in_=pkt_dram[:, h * 64:(h + 1) * 64].rearrange(
                    "(n p) d -> p n d", p=64)),
                writes=[bkt[hh]], dma=True, tag=f"{tag}_pktld{h}")
            P.op("sync", lambda e, h=h, hh=hh: e.dma_start(
                out=va_sb[hh].ap().rearrange("p (n d) -> p n d", d=65),
                in_=vaug_dram[:, h * 65:(h + 1) * 65].rearrange(
                    "(n p) d -> p n d", p=64)),
                writes=[bva[hh]], dma=True, tag=f"{tag}_vald{h}")
            P.op("vector", lambda e, hh=hh: e.memset(s32[hh][:], 0.0),
                 writes=[bs32[hh]], tag=f"{tag}_s0_{hp}_{hh}")
            P.op("vector", lambda e, hh=hh: e.memset(s_bf[hh][:], 0.0),
                 writes=[bsbf[hh]], tag=f"{tag}_sb0_{hp}_{hh}")
        for n in range(64):
            for hh in range(2):
                P.op("tensor", lambda e, hh=hh, n=n: e.matmul(
                    p_a[hh][:], pk_sb[hh][:, n * 64:(n + 1) * 64],
                    pq_sb[hh][:, n * 64:(n + 1) * 64],
                    start=True, stop=True),
                    reads=[bkf[hh], bq[hh]], writes=[ba[hh]],
                    tag=f"{tag}_amm{hp}_{n}_{hh}")
                P.op("vector", lambda e, hh=hh: e.scalar_tensor_tensor(
                    at_sb[hh][:], p_a[hh][:], 1.0, mask, ALU.mult, ALU.mult),
                    reads=[ba[hh], ctx.cbuf], writes=[bat[hh]],
                    tag=f"{tag}_amsk{hp}_{n}_{hh}")
                P.op("tensor", lambda e, hh=hh, n=n: e.matmul(
                    p_y[hh][:], s_bf[hh][:], pq_sb[hh][:, n * 64:(n + 1) * 64],
                    start=True, stop=False),
                    reads=[bsbf[hh], bq[hh]], writes=[by[hh]],
                    tag=f"{tag}_imm{hp}_{n}_{hh}")
                P.op("tensor", lambda e, hh=hh, n=n: e.matmul(
                    p_y[hh][:], va_sb[hh][:, n * 65:(n + 1) * 65],
                    at_sb[hh][:], start=False, stop=True),
                    reads=[bva[hh], bat[hh]], writes=[by[hh]],
                    tag=f"{tag}_tmm{hp}_{n}_{hh}")
                P.op("tensor", lambda e, hh=hh, n=n: e.matmul(
                    p_k[hh][:], pkt_sb[hh][:, n * 64:(n + 1) * 64],
                    va_sb[hh][:, n * 65:(n + 1) * 65],
                    start=True, stop=True),
                    reads=[bkt[hh], bva[hh]], writes=[bk[hh]],
                    tag=f"{tag}_kmm{hp}_{n}_{hh}")
                P.op("vector", lambda e, hh=hh: e.tensor_add(
                    s32[hh][:], s32[hh][:], p_k[hh][:]),
                    reads=[bs32[hh], bk[hh]], writes=[bs32[hh]],
                    tag=f"{tag}_sup{hp}_{n}_{hh}")
                P.op("vector", lambda e, hh=hh: e.tensor_copy(
                    s_bf[hh][:], s32[hh][:]),
                    reads=[bs32[hh]], writes=[bsbf[hh]],
                    tag=f"{tag}_scp{hp}_{n}_{hh}")
                P.op("scalar", lambda e, hh=hh, n=n: e.copy(
                    yt_sb[hh][:, n * 64:(n + 1) * 64], p_y[hh].ap()[0:64, :]),
                    reads=[by[hh]], writes=[byt[hh]],
                    tag=f"{tag}_yev{hp}_{n}_{hh}")
                P.op("scalar", lambda e, hh=hh, n=n: e.copy(
                    den_sb[0:1, hh * T + n * 64: hh * T + (n + 1) * 64],
                    p_y[hh].ap()[64:65, :]),
                    reads=[by[hh]], writes=[bden],
                    tag=f"{tag}_dev{hp}_{n}_{hh}")
        P.op("vector", lambda e: e.tensor_scalar_max(den_sb[:], den_sb[:], 1e-6),
             reads=[bden], writes=[bden], tag=f"{tag}_dcl{hp}")
        P.op("vector", lambda e: e.reciprocal(den_sb[:], den_sb[:]),
             reads=[bden], writes=[bden], tag=f"{tag}_drc{hp}")
        for hh in range(2):
            h = 2 * hp + hh
            for tt in range(8):
                P.op("tensor", lambda e, hh=hh, tt=tt: e.matmul(
                    p_bc[:], ctx.cf32["ones_row"][0:1, 0:64],
                    den_sb[0:1, hh * T + tt * 512: hh * T + (tt + 1) * 512],
                    start=True, stop=True),
                    reads=[bden, ctx.cbuf], writes=[bbc],
                    tag=f"{tag}_nbc{hp}_{hh}_{tt}")
                P.op("scalar", lambda e: e.copy(rbc_sb[:], p_bc[:]),
                     reads=[bbc], writes=[brbc], tag=f"{tag}_nbe{hp}_{hh}_{tt}")
                P.op("vector", lambda e, hh=hh, tt=tt: e.tensor_mul(
                    yo_sb[:, tt * 512:(tt + 1) * 512],
                    yt_sb[hh][:, tt * 512:(tt + 1) * 512], rbc_sb[:]),
                    reads=[byt[hh], brbc], writes=[byo],
                    tag=f"{tag}_nml{hp}_{hh}_{tt}")
            P.op("sync", lambda e, h=h: e.dma_start(
                out=y_dram[h * 64:(h + 1) * 64, :], in_=yo_sb[:]),
                reads=[byo], dma=True, tag=f"{tag}_yst{h}")


CAPACITY1 = 10240.0


def emit_moe1_route(ctx, s2_dram, gwT_dram, cmb_dram, rank_dram, counts_dram,
                    tag="m1r"):
    nc, P, a = ctx.nc, ctx.P, ctx.alloc
    gw_sb = a.sb(f"{tag}_gw", [128, DCH * 8], BF16)
    gw32 = a.sb(f"{tag}_gw32", [128, DCH * 8], F32)
    xn_t = a.sb(f"{tag}_xn", [128, DCH * 512], BF16)
    pr = a.sb(f"{tag}_pr", [128, 8], F32)
    top = a.sb(f"{tag}_top", [128, 8], F32)
    sel = a.sb(f"{tag}_sel", [128, 8], F32)
    cnt = a.sb(f"{tag}_cnt", [128, 8], BF16)
    den = a.sb(f"{tag}_den", [128, 1], F32)
    cmb_t = a.sb(f"{tag}_cmb", [128, 4 * 8], F32)
    rank_t = a.sb(f"{tag}_rank", [128, 4 * 8], F32)
    offs = a.sb(f"{tag}_offs", [1, 8], F32)
    offb = a.sb(f"{tag}_offb", [128, 8], F32)
    tot = a.sb(f"{tag}_tot", [1, 8], F32)
    p_lg = a.ps(f"{tag}_plg", [128, 8], F32)
    p_pf = a.ps(f"{tag}_ppf", [128, 8], F32)
    p_t = a.ps(f"{tag}_pt", [1, 8], F32)
    p_ob = a.ps(f"{tag}_pob", [128, 8], F32)
    b = {n: ctx.buf(f"{tag}_{n}") for n in
         ("gw", "gw32", "xn", "pr", "top", "sel", "cnt", "den", "cmb",
          "rank", "offs", "offb", "tot", "plg", "ppf", "pt", "pob")}
    striu = ctx.cb16["striu"]
    ones_col = ctx.cb16["ones_col"]

    P.op("sync", lambda e: e.dma_start(
        out=gw32.ap().rearrange("p (c o) -> p c o", c=DCH),
        in_=gwT_dram.rearrange("(c p) o -> p c o", p=128)),
        writes=[b["gw32"]], dma=True, tag=f"{tag}_gld")
    P.op("vector", lambda e: e.tensor_copy(gw_sb[:], gw32[:]),
         reads=[b["gw32"]], writes=[b["gw"]], tag=f"{tag}_gc")
    P.op("vector", lambda e: e.memset(offs[:], 0.0),
         writes=[b["offs"]], tag=f"{tag}_o0")

    for n in range(NT):
        P.op("sync", lambda e, n=n: e.dma_start(
            out=xn_t.ap().rearrange("p (c t) -> p c t", c=DCH),
            in_=s2_dram.rearrange("(c p) t -> p c t", p=128)[:, :, n * 512:(n + 1) * 512]),
            writes=[b["xn"]], dma=True, tag=f"{tag}_xld{n}")
        for ms in range(4):
            for k in range(DCH):
                P.op("tensor", lambda e, ms=ms, k=k: e.matmul(
                    p_lg[:], xn_t[:, k * 512 + ms * 128: k * 512 + (ms + 1) * 128],
                    gw_sb[:, k * 8:(k + 1) * 8],
                    start=(k == 0), stop=(k == DCH - 1)),
                    reads=[b["xn"], b["gw"]], writes=[b["plg"]],
                    tag=f"{tag}_lmm{n}_{ms}_{k}")
            P.op("scalar", lambda e: e.activation(
                out=pr[:], in_=p_lg[:], func=AF.Sigmoid),
                reads=[b["plg"]], writes=[b["pr"]], tag=f"{tag}_sig{n}_{ms}")
            P.op("vector", lambda e: e.max(out=top[:], in_=pr[:]),
                 reads=[b["pr"]], writes=[b["top"]], tag=f"{tag}_top{n}_{ms}")
            P.op("vector", lambda e: e.tensor_single_scalar(
                sel[:], pr[:], top[:, 1:2], ALU.is_ge),
                reads=[b["pr"], b["top"]], writes=[b["sel"]],
                tag=f"{tag}_sel{n}_{ms}")
            P.op("vector", lambda e: e.tensor_add(den[:], top[:, 0:1], top[:, 1:2]),
                 reads=[b["top"]], writes=[b["den"]], tag=f"{tag}_dn{n}_{ms}")
            P.op("vector", lambda e: e.tensor_scalar_add(den[:], den[:], 1e-6),
                 reads=[b["den"]], writes=[b["den"]], tag=f"{tag}_dne{n}_{ms}")
            P.op("vector", lambda e: e.reciprocal(den[:], den[:]),
                 reads=[b["den"]], writes=[b["den"]], tag=f"{tag}_dnr{n}_{ms}")
            P.op("vector", lambda e, ms=ms: e.tensor_mul(
                cmb_t[:, ms * 8:(ms + 1) * 8], sel[:], pr[:]),
                reads=[b["sel"], b["pr"]], writes=[b["cmb"]],
                tag=f"{tag}_cw{n}_{ms}")
            P.op("vector", lambda e, ms=ms: e.tensor_scalar_mul(
                cmb_t[:, ms * 8:(ms + 1) * 8], cmb_t[:, ms * 8:(ms + 1) * 8],
                den[:]),
                reads=[b["cmb"], b["den"]], writes=[b["cmb"]],
                tag=f"{tag}_cs{n}_{ms}")
            P.op("vector", lambda e: e.tensor_copy(cnt[:], sel[:]),
                 reads=[b["sel"]], writes=[b["cnt"]], tag=f"{tag}_cb{n}_{ms}")
            P.op("tensor", lambda e: e.matmul(
                p_pf[:], striu, cnt[:], start=True, stop=True),
                reads=[b["cnt"], ctx.cbuf], writes=[b["ppf"]],
                tag=f"{tag}_pf{n}_{ms}")
            P.op("tensor", lambda e: e.matmul(
                p_t[:], ones_col, cnt[:], start=True, stop=True),
                reads=[b["cnt"], ctx.cbuf], writes=[b["pt"]],
                tag=f"{tag}_tt{n}_{ms}")
            P.op("tensor", lambda e: e.matmul(
                p_ob[:], ctx.cf32["ones_row"], offs[:], start=True, stop=True),
                reads=[b["offs"], ctx.cbuf], writes=[b["pob"]],
                tag=f"{tag}_ob{n}_{ms}")
            P.op("vector", lambda e, ms=ms: e.tensor_add(
                rank_t[:, ms * 8:(ms + 1) * 8], p_pf[:], offb[:]) if False
                else e.tensor_copy(offb[:], p_ob[:]),
                reads=[b["pob"]], writes=[b["offb"]], tag=f"{tag}_obe{n}_{ms}")
            P.op("vector", lambda e, ms=ms: e.tensor_add(
                rank_t[:, ms * 8:(ms + 1) * 8], p_pf[:], offb[:]),
                reads=[b["ppf"], b["offb"]], writes=[b["rank"]],
                tag=f"{tag}_rk{n}_{ms}")
            P.op("vector", lambda e: e.tensor_copy(tot[:], p_t[:]),
                 reads=[b["pt"]], writes=[b["tot"]], tag=f"{tag}_te{n}_{ms}")
            P.op("vector", lambda e: e.tensor_add(offs[:], offs[:], tot[:]),
                 reads=[b["offs"], b["tot"]], writes=[b["offs"]],
                 tag=f"{tag}_oa{n}_{ms}")
        P.op("sync", lambda e, n=n: e.dma_start(
            out=cmb_dram[n * 512:(n + 1) * 512, :].rearrange(
                "(ms p) e -> p ms e", p=128),
            in_=cmb_t.ap().rearrange("p (ms e) -> p ms e", e=8)),
            reads=[b["cmb"]], dma=True, tag=f"{tag}_cst{n}")
        P.op("sync", lambda e, n=n: e.dma_start(
            out=rank_dram[n * 512:(n + 1) * 512, :].rearrange(
                "(ms p) e -> p ms e", p=128),
            in_=rank_t.ap().rearrange("p (ms e) -> p ms e", e=8)),
            reads=[b["rank"]], dma=True, tag=f"{tag}_rst{n}")
    P.op("sync", lambda e: e.dma_start(out=counts_dram, in_=offs[:]),
         reads=[b["offs"]], dma=True, tag=f"{tag}_tst")


def emit_moe1_gate(ctx, cmb_dram, rank_dram, counts_all_dram, coremask_dram,
                   gct_dram, tag="m1g"):
    nc, P, a = ctx.nc, ctx.P, ctx.alloc
    ca = a.sb(f"{tag}_ca", [8, 8], F32)
    cm = a.sb(f"{tag}_cm", [8, 1], F32)
    offb = a.sb(f"{tag}_offb", [128, 8], F32)
    cmb_t = a.sb(f"{tag}_cmb", [128, 4 * 8], F32)
    rank_t = a.sb(f"{tag}_rank", [128, 4 * 8], F32)
    g_t = a.sb(f"{tag}_g", [128, 8], BF16)
    gct = a.sb(f"{tag}_gct", [8, 512], BF16)
    p_o = a.ps(f"{tag}_po", [128, 8], F32)
    p_tr = a.ps(f"{tag}_ptr", [8, 128], BF16)
    b = {n: ctx.buf(f"{tag}_{n}") for n in
         ("ca", "cm", "offb", "cmb", "rank", "g", "gct", "po", "ptr")}
    identb = ctx.cb16["ident"]

    P.op("sync", lambda e: e.dma_start(out=ca[:], in_=counts_all_dram),
         writes=[b["ca"]], dma=True, tag=f"{tag}_cald")
    P.op("sync", lambda e: e.dma_start(out=cm[:], in_=coremask_dram),
         writes=[b["cm"]], dma=True, tag=f"{tag}_cmld")
    P.op("tensor", lambda e: e.matmul(
        p_o[0:1, :], cm[:], ca[:], start=True, stop=True),
        reads=[b["ca"], b["cm"]], writes=[b["po"]], tag=f"{tag}_offmm")
    P.op("scalar", lambda e: e.copy(offb[0:1, :], p_o[0:1, :]),
         reads=[b["po"]], writes=[b["offb"]], tag=f"{tag}_offe")
    P.op("tensor", lambda e: e.matmul(
        p_o[:], ctx.cf32["ones_row"], offb[0:1, :], start=True, stop=True),
        reads=[b["offb"], ctx.cbuf], writes=[b["po"]], tag=f"{tag}_obc")
    P.op("scalar", lambda e: e.copy(offb[:], p_o[:]),
         reads=[b["po"]], writes=[b["offb"]], tag=f"{tag}_obce")

    for n in range(NT):
        P.op("sync", lambda e, n=n: e.dma_start(
            out=cmb_t.ap().rearrange("p (ms e) -> p ms e", e=8),
            in_=cmb_dram[n * 512:(n + 1) * 512, :].rearrange(
                "(ms p) e -> p ms e", p=128)),
            writes=[b["cmb"]], dma=True, tag=f"{tag}_cld{n}")
        P.op("sync", lambda e, n=n: e.dma_start(
            out=rank_t.ap().rearrange("p (ms e) -> p ms e", e=8),
            in_=rank_dram[n * 512:(n + 1) * 512, :].rearrange(
                "(ms p) e -> p ms e", p=128)),
            writes=[b["rank"]], dma=True, tag=f"{tag}_rld{n}")
        for ms in range(4):
            P.op("vector", lambda e, ms=ms: e.tensor_add(
                rank_t[:, ms * 8:(ms + 1) * 8], rank_t[:, ms * 8:(ms + 1) * 8],
                offb[:]),
                reads=[b["rank"], b["offb"]], writes=[b["rank"]],
                tag=f"{tag}_ga{n}_{ms}")
            P.op("vector", lambda e, ms=ms: e.tensor_single_scalar(
                rank_t[:, ms * 8:(ms + 1) * 8], rank_t[:, ms * 8:(ms + 1) * 8],
                CAPACITY1, ALU.is_lt),
                reads=[b["rank"]], writes=[b["rank"]], tag=f"{tag}_gv{n}_{ms}")
            P.op("vector", lambda e, ms=ms: e.tensor_mul(
                g_t[:], rank_t[:, ms * 8:(ms + 1) * 8],
                cmb_t[:, ms * 8:(ms + 1) * 8]),
                reads=[b["rank"], b["cmb"]], writes=[b["g"]],
                tag=f"{tag}_gm{n}_{ms}")
            P.op("tensor", lambda e: e.transpose(p_tr[:], g_t[:], identb),
                 reads=[b["g"], ctx.cbuf], writes=[b["ptr"]],
                 tag=f"{tag}_tr{n}_{ms}")
            P.op("scalar", lambda e, ms=ms: e.copy(
                gct[:, ms * 128:(ms + 1) * 128], p_tr[:]),
                reads=[b["ptr"]], writes=[b["gct"]], tag=f"{tag}_te{n}_{ms}")
        P.op("sync", lambda e, n=n: e.dma_start(
            out=gct_dram[:, n * 512:(n + 1) * 512], in_=gct[:]),
            reads=[b["gct"]], dma=True, tag=f"{tag}_st{n}")


def emit_moe1_experts(ctx, s2_dram, gct_dram, w1_dram, w2_dram, experts,
                      out_dram, accumulate, tag="m1e"):
    nc, P, a = ctx.nc, ctx.P, ctx.alloc
    ne = len(experts)
    HID = 2048
    HCH = HID // 128
    w1 = a.sb(f"{tag}_w1", [128, ne * DCH * HID], BF16)
    w2 = a.sb(f"{tag}_w2", [128, ne * HCH * D], BF16)
    stage = [a.sb(f"{tag}_st{i}", [128, 2048], F32) for i in range(1)]
    stb = [ctx.buf(f"{tag}_st{i}") for i in range(1)]
    xn_t = a.sb(f"{tag}_xn", [128, DCH * 512], BF16)
    gct_t = a.sb(f"{tag}_gct", [8, 512], BF16)
    cbc = a.sb(f"{tag}_cbc", [128, 512], BF16)
    h_sb = a.sb(f"{tag}_h", [128, ne * HCH * 512], BF16)
    rout = a.sb(f"{tag}_ro", [128, 2 * 512], F32)
    p_g = [a.ps(f"{tag}_pg{i}", [128, 512], F32) for i in range(4)]
    p_d = [a.ps(f"{tag}_pd{i}", [128, 512], F32) for i in range(3)]
    p_cb = a.ps(f"{tag}_pcb", [128, 512], F32)
    b = {n: ctx.buf(f"{tag}_{n}") for n in
         ("w1", "w2", "xn", "gct", "cbc", "h", "ro", "pcb")}
    bg = [ctx.buf(f"{tag}_pg{i}") for i in range(4)]
    bd = [ctx.buf(f"{tag}_pd{i}") for i in range(3)]
    sel8 = ctx.cb16["sel8"]

    j = 0
    for ei, ex in enumerate(experts):
        for k in range(DCH):
            st, sb_ = stage[0], stb[0]
            P.op("sync", lambda e, ex=ex, k=k, st=st: e.dma_start(
                out=st[:], in_=w1_dram[ex * 1024 + k * 128: ex * 1024 + (k + 1) * 128, :]),
                writes=[sb_], dma=True, tag=f"{tag}_w1l{ei}_{k}")
            P.op("vector", lambda e, ei=ei, k=k, st=st: e.tensor_copy(
                w1[:, (ei * DCH + k) * HID:(ei * DCH + k + 1) * HID], st[:]),
                reads=[sb_], writes=[b["w1"]], tag=f"{tag}_w1c{ei}_{k}")
        for k2 in range(HCH):
            st, sb_ = stage[0], stb[0]
            P.op("sync", lambda e, ex=ex, k2=k2, st=st: e.dma_start(
                out=st[:, 0:1024],
                in_=w2_dram[ex * 2048 + k2 * 128: ex * 2048 + (k2 + 1) * 128, :]),
                writes=[sb_], dma=True, tag=f"{tag}_w2l{ei}_{k2}")
            P.op("vector", lambda e, ei=ei, k2=k2, st=st: e.tensor_copy(
                w2[:, (ei * HCH + k2) * D:(ei * HCH + k2 + 1) * D], st[:, 0:1024]),
                reads=[sb_], writes=[b["w2"]], tag=f"{tag}_w2c{ei}_{k2}")

    for n in range(NT):
        P.op("sync", lambda e, n=n: e.dma_start(
            out=xn_t.ap().rearrange("p (c t) -> p c t", c=DCH),
            in_=s2_dram.rearrange("(c p) t -> p c t", p=128)[:, :, n * 512:(n + 1) * 512]),
            writes=[b["xn"]], dma=True, tag=f"{tag}_xld{n}")
        P.op("sync", lambda e, n=n: e.dma_start(
            out=gct_t[:], in_=gct_dram[:, n * 512:(n + 1) * 512]),
            writes=[b["gct"]], dma=True, tag=f"{tag}_gld{n}")
        for ei, ex in enumerate(experts):
            P.op("tensor", lambda e, ex=ex: e.matmul(
                p_cb[:], sel8[:, ex * 128:(ex + 1) * 128], gct_t[:],
                start=True, stop=True),
                reads=[b["gct"], ctx.cbuf], writes=[b["pcb"]],
                tag=f"{tag}_cbm{n}_{ei}")
            P.op("scalar", lambda e: e.copy(cbc[:], p_cb[:]),
                 reads=[b["pcb"]], writes=[b["cbc"]], tag=f"{tag}_cbe{n}_{ei}")
            for m in range(HCH):
                bk = m % 4
                for k in range(DCH):
                    P.op("tensor", lambda e, ei=ei, m=m, k=k, bk=bk: e.matmul(
                        p_g[bk][:],
                        w1[:, (ei * DCH + k) * HID + m * 128: (ei * DCH + k) * HID + (m + 1) * 128],
                        xn_t[:, k * 512:(k + 1) * 512],
                        start=(k == 0), stop=(k == DCH - 1)),
                        reads=[b["w1"], b["xn"]], writes=[bg[bk]],
                        tag=f"{tag}_1mm{n}_{ei}_{m}_{k}")
                P.op("scalar", lambda e, ei=ei, m=m, bk=bk: e.activation(
                    out=h_sb[:, (ei * HCH + m) * 512:(ei * HCH + m + 1) * 512],
                    in_=p_g[bk][:], func=AF.Gelu),
                    reads=[bg[bk]], writes=[b["h"]], tag=f"{tag}_gel{n}_{ei}_{m}")
                P.op("vector", lambda e, ei=ei, m=m: e.tensor_mul(
                    h_sb[:, (ei * HCH + m) * 512:(ei * HCH + m + 1) * 512],
                    h_sb[:, (ei * HCH + m) * 512:(ei * HCH + m + 1) * 512],
                    cbc[:]),
                    reads=[b["h"], b["cbc"]], writes=[b["h"]],
                    tag=f"{tag}_hcb{n}_{ei}_{m}")
        for mo in range(DCH):
            bk = mo % 3
            nacc = ne * HCH
            jj = 0
            for ei in range(ne):
                for k2 in range(HCH):
                    P.op("tensor", lambda e, ei=ei, k2=k2, mo=mo, bk=bk, jj=jj: e.matmul(
                        p_d[bk][:],
                        w2[:, (ei * HCH + k2) * D + mo * 128: (ei * HCH + k2) * D + (mo + 1) * 128],
                        h_sb[:, (ei * HCH + k2) * 512:(ei * HCH + k2 + 1) * 512],
                        start=(jj == 0), stop=(jj == nacc - 1)),
                        reads=[b["w2"], b["h"]], writes=[bd[bk]],
                        tag=f"{tag}_2mm{n}_{mo}_{ei}_{k2}")
                    jj += 1
            P.op("scalar", lambda e, mo=mo, bk=bk: e.copy(
                rout[:, (mo % 2) * 512:(mo % 2 + 1) * 512], p_d[bk][:]),
                reads=[bd[bk]], writes=[b["ro"]], tag=f"{tag}_dev{n}_{mo}")
            if mo % 2 == 1:
                q_ = "gpsimd" if accumulate else "sync"
                P.op(q_, lambda e, n=n, mo=mo: e.dma_start(
                    out=out_dram.rearrange("(c p) t -> p c t", p=128)[
                        :, mo - 1:mo + 1, n * 512:(n + 1) * 512],
                    in_=rout.ap().rearrange("p (c t) -> p c t", c=2),
                    **(dict(accum_op=ALU.add) if accumulate else {})),
                    reads=[b["ro"]], dma=True, tag=f"{tag}_ost{n}_{mo}")


def run_ln(x_cores, outs_spec, kind="ln", w=None, eps=1e-5, cache_key="ln"):
    names = ["ones_row", "ones_col"]
    specs = {"x_fm": ((D, T), F32, "ExternalInput"), **const_specs(names)}
    for nm, dt in outs_spec:
        specs[nm] = ((D, T), dt, "ExternalOutput")
    if w is not None:
        specs["lnw"] = ((128, DCH), F32, "ExternalInput")

    def ph(ctx, dr):
        emit_ln(ctx, dr["x_fm"], [(dr[nm], dt) for nm, dt in outs_spec],
                kind=kind, w_tiled_dram=dr.get("lnw"), eps=eps, tag="ln")

    base = dict(**const_inmaps(names))
    if w is not None:
        base["lnw"] = _tile_bias(w)
    im = [dict(x_fm=x, **base) for x in x_cores]
    res = _run(cache_key, specs, names, [ph], im)
    return {nm: [r[nm] for r in res] for nm, _ in outs_spec}


def run_linscan(pq, pk, pkt, vaug):
    names = ["ones_row", "triu_mask"]
    specs = {
        "pq": ((D, T), BF16, "ExternalInput"),
        "pk": ((D, T), BF16, "ExternalInput"),
        "pkt": ((T, D), BF16, "ExternalInput"),
        "vaug": ((T, 16 * 65), BF16, "ExternalInput"),
        "y": ((D, T), BF16, "ExternalOutput"),
        **const_specs(names),
    }

    def ph(ctx, dr):
        emit_linscan(ctx, dr["pq"], dr["pk"], dr["pkt"], dr["vaug"], dr["y"],
                     tag="ls")

    base = const_inmaps(names)
    im = [dict(pq=pq[c], pk=pk[c], pkt=pkt[c], vaug=vaug[c], **base)
          for c in range(N_CORES)]
    res = _run("linscan", specs, names, [ph], im)
    return [r["y"] for r in res]


def run_moe1_route(s2_cores, gate_w):
    names = ["ones_row", "ones_col", "striu"]
    specs = {
        "s2": ((D, T), BF16, "ExternalInput"),
        "gwT": ((D, 8), F32, "ExternalInput"),
        "cmb": ((T, 8), F32, "ExternalOutput"),
        "rank": ((T, 8), F32, "ExternalOutput"),
        "counts": ((1, 8), F32, "ExternalOutput"),
        **const_specs(names),
    }

    def ph(ctx, dr):
        emit_moe1_route(ctx, dr["s2"], dr["gwT"], dr["cmb"], dr["rank"],
                        dr["counts"], tag="m1r")

    base = dict(gwT=np.ascontiguousarray(gate_w), **const_inmaps(names))
    im = [dict(s2=s2, **base) for s2 in s2_cores]
    res = _run("moe1route", specs, names, [ph], im)
    return ([r["cmb"] for r in res], [r["rank"] for r in res],
            [r["counts"] for r in res])


def run_moe1_gate(cmb_cores, rank_cores, all_counts, core_masks):
    names = ["ones_row", "ident"]
    specs = {
        "cmb": ((T, 8), F32, "ExternalInput"),
        "rank": ((T, 8), F32, "ExternalInput"),
        "ca": ((8, 8), F32, "ExternalInput"),
        "cm": ((8, 1), F32, "ExternalInput"),
        "gct": ((8, T), BF16, "ExternalOutput"),
        **const_specs(names),
    }

    def ph(ctx, dr):
        emit_moe1_gate(ctx, dr["cmb"], dr["rank"], dr["ca"], dr["cm"],
                       dr["gct"], tag="m1g")

    base = const_inmaps(names)
    im = [dict(cmb=cmb_cores[c], rank=rank_cores[c], ca=all_counts,
               cm=core_masks[c], **base) for c in range(N_CORES)]
    res = _run("moe1gate", specs, names, [ph], im)
    return [r["gct"] for r in res]


def run_moe1_experts(s2_cores, gct_cores, w1, w2, experts, cache_key):
    names = ["sel8"]
    specs = {
        "s2": ((D, T), BF16, "ExternalInput"),
        "gct": ((8, T), BF16, "ExternalInput"),
        "w1": ((8 * 1024, 2048), F32, "ExternalInput"),
        "w2": ((8 * 2048, 1024), F32, "ExternalInput"),
        "out": ((D, T), F32, "ExternalOutput"),
        **const_specs(names),
    }

    def ph(ctx, dr):
        emit_moe1_experts(ctx, dr["s2"], dr["gct"], dr["w1"], dr["w2"],
                          experts, dr["out"], False, tag="m1e")

    base = dict(w1=w1.reshape(8 * 1024, 2048), w2=w2.reshape(8 * 2048, 1024),
                **const_inmaps(names))
    im = [dict(s2=s2_cores[c], gct=gct_cores[c], **base) for c in range(N_CORES)]
    res = _run(cache_key, specs, names, [ph], im)
    return [r["out"] for r in res]



def kernel(**inp):
    x = inp["L0_attn_x"]
    x_fm = [np.ascontiguousarray(x[c].T) for c in range(N_CORES)]

    xn0 = run_ln0(x_fm)
    q0 = run_mlp(xn0, inp["L0_attn_q_w1"], inp["L0_attn_q_b1"],
                 inp["L0_attn_q_w2"], inp["L0_attn_q_b2"], "fm", "q")
    k0 = run_mlp(xn0, inp["L0_attn_k_w1"], inp["L0_attn_k_b1"],
                 inp["L0_attn_k_w2"], inp["L0_attn_k_b2"], "fm", "k")
    v0 = run_mlp(xn0, inp["L0_attn_v_w1"], inp["L0_attn_v_b1"],
                 inp["L0_attn_v_w2"], inp["L0_attn_v_b2"], "tm_aug", "v")
    attn0x = run_attn(q0, k0, v0, inp["L0_attn_proj_w"], inp["L0_attn_proj_b"],
                      16, 16, False, res_cores=x_fm, cache_key="attn0")
    sg0 = run_linear(xn0, _t(inp["L0_ffn_shared_gate_w"]), None, "fm", 1024,
                     2048, act="silu", cache_key="shg")
    su0 = run_linear(xn0, _t(inp["L0_ffn_shared_up_w"]), None, "fm", 1024,
                     2048, act=None, cache_key="shu")
    ffn0 = run_linear(sg0, _t(inp["L0_ffn_shared_down_w"]), None, "fm", 2048,
                      1024, act=None, out_dtype=F32, x2_cores=su0,
                      cache_key="shd")
    cmb0 = run_ffn0_route(xn0, inp["L0_ffn_router_w"],
                          inp["L0_ffn_correction_bias"])
    rA = run_ffn0_experts(xn0, cmb0, inp["L0_ffn_gate_w"], inp["L0_ffn_up_w"],
                          inp["L0_ffn_down_w"], list(range(0, 8)), "exA")
    rB = run_ffn0_experts(xn0, cmb0, inp["L0_ffn_gate_w"], inp["L0_ffn_up_w"],
                          inp["L0_ffn_down_w"], list(range(8, 16)), "exB")
    x1 = run_add([attn0x, ffn0, rA, rB], "comb_l0")

    ln1 = run_ln(x1, [("xn1b", BF16), ("xn1f", F32)], cache_key="ln1")
    xn1b, xn1f = ln1["xn1b"], ln1["xn1f"]
    pq = run_linear(xn1b, _t(inp["L1_attn_W_q"]), None, "fm", 1024, 1024,
                    act="elu1", cache_key="l1q")
    pk = run_linear(xn1b, _t(inp["L1_attn_W_k"]), None, "fm", 1024, 1024,
                    act="elu1", cache_key="l1k")
    pkt = run_linear(xn1b, _t(inp["L1_attn_W_k"]), None, "tm", 1024, 1024,
                     act="elu1", cache_key="l1kt")
    v1 = run_linear(xn1b, _t(inp["L1_attn_W_v"]), None, "tm", 1024, 1024,
                    aug=True, cache_key="l1v")
    y1 = run_linscan(pq, pk, pkt, v1)
    attn1 = run_linear(y1, _t(inp["L1_attn_W_o"]), None, "fm", 1024, 1024,
                       act=None, out_dtype=F32, cache_key="l1o")
    qg = run_linear(xn1b, inp["L1_ffn_qw"], inp["L1_ffn_qb"], "fm", 1024,
                    1024, cache_key="gqaq")
    kg = run_linear(xn1b, inp["L1_ffn_kw"], inp["L1_ffn_kb"], "fm", 1024,
                    256, cache_key="gqak")
    vg = run_linear(xn1b, inp["L1_ffn_vw"], None, "tm", 1024, 256, aug=True,
                    b_row=inp["L1_ffn_vb"], cache_key="gqav")
    s1 = run_attn(qg, kg, vg, inp["L1_ffn_ow"], inp["L1_ffn_ob"], 16, 4, True,
                  res_cores=xn1f, cache_key="gqa_at")
    rms1 = run_ln(s1, [("s2b", BF16), ("s2f", F32)], kind="rms",
                  w=inp["L1_ffn_norm1_w"], eps=1e-6, cache_key="rms1")
    s2b, s2f = rms1["s2b"], rms1["s2f"]
    cmb1, rank1, counts = run_moe1_route(s2b, inp["L1_ffn_gate_w"])
    all_counts = np.ascontiguousarray(
        np.concatenate([counts[c] for c in range(N_CORES)], axis=0))
    core_masks = []
    for c in range(N_CORES):
        m = np.zeros((8, 1), np.float32)
        m[:c, 0] = 1.0
        core_masks.append(m)
    gct = run_moe1_gate(cmb1, rank1, all_counts, core_masks)
    mo = []
    for gi in range(4):
        mo.append(run_moe1_experts(s2b, gct, inp["L1_ffn_w1"],
                                   inp["L1_ffn_w2"],
                                   [2 * gi, 2 * gi + 1], f"m1e{gi}"))
    s3 = run_add([s2f] + mo, "comb_s3")
    rms2 = run_ln(s3, [("f1", F32)], kind="rms", w=inp["L1_ffn_norm2_w"],
                  eps=1e-6, cache_key="rms2")
    ffn1 = rms2["f1"]
    out_fm = run_add([x1, attn1, ffn1], "comb_out")
    return np.ascontiguousarray(
        np.stack([out_fm[c].T for c in range(N_CORES)]))



ALL_CONSTS = ["ident", "ones_all", "ones_row", "ones_col", "striu",
              "triu_mask", "sel16", "sel8"]


def build_fused(part="all"):
    nc = bass.Bass("TRN2", target_bir_lowering=False, debug=False)
    dr = {}

    def din(nm, shape, dtype=F32):
        dr[nm] = nc.dram_tensor(nm, list(shape), dtype, kind="ExternalInput").ap()

    import os
    taps = set(os.environ.get("DEBUG_TAPS", "").split(",")) - {""}
    HANDOFF = ("x1", "attn1", "s2b", "s2f", "cmb1", "rank1", "cnts")
    if part == "C":
        HANDOFF = HANDOFF + ("s3",)

    def dint(nm, shape, dtype=F32):
        kind = "ExternalOutput" if nm in taps else "Internal"
        if part == "A" and nm in HANDOFF:
            kind = "ExternalOutput"
        if part in ("B", "C") and nm in HANDOFF:
            kind = "ExternalInput"
        dr[nm] = nc.dram_tensor(nm, list(shape), dtype, kind=kind).ap()

    din("x_fm", (D, T))
    for nm in ALL_CONSTS:
        din("c_" + nm, CONST_SHAPES[nm])
    for p_ in ("q", "k", "v"):
        din(f"{p_}w1", (D, 4096)); din(f"{p_}b1", (128, 32))
        din(f"{p_}w2", (4096, D)); din(f"{p_}b2", (128, 8))
    din("vb2row", (1, D))
    din("pjw", (D, D)); din("pjb", (128, 8))
    din("sgwT", (D, 2048)); din("suwT", (D, 2048)); din("sdwT", (2048, D))
    din("rwT", (D, 16)); din("rcb", (1, 16))
    din("egw", (16 * 1024, 256)); din("euw", (16 * 1024, 256))
    din("edw", (16 * 256, 1024))
    din("wqT", (D, D)); din("wkT", (D, D)); din("wvT", (D, D)); din("woT", (D, D))
    din("gqw", (D, D)); din("gqb", (128, 8))
    din("gkw", (D, 256)); din("gkb", (128, 2))
    din("gvw", (D, 256)); din("gvbrow", (1, 256))
    din("gow", (D, D)); din("gob", (128, 8))
    din("n1w", (128, DCH)); din("n2w", (128, DCH))
    din("mgw", (D, 8))
    din("mw1", (8 * 1024, 2048)); din("mw2", (8 * 2048, 1024))
    din("ca8", (8, 8)); din("cmask", (8, 1))
    out = nc.dram_tensor("out_fm", [D, T], F32, kind="ExternalOutput").ap()

    dint("xn0", (D, T), BF16)
    dint("q0", (D, T), BF16); dint("k0", (D, T), BF16)
    dint("v0", (T, 16 * 65), BF16)
    dint("attn0x", (D, T), F32)
    dint("sg", (D2 := 2048, T), BF16); dint("su", (2048, T), BF16)
    dint("ffn0", (D, T), F32)
    dint("cmb0", (16, T), BF16)
    dint("x1", (D, T), F32)
    dint("xn1b", (D, T), BF16); dint("xn1f", (D, T), F32)
    dint("pq", (D, T), BF16); dint("pk", (D, T), BF16)
    dint("pkt", (T, D), BF16); dint("v1", (T, 16 * 65), BF16)
    dint("y1", (D, T), BF16)
    dint("attn1", (D, T), F32)
    dint("qg", (D, T), BF16); dint("kg", (256, T), BF16)
    dint("vg", (T, 4 * 65), BF16)
    dint("s1", (D, T), F32)
    dint("s2b", (D, T), BF16); dint("s2f", (D, T), F32)
    dint("cmb1", (T, 8), F32); dint("rank1", (T, 8), F32)
    dint("cnts", (1, 8), F32)
    dint("gct", (8, T), BF16)
    dint("moe", (D, T), F32)
    dint("s3", (D, T), F32)
    dint("ffn1", (D, T), F32)

    from contextlib import ExitStack
    with ExitStack() as outer:
        P = Prog(nc)
        ctx = Ctx(nc, P, Alloc(nc, outer))
        cd = {nm: dr["c_" + nm] for nm in ALL_CONSTS}
        with ExitStack() as ces:
            emit_load_constants(ctx, cd, ALL_CONSTS,
                                stage_alloc=Alloc(nc, ces))
        P.phase_barrier()
        if os.environ.get("DEBUG_CONSTS"):
            dbgc = nc.dram_tensor("dbgc", [128, 256], F32,
                                  kind="ExternalOutput").ap()
            dsb = outer.enter_context(nc.sbuf_tensor("dbgc_sb", [128, 256], F32))
            bdb = ctx.buf("dbgc_sb")
            P.op("vector", lambda e: e.tensor_copy(
                dsb[:, 0:1], ctx.cb16["ones_col"]),
                reads=[ctx.cbuf], writes=[bdb], tag="dbgc1")
            P.op("vector", lambda e: e.tensor_copy(
                dsb[0:1, 1:129], ctx.cb16["ones_row"]),
                reads=[ctx.cbuf], writes=[bdb], tag="dbgc2")
            P.op("vector", lambda e: e.tensor_copy(
                dsb[0:1, 129:130], ctx.cf32["ones_row"][0:1, 0:1]),
                reads=[ctx.cbuf], writes=[bdb], tag="dbgc3")
            P.op("sync", lambda e: e.dma_start(out=dbgc, in_=dsb[:]),
                 reads=[bdb], dma=True, tag="dbgc_st")
            P.phase_barrier()

        def phase(fn):
            with ExitStack() as pes:
                ctx.alloc = Alloc(nc, pes)
                fn()
            P.phase_barrier()

        PA = part in ("all", "A")
        PB = part in ("all", "B")
        PC = part == "C"
        if PA:
            phase(lambda: emit_ln(ctx, dr["x_fm"], [(dr["xn0"], BF16)],
                                  kind="ln", tag="Pln0"))
        if PA:
            phase(lambda: emit_mlp(ctx, dr["xn0"], dr["qw1"], dr["qb1"],
                                   dr["qw2"], dr["qb2"], "fm", dr["q0"], tag="Pmq"))
            phase(lambda: emit_mlp(ctx, dr["xn0"], dr["kw1"], dr["kb1"],
                                   dr["kw2"], dr["kb2"], "fm", dr["k0"], tag="Pmk"))
            phase(lambda: emit_mlp(ctx, dr["xn0"], dr["vw1"], dr["vb1"],
                                   dr["vw2"], dr["vb2"], "tm_aug", dr["v0"],
                                   b2row_dram=dr["vb2row"], tag="Pmv"))
            phase(lambda: emit_attn(ctx, dr["q0"], dr["k0"], False, dr["v0"],
                                    16, 16, dr["pjw"], dr["pjb"], dr["attn0x"],
                                    res_drams=[dr["x_fm"]], tag="Pat0"))
            phase(lambda: emit_linear(ctx, dr["xn0"], dr["sgwT"], None, "fm",
                                      dr["sg"], 1024, 2048, act="silu", tag="Psg"))
            phase(lambda: emit_linear(ctx, dr["xn0"], dr["suwT"], None, "fm",
                                      dr["su"], 1024, 2048, act=None, tag="Psu"))
            phase(lambda: emit_linear(ctx, dr["sg"], dr["sdwT"], None, "fm",
                                      dr["ffn0"], 2048, 1024, act=None,
                                      out_dtype=F32, x2_dram=dr["su"], tag="Psd"))
            phase(lambda: emit_ffn0_route(ctx, dr["xn0"], dr["rwT"], dr["rcb"],
                                          dr["cmb0"], tag="Prt"))
            phase(lambda: emit_ffn0_experts(ctx, dr["xn0"], dr["cmb0"], dr["egw"],
                                            dr["euw"], dr["edw"], list(range(8)),
                                            dr["ffn0"], True, tag="PxA"))
            phase(lambda: emit_ffn0_experts(ctx, dr["xn0"], dr["cmb0"], dr["egw"],
                                            dr["euw"], dr["edw"], list(range(8, 16)),
                                            dr["ffn0"], True, tag="PxB"))
            phase(lambda: emit_add(ctx, [dr["attn0x"], dr["ffn0"]], dr["x1"],
                                   tag="Pc0"))
            phase(lambda: emit_ln(ctx, dr["x1"], [(dr["xn1b"], BF16),
                                                  (dr["xn1f"], F32)],
                                  kind="ln", tag="Pln1"))
            phase(lambda: emit_linear(ctx, dr["xn1b"], dr["wqT"], None, "fm",
                                      dr["pq"], 1024, 1024, act="elu1", tag="Plq"))
            phase(lambda: emit_linear(ctx, dr["xn1b"], dr["wkT"], None, "fm",
                                      dr["pk"], 1024, 1024, act="elu1", tag="Plk"))
            phase(lambda: emit_linear(ctx, dr["xn1b"], dr["wkT"], None, "tm",
                                      dr["pkt"], 1024, 1024, act="elu1", tag="Plkt"))
            phase(lambda: emit_linear(ctx, dr["xn1b"], dr["wvT"], None, "tm",
                                      dr["v1"], 1024, 1024, aug=True, tag="Plv"))
            phase(lambda: emit_linscan(ctx, dr["pq"], dr["pk"], dr["pkt"],
                                       dr["v1"], dr["y1"], tag="Pls"))
            phase(lambda: emit_linear(ctx, dr["y1"], dr["woT"], None, "fm",
                                      dr["attn1"], 1024, 1024, act=None,
                                      out_dtype=F32, tag="Plo"))
            phase(lambda: emit_linear(ctx, dr["xn1b"], dr["gqw"], dr["gqb"], "fm",
                                      dr["qg"], 1024, 1024, tag="Pgq"))
            phase(lambda: emit_linear(ctx, dr["xn1b"], dr["gkw"], dr["gkb"], "fm",
                                      dr["kg"], 1024, 256, tag="Pgk"))
            phase(lambda: emit_linear(ctx, dr["xn1b"], dr["gvw"], None, "tm",
                                      dr["vg"], 1024, 256, aug=True,
                                      b_row_dram=dr["gvbrow"], tag="Pgv"))
            phase(lambda: emit_attn(ctx, dr["qg"], dr["kg"], True, dr["vg"],
                                    16, 4, dr["gow"], dr["gob"], dr["s1"],
                                    res_drams=[dr["xn1f"]], tag="Pga"))
            phase(lambda: emit_ln(ctx, dr["s1"], [(dr["s2b"], BF16),
                                                  (dr["s2f"], F32)],
                                  kind="rms", w_tiled_dram=dr["n1w"], eps=1e-6,
                                  tag="Pr1"))
            phase(lambda: emit_moe1_route(ctx, dr["s2b"], dr["mgw"], dr["cmb1"],
                                          dr["rank1"], dr["cnts"], tag="Pm1r"))
        if PB:
            phase(lambda: emit_moe1_gate(ctx, dr["cmb1"], dr["rank1"], dr["ca8"],
                                         dr["cmask"], dr["gct"], tag="Pm1g"))
            for gi in range(4):
                phase(lambda gi=gi: emit_moe1_experts(
                    ctx, dr["s2b"], dr["gct"], dr["mw1"], dr["mw2"],
                    [2 * gi, 2 * gi + 1], dr["moe"], gi > 0, tag=f"Pme{gi}"))
        if PB:
            phase(lambda: emit_add(ctx, [dr["s2f"], dr["moe"]], dr["s3"],
                                   tag="Pc3"))
        if PB or PC:
            phase(lambda: emit_ln(ctx, dr["s3"], [(dr["ffn1"], F32)],
                                  kind="rms", w_tiled_dram=dr["n2w"], eps=1e-6,
                                  tag="Pr2"))
        if PB:
            phase(lambda: emit_add(ctx, [dr["x1"], dr["attn1"], dr["ffn1"]],
                                   out, tag="Pco"))

        block = outer.enter_context(nc.Block())
        P.finalize(block, outer, nc)
    return nc


_FUSED_NC = None


def fused_in_maps(inp):
    x = inp["L0_attn_x"]
    ch = get_const_host()
    base = {
        "qw1": np.ascontiguousarray(inp["L0_attn_q_w1"]),
        "qb1": _tile_bias(inp["L0_attn_q_b1"]),
        "qw2": np.ascontiguousarray(inp["L0_attn_q_w2"]),
        "qb2": _tile_bias(inp["L0_attn_q_b2"]),
        "kw1": np.ascontiguousarray(inp["L0_attn_k_w1"]),
        "kb1": _tile_bias(inp["L0_attn_k_b1"]),
        "kw2": np.ascontiguousarray(inp["L0_attn_k_w2"]),
        "kb2": _tile_bias(inp["L0_attn_k_b2"]),
        "vw1": np.ascontiguousarray(inp["L0_attn_v_w1"]),
        "vb1": _tile_bias(inp["L0_attn_v_b1"]),
        "vw2": np.ascontiguousarray(inp["L0_attn_v_w2"]),
        "vb2": _tile_bias(inp["L0_attn_v_b2"]),
        "vb2row": np.ascontiguousarray(inp["L0_attn_v_b2"].reshape(1, D)),
        "pjw": np.ascontiguousarray(inp["L0_attn_proj_w"]),
        "pjb": _tile_bias(inp["L0_attn_proj_b"]),
        "sgwT": _t(inp["L0_ffn_shared_gate_w"]),
        "suwT": _t(inp["L0_ffn_shared_up_w"]),
        "sdwT": _t(inp["L0_ffn_shared_down_w"]),
        "rwT": _t(inp["L0_ffn_router_w"]),
        "rcb": np.ascontiguousarray(inp["L0_ffn_correction_bias"].reshape(1, 16)),
        "egw": np.ascontiguousarray(inp["L0_ffn_gate_w"].reshape(16 * 1024, 256)),
        "euw": np.ascontiguousarray(inp["L0_ffn_up_w"].reshape(16 * 1024, 256)),
        "edw": np.ascontiguousarray(inp["L0_ffn_down_w"].reshape(16 * 256, 1024)),
        "wqT": _t(inp["L1_attn_W_q"]),
        "wkT": _t(inp["L1_attn_W_k"]),
        "wvT": _t(inp["L1_attn_W_v"]),
        "woT": _t(inp["L1_attn_W_o"]),
        "gqw": np.ascontiguousarray(inp["L1_ffn_qw"]),
        "gqb": _tile_bias(inp["L1_ffn_qb"]),
        "gkw": np.ascontiguousarray(inp["L1_ffn_kw"]),
        "gkb": _tile_bias(inp["L1_ffn_kb"]),
        "gvw": np.ascontiguousarray(inp["L1_ffn_vw"]),
        "gvbrow": np.ascontiguousarray(inp["L1_ffn_vb"].reshape(1, 256)),
        "gow": np.ascontiguousarray(inp["L1_ffn_ow"]),
        "gob": _tile_bias(inp["L1_ffn_ob"]),
        "n1w": _tile_bias(inp["L1_ffn_norm1_w"]),
        "n2w": _tile_bias(inp["L1_ffn_norm2_w"]),
        "mgw": np.ascontiguousarray(inp["L1_ffn_gate_w"]),
        "mw1": np.ascontiguousarray(inp["L1_ffn_w1"].reshape(8 * 1024, 2048)),
        "mw2": np.ascontiguousarray(inp["L1_ffn_w2"].reshape(8 * 2048, 1024)),
        "ca8": np.zeros((8, 8), np.float32),
    }
    for nm in ALL_CONSTS:
        base["c_" + nm] = ch[nm]
    im = []
    for c in range(N_CORES):
        m = dict(base)
        m["x_fm"] = np.ascontiguousarray(x[c].T)
        cmk = np.zeros((8, 1), np.float32)
        cmk[:c, 0] = 1.0
        m["cmask"] = cmk
        im.append(m)
    return im


def kernel_fused(**inp):
    global _FUSED_NC
    if _FUSED_NC is None:
        _FUSED_NC = build_fused()
    nc = _FUSED_NC
    im = fused_in_maps(inp)
    res = run_bass_kernel_spmd(nc, im, core_ids=list(range(N_CORES)))
    return np.ascontiguousarray(
        np.stack([res.results[c]["out_fm"].T for c in range(N_CORES)]))


kernel_multi = kernel
kernel = kernel_fused


def _input_names(nc):
    names = []
    for alloc in nc.m.functions[0].allocations:
        if isinstance(alloc, mybir.MemoryLocationSet) and alloc.kind == "ExternalInput":
            nm = alloc.name
            if nm.endswith("_set"):
                nm = nm[:-4]
            if nm != "partition_id":
                names.append(nm)
    return set(names)


_AB_NC = {}


def kernel_ab(**inp):
    if "A" not in _AB_NC:
        _AB_NC["A"] = build_fused("A")
        _AB_NC["B"] = build_fused("B")
    ncA, ncB = _AB_NC["A"], _AB_NC["B"]
    namesA, namesB = _input_names(ncA), _input_names(ncB)
    im = fused_in_maps(inp)
    imA = [{k: v for k, v in m.items() if k in namesA} for m in im]
    resA = run_bass_kernel_spmd(ncA, imA, core_ids=list(range(N_CORES)))
    all_counts = np.ascontiguousarray(np.concatenate(
        [np.asarray(resA.results[c]["cnts"]) for c in range(N_CORES)], axis=0))
    imB = []
    for c in range(N_CORES):
        m = {k: v for k, v in im[c].items() if k in namesB}
        for nm in ("x1", "attn1", "s2b", "s2f", "cmb1", "rank1", "cnts"):
            if nm in namesB:
                m[nm] = np.asarray(resA.results[c][nm])
        m["ca8"] = all_counts
        imB.append(m)
    resB = run_bass_kernel_spmd(ncB, imB, core_ids=list(range(N_CORES)))
    return np.ascontiguousarray(
        np.stack([resB.results[c]["out_fm"].T for c in range(N_CORES)]))


kernel = kernel_ab
